# version 24
# baseline (speedup 1.0000x reference)
from contextlib import ExitStack

import numpy as np
import ml_dtypes

import concourse.bass as bass
import concourse.mybir as mybir
from concourse.bass_utils import run_bass_kernel_spmd
from concourse.aot_env import allow_ici_all_to_all

F32 = mybir.dt.float32
FP8 = mybir.dt.float8e4
BF16 = mybir.dt.bfloat16
ALU = mybir.AluOpType
AF = mybir.ActivationFunctionType
AX = mybir.AxisListType.X

C32 = float(np.float32(1.5 * 2.0 ** 23))
C52 = float(np.float32(1.5 * 2.0 ** 52))

S = 8192
D = 2048
NH = 16
NKV = 4
DH = 128
DKV = NKV * DH
FI = 8192
NC = 8
EPS_LN = 1e-5
EPS_RMS = 1e-6
EPS_Q = 1e-5

CH = S // 16
TQ = 2 * CH
NXT = TQ // 128
NKT = S // 128
NDC = D // 128
NIC = FI // 128
WGS = FI // NC
WDS = FI // NC

STOP_AFTER = None
ENGS = ("y", "t", "v", "s", "g")
DMAQ = {"y": "DY", "s": "DS", "g": "DG"}
ESEM = {"t": "T", "v": "V", "s": "S", "g": "G", "y": "Y"}


class TT:
    __slots__ = ("ap", "w", "r")

    def __init__(self, ap):
        self.ap = ap
        self.w = {}
        self.r = {}

    def __getitem__(self, k):
        return self.ap[k]


NQSEM = 8


class Bld:
    def __init__(self, nc):
        self.nc = nc
        self.phase = "init"
        self.pend = {e: [] for e in ENGS}
        self.ops = {e: [] for e in ENGS}
        self.semh = {}
        self.cnt = {}
        for nm in ["T", "V", "S", "G", "Y"] + [f"{q}{i}" for q in ("DY", "DS", "DG")
                                              for i in range(NQSEM)]:
            self.semh[nm] = nc.alloc_semaphore(name="sem" + nm)
            self.cnt[nm] = 0
        self.qidx = {"DY": 0, "DS": 0, "DG": 0}
        self.ncc = 0
        self.hwm = {e: {} for e in ENGS}

    def wait(self, eng, ev):
        if ev is None:
            return
        sem, val = ev
        if self.hwm[eng].get(sem, 0) >= val:
            return
        self.hwm[eng][sem] = val
        h = self.semh[sem]
        self.ops[eng].append(lambda E, h=h, val=val: E.wait_ge(h, val))

    def waits(self, eng, evs):
        best = {}
        for ev in evs:
            if ev is None:
                continue
            sem, val = ev
            if best.get(sem, -1) < val:
                best[sem] = val
        for sem, val in best.items():
            self.wait(eng, (sem, val))

    def raw(self, eng, fn, inc):
        step = 16 if inc.startswith("D") else 1
        self.cnt[inc] += step
        val = self.cnt[inc]
        h = self.semh[inc]
        ph = self.phase

        def f(E, fn=fn, h=h, step=step, ph=ph, nc=self.nc):
            nc.push_named_scope(ph)
            try:
                fn(E).then_inc(h, step)
            finally:
                nc.pop_named_scope(ph)
        self.ops[eng].append(f)
        return (inc, val)

    def emit(self, eng, fn, reads=(), writes=(), dma=False, sem=None, inc=True):
        evs = []
        for t in reads:
            if t is not None:
                evs.extend(t.w.items())
        for t in writes:
            if t is not None:
                evs.extend(t.w.items())
                evs.extend(t.r.items())
        self.waits(eng, evs)
        if not inc:
            ph = self.phase

            def f(E, fn=fn, ph=ph, nc=self.nc):
                nc.push_named_scope(ph)
                try:
                    fn(E)
                finally:
                    nc.pop_named_scope(ph)
            self.ops[eng].append(f)
            for t in reads:
                if t is not None:
                    self.pend[eng].append((t, False))
            for t in writes:
                if t is not None:
                    self.pend[eng].append((t, True))
            return None
        if sem is None:
            if dma:
                q = DMAQ[eng]
                sem = f"{q}{self.qidx[q]}"
                self.qidx[q] = (self.qidx[q] + 1) % NQSEM
                self.wait(eng, (sem, self.cnt[sem]))
            else:
                sem = ESEM[eng]
        ev = self.raw(eng, fn, sem)
        s, v = ev
        for t in reads:
            if t is not None and t.r.get(s, 0) < v:
                t.r[s] = v
        for t in writes:
            if t is not None:
                t.w = {s: v}
                t.r = {}
        if self.pend[eng] and not sem.startswith("D") and sem in ESEM.values():
            for t, is_w in self.pend[eng]:
                if is_w:
                    t.w[s] = v
                else:
                    if t.r.get(s, 0) < v:
                        t.r[s] = v
            self.pend[eng] = []
        return ev

    def phase_barrier(self, tts):
        evs = []
        for t in tts:
            evs.extend(t.w.items())
            evs.extend(t.r.items())
        for e in ENGS:
            self.waits(e, evs)

    def collective(self, kind, op, ins_ap, outs_ap, reads=(), writes=()):
        nm = f"CC{self.ncc}"
        self.ncc += 1
        self.semh[nm] = self.nc.alloc_semaphore(name="sem" + nm)
        self.cnt[nm] = 0
        return self.emit("g", lambda E: E.collective_compute(
            kind, op, replica_groups=[list(range(NC))], ins=[ins_ap], outs=[outs_ap]),
            reads=reads, writes=writes, sem=nm)

    def dma(self, eng, dst, dst_ap, src, src_ap):
        return self.emit(eng, lambda E, o=dst_ap, i=src_ap: E.dma_start(out=o, in_=i),
                         reads=(src,), writes=(dst,), dma=True)

    def finalize(self):
        with allow_ici_all_to_all(), self.nc.Block() as block:
            for ename, attr in (("y", "sync"), ("t", "tensor"), ("v", "vector"),
                                ("s", "scalar"), ("g", "gpsimd")):
                def body(E, ops=self.ops[ename]):
                    for f in ops:
                        f(E)
                getattr(block, attr)(body)


def build_program(taps=()):
    nc = bass.Bass("TRN2", target_bir_lowering=False)
    b = Bld(nc)
    ctx = ExitStack()

    def sb(name, shape, dtype):
        return TT(ctx.enter_context(nc.sbuf_tensor(name, shape, dtype)).ap())

    def dram(name, shape, dtype, kind="Internal"):
        return TT(nc.dram_tensor(name, shape, dtype, kind=kind).ap())

    x_in = dram("x_own", [TQ, D], F32, "ExternalInput")
    wq_in = dram("Wq", [D, D], F32, "ExternalInput")
    wk_in = dram("Wk", [D, DKV], F32, "ExternalInput")
    wv_in = dram("Wv", [D, DKV], F32, "ExternalInput")
    wo_in = dram("Wo", [D, D], F32, "ExternalInput")
    gq_in = dram("gq", [1, DH], F32, "ExternalInput")
    gk_in = dram("gk", [1, DH], F32, "ExternalInput")
    wg_in = dram("wg_sl", [WGS, D], F32, "ExternalInput")
    wu_in = dram("wu_sl", [WGS, D], F32, "ExternalInput")
    wd_in = dram("wd_sl", [D, WDS], F32, "ExternalInput")
    out_d = dram("out", [TQ, D], F32, "ExternalOutput")

    KVP = 4 * DH * TQ
    kv_in = dram("kv_in", [NC, KVP], BF16)
    kv_out = dram("kv_out", [NC, KVP], BF16)
    QOP = 4 * DH * TQ
    q_ain = dram("q_ain", [NC, QOP], BF16)
    q_aout = dram("q_aout", [NC, QOP], BF16)
    OHP = 2 * DH * TQ
    o_ain0 = dram("o_ain0", [NC, OHP], BF16)
    o_aout0 = dram("o_aout0", [NC, OHP], BF16)
    o_ain1 = dram("o_ain1", [NC, OHP], BF16)
    o_aout1 = dram("o_aout1", [NC, OHP], BF16)
    wo_split_d = dram("wo_split_d", [2, NDC, 128, D], BF16)

    GPART = D * WGS
    DPART = WDS * D
    tern_in = dram("tern_in", [2 * GPART + DPART], FP8)
    tern_outg = dram("tern_outg", [NC * GPART], FP8)
    tern_outu = dram("tern_outu", [NC * GPART], FP8)
    tern_outd = dram("tern_outd", [NC * DPART], FP8)
    TERN_STRIDE = 2 * GPART + DPART

    ar_in = dram("ar_in", [1, 3], F32)
    ar_out = dram("ar_out", [1, 3], F32)

    x_tm_d = dram("x_tm_d", [NXT, 128, D], F32)
    x1_d = dram("x1_d", [NXT, 128, D], F32)

    Hm = np.ones((1, 1), dtype=np.float64)
    while Hm.shape[0] < 128:
        Hm = np.block([[Hm, Hm], [Hm, -Hm]])
    consts = {
        "h128": (np.asarray(Hm, dtype=ml_dtypes.bfloat16), BF16, [128, 128]),
        "identb": (np.eye(128, dtype=ml_dtypes.bfloat16), BF16, [128, 128]),
        "identf": (np.eye(128, dtype=np.float32), F32, [128, 128]),
        "onesb": (np.ones((128, 1), dtype=ml_dtypes.bfloat16), BF16, [128, 1]),
        "onesfr": (np.ones((1, 128), dtype=np.float32), F32, [1, 128]),
        "onesfc": (np.ones((128, 1), dtype=np.float32), F32, [128, 1]),
        "tri": (np.tril(np.ones((128, 128), dtype=np.float32)).T.copy(), F32, [128, 128]),
    }
    C = {}
    for nm, (arr, dt, shape) in consts.items():
        h = nc.inline_tensor(np.ascontiguousarray(arr), name=nm + "_c")
        t = sb(nm, shape, dt)
        b.dma("y", t, t.ap[:, :], TT(h.ap()), h.ap())
        C[nm] = t

    gq_r = sb("gq_r", [1, 128], F32)
    gk_r = sb("gk_r", [1, 128], F32)
    gq_c = sb("gq_c", [128, 1], F32)
    gk_c = sb("gk_c", [128, 1], F32)
    b.dma("y", gq_r, gq_r.ap[:, :], gq_in, gq_in.ap)
    b.dma("y", gk_r, gk_r.ap[:, :], gk_in, gk_in.ap)
    b.dma("y", gq_c, gq_c.ap[:, :], gq_in, gq_in.ap.rearrange("a d -> d a"))
    b.dma("y", gk_c, gk_c.ap[:, :], gk_in, gk_in.ap.rearrange("a d -> d a"))

    banks = [TT(ctx.enter_context(nc.psum_tensor(f"bank{i}", [128, 512], F32)).ap())
             for i in range(8)]

    blend = sb("blend", [128, 2], F32)
    mtmp = sb("mtmp", [1, 8], F32)
    ebias = sb("ebias", [128, 1], F32)
    gabs = sb("gabs", [1, 256], F32)
    svals = sb("svals", [1, 8], F32)
    shalfB = sb("shalfB", [128, 8], F32)
    b.emit("v", lambda E: E.memset(blend.ap[:, 0:1], 1.0), writes=(blend,))
    b.emit("v", lambda E: E.tensor_scalar(blend.ap[:, 0:1], blend.ap[:, 0:1], C52, C52,
                                          ALU.add, ALU.subtract), reads=(blend,), writes=(blend,))
    b.emit("v", lambda E: E.tensor_scalar(blend.ap[:, 1:2], blend.ap[:, 0:1], -1.0, 1.0,
                                          ALU.mult, ALU.add), reads=(blend,), writes=(blend,))

    def emit_rne(eng, out_t, out_ap, in_t, in_ap, scale1, t1, t2):
        sc_reads = (in_t,)
        if isinstance(scale1, tuple):
            sc_t, sc_ap = scale1
            sc_reads = (in_t, sc_t)
        else:
            sc_ap = scale1
        b.emit(eng, lambda E: E.tensor_scalar(t1.ap, in_ap, sc_ap, C32, ALU.mult, ALU.add),
               reads=sc_reads, writes=(t1,))
        b.emit(eng, lambda E: E.tensor_scalar(t2.ap, t1.ap, C32, C52, ALU.subtract, ALU.add),
               reads=(t1,), writes=(t2,))
        b.emit(eng, lambda E: E.tensor_scalar(t2.ap, t2.ap, C52, None, ALU.subtract),
               reads=(t2,), writes=(t2,))
        b.emit(eng, lambda E: E.tensor_scalar(t1.ap, t1.ap, C32, None, ALU.subtract),
               reads=(t1,), writes=(t1,))
        return b.emit(eng, lambda E: E.scalar_tensor_tensor(out_ap, t1.ap, blend.ap[:, 1:2],
                                                            t2.ap, ALU.mult, ALU.add),
                      reads=(t1, t2, blend), writes=(out_t,))

    b.phase = "p0a"

    p0 = ExitStack()
    p0_tts = []

    def sb0(name, shape, dtype):
        t = TT(p0.enter_context(nc.sbuf_tensor(name, shape, dtype)).ap())
        p0_tts.append(t)
        return t

    wstage = [sb0(f"wstage{i}", [128, D], F32) for i in range(2)]
    q1 = sb0("q1", [128, D], F32)
    q2 = sb0("q2", [128, D], F32)
    q3 = sb0("q3", [128, D], BF16)
    q4 = sb0("q4", [128, 1024], FP8)
    wacc = sb0("wacc", [128, 4], F32)
    wred = sb0("wred", [128, 2], F32)

    wtiles = []
    for j in range(WGS // 128):
        wtiles.append((0, wg_in, wg_in.ap[128 * j:128 * (j + 1), :], D))
    for j in range(WGS // 128):
        wtiles.append((1, wu_in, wu_in.ap[128 * j:128 * (j + 1), :], D))
    for j in range(D // 128):
        wtiles.append((2, wd_in, wd_in.ap[128 * j:128 * (j + 1), :], WDS))

    b.phase = "p1ln"
    p1 = ExitStack()
    p1_tts = []

    def sb1(name, shape, dtype):
        t = TT(p1.enter_context(nc.sbuf_tensor(name, shape, dtype)).ap())
        p1_tts.append(t)
        return t

    ln_hi = sb1("ln_hi", [128, NDC * TQ], BF16)
    ln_lo = sb1("ln_lo", [128, NDC * TQ], BF16)
    p1a = ExitStack()
    p1a_tts = []

    def sb1a(name, shape, dtype):
        t = TT(p1a.enter_context(nc.sbuf_tensor(name, shape, dtype)).ap())
        p1a_tts.append(t)
        return t

    xstage = [sb1a(f"xstage{i}", [128, D], F32) for i in range(2)]
    xsq_b = [sb1a("xsq0", [128, D], F32)] * 2
    xn32_b = [sb1a("xn320", [128, D], F32)] * 2
    xnhi_b = [sb1a(f"xnhi{i}", [128, D], BF16) for i in range(2)]
    xnlo_b = [sb1a(f"xnlo{i}", [128, D], BF16) for i in range(2)]
    lnst_b = [sb1a(f"lnst{i}", [128, 8], F32) for i in range(2)]

    b.emit("v", lambda E: E.memset(wacc.ap[:, 0:3], 0.0), writes=(wacc,))

    def wsum_job(i):
        mat, src_t, src_ap, cols = wtiles[i]
        stw = wstage[i % 2]
        b.dma("s", stw, stw.ap[:, 0:cols], src_t, src_ap)
        b.emit("v", lambda E, stw=stw, cols=cols: E.reduce_sum(
            wred.ap[:, 0:1], stw.ap[:, 0:cols], AX, apply_absolute_value=True),
            reads=(stw,), writes=(wred,))
        b.emit("v", lambda E, mat=mat: E.tensor_add(
            wacc.ap[:, mat:mat + 1], wacc.ap[:, mat:mat + 1], wred.ap[:, 0:1]),
            reads=(wacc, wred), writes=(wacc,))

    NWJ = len(wtiles)
    wj = [0]
    for i in range(NXT):
        st = xstage[i % 2]
        xsq = xsq_b[i % 2]
        xn32 = xn32_b[i % 2]
        xnhi_tm = xnhi_b[i % 2]
        xnlo_tm = xnlo_b[i % 2]
        lnst = lnst_b[i % 2]
        b.dma("y", st, st.ap[:, :], x_in, x_in.ap[128 * i:128 * (i + 1), :])
        b.emit("v", lambda E, st=st, lnst=lnst: E.reduce_sum(lnst.ap[:, 0:1], st.ap, AX),
               reads=(st,), writes=(lnst,))
        b.emit("s", lambda E, st=st, xsq=xsq: E.activation(xsq.ap, st.ap, AF.Square),
               reads=(st,), writes=(xsq,))
        b.emit("v", lambda E, lnst=lnst, xsq=xsq: E.reduce_sum(lnst.ap[:, 1:2], xsq.ap, AX),
               reads=(xsq,), writes=(lnst,))
        b.emit("v", lambda E, lnst=lnst: E.tensor_scalar(lnst.ap[:, 2:4], lnst.ap[:, 0:2], 1.0 / D,
                                              None, ALU.mult), reads=(lnst,), writes=(lnst,))
        b.emit("v", lambda E, lnst=lnst: E.tensor_scalar(lnst.ap[:, 4:5], lnst.ap[:, 2:3],
                                              lnst.ap[:, 2:3], None, ALU.mult),
               reads=(lnst,), writes=(lnst,))
        b.emit("v", lambda E, lnst=lnst: E.tensor_sub(lnst.ap[:, 5:6], lnst.ap[:, 3:4], lnst.ap[:, 4:5]),
               reads=(lnst,), writes=(lnst,))
        b.emit("v", lambda E, lnst=lnst: E.tensor_scalar(lnst.ap[:, 5:6], lnst.ap[:, 5:6], EPS_LN,
                                              None, ALU.add), reads=(lnst,), writes=(lnst,))
        b.emit("s", lambda E, lnst=lnst: E.activation(lnst.ap[:, 5:6], lnst.ap[:, 5:6], AF.Sqrt),
               reads=(lnst,), writes=(lnst,))
        b.emit("v", lambda E, lnst=lnst: E.reciprocal(lnst.ap[:, 6:7], lnst.ap[:, 5:6]),
               reads=(lnst,), writes=(lnst,))
        b.emit("v", lambda E, lnst=lnst: E.tensor_scalar(lnst.ap[:, 7:8], lnst.ap[:, 2:3],
                                              lnst.ap[:, 6:7], -1.0, ALU.mult, ALU.mult),
               reads=(lnst,), writes=(lnst,))
        b.emit("v", lambda E, st=st, lnst=lnst, xnhi_tm=xnhi_tm: E.tensor_scalar(
            xnhi_tm.ap, st.ap, lnst.ap[:, 6:7], lnst.ap[:, 7:8], ALU.mult, ALU.add),
            reads=(st, lnst), writes=(xnhi_tm,))
        b.emit("v", lambda E, st=st, lnst=lnst, xn32=xn32: E.tensor_scalar(
            xn32.ap, st.ap, lnst.ap[:, 6:7], lnst.ap[:, 7:8], ALU.mult, ALU.add),
            reads=(st, lnst), writes=(xn32,))
        b.emit("v", lambda E, xn32=xn32, xnhi_tm=xnhi_tm, xnlo_tm=xnlo_tm: E.tensor_sub(
            xnlo_tm.ap, xn32.ap, xnhi_tm.ap),
               reads=(xn32, xnhi_tm), writes=(xnlo_tm,))
        b.dma("g", x_tm_d, x_tm_d.ap[i], st, st.ap)
        for half, (srcT, dstT) in enumerate(((xnhi_tm, ln_hi), (xnlo_tm, ln_lo))):
            for grp in range(2):
                bk = banks[2 + (half * 2 + grp) % 2]
                bkb = bk.ap.bitcast(BF16)
                for p in range(8):
                    cch = 8 * grp + p
                    b.emit("t", lambda E, bkb=bkb, p=p, srcT=srcT, cch=cch: E.transpose(
                        bkb[:, 128 * p:128 * (p + 1)], srcT.ap[:, 128 * cch:128 * (cch + 1)],
                        C["identb"].ap), reads=(srcT, C["identb"]), writes=(bk,))
                def evac(E, bkb=bkb, dstT=dstT, grp=grp, i=i):
                    dst = dstT.ap.rearrange("p (c q) -> p c q", c=NDC)[
                        :, 8 * grp:8 * grp + 8, 128 * i:128 * (i + 1)]
                    return E.tensor_copy(dst, bkb.rearrange("p (c q) -> p c q", c=8))
                b.emit("v", evac, reads=(bk,), writes=(dstT,))
        b.phase = "p0a"
        for _ in range(2):
            if wj[0] < NWJ:
                wsum_job(wj[0])
                wj[0] += 1
        b.phase = "p1ln"

    b.phase_barrier(p1a_tts)
    p1a.close()

    b.phase = "p1m"

    b.emit("s", lambda E: E.activation(gabs.ap[0:1, 0:128], gq_r.ap, AF.Abs),
           reads=(gq_r,), writes=(gabs,))
    b.emit("s", lambda E: E.activation(gabs.ap[0:1, 128:256], gk_r.ap, AF.Abs),
           reads=(gk_r,), writes=(gabs,))
    b.emit("v", lambda E: E.reduce_max(mtmp.ap[0:1, 0:1], gabs.ap[0:1, 0:128], AX),
           reads=(gabs,), writes=(mtmp,))
    b.emit("v", lambda E: E.reduce_max(mtmp.ap[0:1, 1:2], gabs.ap[0:1, 128:256], AX),
           reads=(gabs,), writes=(mtmp,))
    b.emit("v", lambda E: E.tensor_scalar(mtmp.ap[0:1, 2:3], mtmp.ap[0:1, 0:1],
                                          mtmp.ap[0:1, 1:2], -float(np.sqrt(128.0)),
                                          ALU.mult, ALU.mult), reads=(mtmp,), writes=(mtmp,))
    b.emit("t", lambda E: E.matmul(banks[4].ap[:, 0:1], C["onesfr"].ap, mtmp.ap[0:1, 2:3],
                                   start=True, stop=True), reads=(C["onesfr"], mtmp), writes=(banks[4],))
    b.emit("v", lambda E: E.tensor_copy(ebias.ap, banks[4].ap[:, 0:1]),
           reads=(banks[4],), writes=(ebias,))

    def split_weight(src_t, src_ap_fn, ntiles, cols, hi_t, lo_t, stage, w32, q="y"):
        w32b = w32 if isinstance(w32, list) else [w32, w32]
        for j in range(ntiles):
            w32 = w32b[j % 2]
            b.dma(q, w32, w32.ap[:, 0:cols], src_t, src_ap_fn(j))
            b.emit("s", lambda E, j=j, w32=w32: E.activation(
                hi_t.ap[:, cols * j:cols * (j + 1)], w32.ap[:, 0:cols], AF.Copy),
                   reads=(w32,), writes=(hi_t,))
            b.emit("v", lambda E, j=j, w32=w32: E.tensor_sub(
                lo_t.ap[:, cols * j:cols * (j + 1)], w32.ap[:, 0:cols],
                hi_t.ap[:, cols * j:cols * (j + 1)]),
                   reads=(w32, hi_t), writes=(lo_t,))

    b.phase = "p1kv"
    kvsq_b = [sb1(f"kvsq{i}", [128, 512], F32) for i in range(2)]
    kvn32_b = [sb1(f"kvn32{i}", [128, 512], F32) for i in range(2)]
    kvhi_b = [sb1(f"kvhi{i}", [128, 512], BF16) for i in range(2)]
    kvlo_b = [sb1(f"kvlo{i}", [128, 512], BF16) for i in range(2)]
    rmsrow_b = [sb1(f"rmsrow{i}", [1, 512], F32) for i in range(2)]

    p1b = ExitStack()
    p1b_tts = []

    def sb1b(name, shape, dtype):
        t = TT(p1b.enter_context(nc.sbuf_tensor(name, shape, dtype)).ap())
        p1b_tts.append(t)
        return t

    wkv32 = [sb1b(f"wkv32_{i}", [128, DKV], F32) for i in range(2)]
    wk_hi = sb1b("wk_hi", [128, NDC * DKV], BF16)
    wk_lo = sb1b("wk_lo", [128, NDC * DKV], BF16)
    wv_hi = sb1b("wv_hi", [128, NDC * DKV], BF16)
    wv_lo = sb1b("wv_lo", [128, NDC * DKV], BF16)
    split_weight(wk_in, lambda j: wk_in.ap[128 * j:128 * (j + 1), :], NDC, DKV, wk_hi, wk_lo, None, wkv32)
    split_weight(wv_in, lambda j: wv_in.ap[128 * j:128 * (j + 1), :], NDC, DKV, wv_hi, wv_lo, None, wkv32)

    def proj_head(w_hi, w_lo, hd, qh, pb):
        combos = ((w_hi, ln_hi), (w_hi, ln_lo), (w_lo, ln_hi))
        n_mm = 3 * NDC
        k = 0
        for (wt, lt) in combos:
            for cchunk in range(NDC):
                wap = wt.ap[:, DKV * cchunk + DH * hd:DKV * cchunk + DH * (hd + 1)]
                lap = lt.ap.rearrange("p (c q) -> p c q", c=NDC)[
                    :, cchunk, 512 * qh:512 * (qh + 1)]
                b.emit("t", lambda E, wap=wap, lap=lap, st=(k == 0), sp=(k == n_mm - 1),
                       pb=pb: E.matmul(pb.ap[:, 0:512], wap, lap, start=st, stop=sp),
                       reads=(wt, lt), writes=(pb,), inc=(k == n_mm - 1))
                k += 1

    def rms_split_store(pb, g_col, eps, dst_t, dst_pairs, do_rms, sbk2=None, par=0):
        kvsq, kvn32, kvhi, kvlo, rmsrow = (kvsq_b[par], kvn32_b[par], kvhi_b[par],
                                           kvlo_b[par], rmsrow_b[par])
        if sbk2 is None:
            sbk2 = banks[6]
        if do_rms:
            b.emit("s", lambda E: E.activation(kvsq.ap, pb.ap[:, 0:512], AF.Square),
                   reads=(pb,), writes=(kvsq,))
            b.emit("t", lambda E, sbk2=sbk2: E.matmul(sbk2.ap[0:1, 0:512], C["onesfc"].ap,
                                           kvsq.ap, start=True, stop=True),
                   reads=(C["onesfc"], kvsq), writes=(sbk2,))
            b.emit("v", lambda E, sbk2=sbk2: E.tensor_scalar(rmsrow.ap, sbk2.ap[0:1, 0:512],
                                                  1.0 / DH, eps, ALU.mult, ALU.add),
                   reads=(sbk2,), writes=(rmsrow,))
            b.emit("s", lambda E: E.activation(rmsrow.ap, rmsrow.ap, AF.Sqrt),
                   reads=(rmsrow,), writes=(rmsrow,))
            b.emit("v", lambda E: E.reciprocal(rmsrow.ap, rmsrow.ap),
                   reads=(rmsrow,), writes=(rmsrow,))
            b.emit("t", lambda E, sbk2=sbk2: E.matmul(sbk2.ap[:, 0:512], C["onesfr"].ap,
                                           rmsrow.ap, start=True, stop=True),
                   reads=(C["onesfr"], rmsrow), writes=(sbk2,))
            b.emit("v", lambda E, sbk2=sbk2: E.tensor_copy(kvsq.ap, sbk2.ap[:, 0:512]),
                   reads=(sbk2,), writes=(kvsq,))
            b.emit("v", lambda E: E.scalar_tensor_tensor(kvn32.ap, pb.ap[:, 0:512],
                                                         g_col.ap, kvsq.ap,
                                                         ALU.mult, ALU.mult),
                   reads=(pb, g_col, kvsq), writes=(kvn32,))
        else:
            b.emit("v", lambda E: E.tensor_copy(kvn32.ap, pb.ap[:, 0:512]),
                   reads=(pb,), writes=(kvn32,))
        b.emit("v", lambda E: E.tensor_copy(kvhi.ap, kvn32.ap), reads=(kvn32,), writes=(kvhi,))
        b.emit("v", lambda E: E.tensor_sub(kvlo.ap, kvn32.ap, kvhi.ap),
               reads=(kvn32, kvhi), writes=(kvlo,))
        for dst_hi_ap, dst_lo_ap in dst_pairs:
            b.dma("g", dst_t, dst_hi_ap, kvhi, kvhi.ap)
            b.dma("g", dst_t, dst_lo_ap, kvlo, kvlo.ap)

    kv_inv = kv_in.ap.rearrange("r (s hl d q) -> r s hl d q", s=2, hl=2, d=DH)
    pcall = [0]
    for hd in range(NKV):
        for qh in range(TQ // 512):
            pb = banks[5 + 2 * (pcall[0] % 2)]
            sbk2 = banks[6 - 2 * (pcall[0] % 2)]
            pcall[0] += 1
            proj_head(wk_hi, wk_lo, hd, qh, pb)
            dsts = []
            for dr in (2 * hd, 2 * hd + 1):
                dsts.append((kv_inv[dr, 0, 0, :, 512 * qh:512 * (qh + 1)],
                             kv_inv[dr, 0, 1, :, 512 * qh:512 * (qh + 1)]))
            rms_split_store(pb, gk_c, EPS_RMS, kv_in, dsts, True, sbk2,
                            par=(pcall[0] % 2))
            if wj[0] < NWJ:
                b.phase = "p0a"
                wsum_job(wj[0])
                wj[0] += 1
                b.phase = "p1kv"

    v32sb_b = kvn32_b
    v32p_b = kvsq_b
    vhip_b = kvhi_b
    vlop_b = kvlo_b
    for hd in range(NKV):
        for qh in range(TQ // 512):
            pb = banks[5 + 2 * (pcall[0] % 2)]
            sbk2 = banks[6 - 2 * (pcall[0] % 2)]
            pcall[0] += 1
            par = pcall[0] % 2
            v32sb, v32p, vhip, vlop = (v32sb_b[par], v32p_b[par], vhip_b[par],
                                       vlop_b[par])
            proj_head(wv_hi, wv_lo, hd, qh, pb)
            b.emit("v", lambda E, pb=pb, v32sb=v32sb: E.tensor_copy(v32sb.ap, pb.ap[:, 0:512]),
                   reads=(pb,), writes=(v32sb,))
            for p in range(4):
                b.emit("t", lambda E, p=p, sbk2=sbk2, v32sb=v32sb: E.transpose(
                    sbk2.ap[:, 128 * p:128 * (p + 1)],
                    v32sb.ap[:, 128 * p:128 * (p + 1)], C["identf"].ap),
                    reads=(v32sb, C["identf"]), writes=(sbk2,), inc=(p == 3))
            b.emit("v", lambda E, sbk2=sbk2, v32p=v32p: E.tensor_copy(v32p.ap, sbk2.ap),
                   reads=(sbk2,), writes=(v32p,))
            b.emit("v", lambda E, v32p=v32p, vhip=vhip: E.tensor_copy(vhip.ap, v32p.ap),
                   reads=(v32p,), writes=(vhip,))
            b.emit("v", lambda E, v32p=v32p, vhip=vhip, vlop=vlop: E.tensor_sub(
                vlop.ap, v32p.ap, vhip.ap), reads=(v32p, vhip), writes=(vlop,))
            for p in range(4):
                tok0 = 512 * qh + 128 * p
                for dr in (2 * hd, 2 * hd + 1):
                    vv = kv_inv[dr, 1].rearrange("hl d q -> hl (d q)").rearrange(
                        "hl (q d) -> hl q d", d=DH)
                    b.dma("g", kv_in, vv[0, tok0:tok0 + 128, :],
                          vhip, vhip.ap[:, 128 * p:128 * (p + 1)])
                    b.dma("g", kv_in, vv[1, tok0:tok0 + 128, :],
                          vlop, vlop.ap[:, 128 * p:128 * (p + 1)])
            if wj[0] < NWJ:
                b.phase = "p0a"
                wsum_job(wj[0])
                wj[0] += 1
                b.phase = "p1kv"

    ev_kv = b.collective("AllToAll", ALU.bypass, kv_in.ap.opt(), kv_out.ap.opt(),
                         reads=(kv_in,), writes=(kv_out,))

    b.phase = "p0a"
    while wj[0] < NWJ:
        wsum_job(wj[0])
        wj[0] += 1
    b.emit("t", lambda E: E.matmul(banks[0].ap[0:1, 0:3], C["onesfc"].ap, wacc.ap[:, 0:3],
                                   start=True, stop=True),
           reads=(C["onesfc"], wacc), writes=(banks[0],))
    b.emit("v", lambda E: E.tensor_copy(svals.ap[0:1, 0:3], banks[0].ap[0:1, 0:3]),
           reads=(banks[0],), writes=(svals,))
    b.dma("g", ar_in, ar_in.ap[0:1, 0:3], svals, svals.ap[0:1, 0:3])
    ev_ar = b.collective("AllReduce", ALU.add, ar_in.ap.opt(), ar_out.ap.opt(),
                         reads=(ar_in,), writes=(ar_out,))

    b.phase_barrier(p1b_tts)
    p1b.close()

    b.phase = "p1q"
    HD2 = D // 2
    wq_hi = sb1("wq_hi", [128, NDC * HD2], BF16)
    wq_lo = sb1("wq_lo", [128, NDC * HD2], BF16)
    wq32 = [sb1(f"wq32_{i}", [128, HD2], F32) for i in range(2)]
    for wh in range(2):
        split_weight(wq_in, lambda j: wq_in.ap[128 * j:128 * (j + 1), HD2 * wh:HD2 * (wh + 1)],
                     NDC, HD2, wq_hi, wq_lo, None, wq32, q=("y" if wh == 0 else "s"))
        for hd8 in range(NH // 2):
            hd = 8 * wh + hd8
            for qh in range(TQ // 512):
                pb = banks[5 + 2 * (pcall[0] % 2)]
                sbk2 = banks[6 - 2 * (pcall[0] % 2)]
                pcall[0] += 1
                combos = ((wq_hi, ln_hi), (wq_hi, ln_lo), (wq_lo, ln_hi))
                n_mm = 3 * NDC
                k = 0
                for (wt, lt) in combos:
                    for cchunk in range(NDC):
                        wap = wt.ap[:, HD2 * cchunk + DH * hd8:HD2 * cchunk + DH * (hd8 + 1)]
                        lap = lt.ap.rearrange("p (c q) -> p c q", c=NDC)[
                            :, cchunk, 512 * qh:512 * (qh + 1)]
                        b.emit("t", lambda E, wap=wap, lap=lap, st=(k == 0), sp=(k == n_mm - 1),
                               pb=pb: E.matmul(pb.ap[:, 0:512], wap, lap, start=st, stop=sp),
                               reads=(wt, lt), writes=(pb,), inc=(k == n_mm - 1))
                        k += 1
                qv = q_ain.ap.rearrange("r (hl g d q) -> r hl g d q", hl=2, g=2, d=DH)
                dst_hi = qv[hd // 2, 0, hd % 2, :, 512 * qh:512 * (qh + 1)]
                dst_lo = qv[hd // 2, 1, hd % 2, :, 512 * qh:512 * (qh + 1)]
                rms_split_store(pb, gq_c, EPS_RMS, q_ain, [(dst_hi, dst_lo)], True,
                                sbk2, par=(pcall[0] % 2))


    ev_q = b.collective("AllToAll", ALU.bypass, q_ain.ap.opt(), q_aout.ap.opt(),
                        reads=(q_ain,), writes=(q_aout,))

    b.phase = "p0a"
    b.dma("g", svals, svals.ap[0:1, 4:7], ar_out, ar_out.ap[0:1, 0:3])
    b.emit("v", lambda E: E.tensor_scalar(svals.ap[0:1, 4:7], svals.ap[0:1, 4:7],
                                          1.0 / (FI * D), EPS_Q, ALU.mult, ALU.max),
           reads=(svals,), writes=(svals,))
    b.emit("v", lambda E: E.tensor_scalar(svals.ap[0:1, 0:3], svals.ap[0:1, 4:7],
                                          0.5, None, ALU.mult), reads=(svals,), writes=(svals,))
    b.emit("t", lambda E: E.matmul(banks[0].ap[:, 0:3], C["onesfr"].ap, svals.ap[0:1, 0:3],
                                   start=True, stop=True),
           reads=(C["onesfr"], svals), writes=(banks[0],))
    b.emit("v", lambda E: E.tensor_copy(shalfB.ap[:, 0:3], banks[0].ap[:, 0:3]),
           reads=(banks[0],), writes=(shalfB,))
    b.emit("v", lambda E: E.tensor_scalar(shalfB.ap[:, 4:7], shalfB.ap[:, 0:3], -1.0,
                                          None, ALU.mult), reads=(shalfB,), writes=(shalfB,))

    b.phase = "p0b"
    tern_g = tern_in.ap[0:2 * GPART].rearrange("(m c p o) -> m c p o", m=2, c=NDC, p=128)
    tern_dv = tern_in.ap[2 * GPART:].rearrange("(bq p o) -> bq p o", bq=WDS // 128, p=128)

    def quant_job(i, jb):
        mat, src_t, src_ap, cols = wtiles[i]
        st = wstage[i % 2]
        b.dma("g", st, st.ap[:, 0:cols], src_t, src_ap)
        b.emit("s", lambda E, st=st, cols=cols: E.activation(q1.ap[:, 0:cols],
                                                             st.ap[:, 0:cols], AF.Abs),
               reads=(st,), writes=(q1,))
        b.emit("s", lambda E, st=st, cols=cols: E.sign(q2.ap[:, 0:cols], st.ap[:, 0:cols]),
               reads=(st,), writes=(q2,))
        b.emit("v", lambda E, cols=cols, mat=mat: E.tensor_single_scalar(
            q1.ap[:, 0:cols], q1.ap[:, 0:cols], shalfB.ap[:, mat:mat + 1], ALU.is_gt),
            reads=(q1, shalfB), writes=(q1,))
        b.emit("v", lambda E, cols=cols: E.tensor_mul(q3.ap[:, 0:cols], q1.ap[:, 0:cols],
                                                      q2.ap[:, 0:cols]),
            reads=(q1, q2), writes=(q3,))
        j = i % (WGS // 128) if mat < 2 else i - 2 * (WGS // 128)
        for grp in range(cols // 1024):
            bk = jb
            bkb = bk.ap.bitcast(BF16)
            for p in range(8):
                cc = 8 * grp + p
                b.emit("t", lambda E, bkb=bkb, p=p, cc=cc, bk=bk: E.transpose(
                    bkb[:, 128 * p:128 * (p + 1)], q3.ap[:, 128 * cc:128 * (cc + 1)],
                    C["identb"].ap), reads=(q3, C["identb"]), writes=(bk,), inc=(p == 7))
            b.emit("v", lambda E, bkb=bkb, bk=bk: E.tensor_copy(q4.ap, bkb[:, 0:1024]),
                   reads=(bk,), writes=(q4,))
            if mat < 2:
                dst = tern_g[mat, 8 * grp:8 * grp + 8, :, 128 * j:128 * (j + 1)]
            else:
                dst = tern_dv[8 * grp:8 * grp + 8, :, 128 * j:128 * (j + 1)]
            b.dma("g", tern_in, dst.transpose([1, 0, 2]),
                  q4, q4.ap.rearrange("p (c f) -> p c f", c=8))

    b.phase = "p2"
    b.phase_barrier(p1_tts)
    p1.close()
    p2 = ExitStack()
    p2_tts = []

    def sb2(name, shape, dtype):
        t = TT(p2.enter_context(nc.sbuf_tensor(name, shape, dtype)).ap())
        p2_tts.append(t)
        return t

    kfull = sb2("kfull", [128, 2 * S], BF16)
    vfull = sb2("vfull", [128, 2 * S], BF16)
    qfull = sb2("qfull", [128, 2 * S], BF16)
    e32b = [sb2(f"e32b{i}", [128, 512], F32) for i in range(4)]
    ehib = [sb2(f"ehib{i}", [128, 512], BF16) for i in range(4)]
    elob = [sb2(f"elob{i}", [128, 512], BF16) for i in range(4)]
    o32s = sb2("o32s", [128, 512], F32)
    ohis = sb2("ohis", [128, 512], BF16)
    olos = sb2("olos", [128, 512], BF16)
    recrow = sb2("recrow", [1, 512], F32)
    sclc = sb2("sclc", [128, 1], F32)
    b.emit("v", lambda E: E.memset(sclc.ap, 1.0 / float(np.sqrt(128.0))), writes=(sclc,))
    wos32 = [sb2(f"wos32_{i}", [128, D], F32) for i in range(2)]
    woh = [sb2(f"woh{i}", [128, D], BF16) for i in range(2)]
    wol = [sb2(f"wol{i}", [128, D], BF16) for i in range(2)]
    wo_sv = wo_split_d.ap.rearrange("hl j p d -> hl j p d")

    def wo_job(j):
        w32 = wos32[j % 2]
        wh, wl = woh[j % 2], wol[j % 2]
        b.dma("s", w32, w32.ap, wo_in, wo_in.ap[128 * j:128 * (j + 1), :])
        b.emit("s", lambda E, w32=w32, wh=wh: E.activation(wh.ap, w32.ap, AF.Copy),
               reads=(w32,), writes=(wh,))
        b.emit("v", lambda E, w32=w32, wh=wh, wl=wl: E.tensor_sub(wl.ap, w32.ap, wh.ap),
               reads=(w32, wh), writes=(wl,))
        b.dma("s", wo_split_d, wo_sv[0, j], wh, wh.ap)
        b.dma("s", wo_split_d, wo_sv[1, j], wl, wl.ap)

    def chunk_rank(ch):
        return ch if ch < NC else 15 - ch

    def chunk_local(ch):
        return 0 if ch < NC else CH

    kv_ov = kv_out.ap.rearrange("r (s hl d q) -> r s hl d q", s=2, hl=2, d=DH)
    q_ov = q_aout.ap.rearrange("r (hl g d q) -> r hl g d q", hl=2, g=2, d=DH)
    o_iv0 = o_ain0.ap.rearrange("r (hl d q) -> r hl d q", hl=2, d=DH)
    o_iv1 = o_ain1.ap.rearrange("r (hl d q) -> r hl d q", hl=2, d=DH)

    for ch in range(16):
        r, lc = chunk_rank(ch), chunk_local(ch)
        for hl in range(2):
            b.dma("y", kfull, kfull.ap[:, hl * S + ch * CH:hl * S + (ch + 1) * CH],
                  kv_out, kv_ov[r, 0, hl, :, lc:lc + CH])
            vsrc = kv_ov[r, 1, hl].rearrange("d q -> (d q)").rearrange(
                "(q d) -> q d", d=DH)[lc:lc + CH, :]
            vsrc = vsrc.rearrange("(sg p) d -> p sg d", p=128)
            vdst = vfull.ap.rearrange("p (hl f) -> p hl f", hl=2)[
                :, hl, ch * CH:(ch + 1) * CH].rearrange("p (sg d) -> p sg d", d=128)
            b.dma("y", vfull, vdst, kv_out, vsrc)

    NQB = S // CH
    KTC = CH // 128
    SB0, SGB, OB0 = 0, 4, 6
    for hloc in range(2):
        for ch in range(16):
            r, lc = chunk_rank(ch), chunk_local(ch)
            for hl in range(2):
                b.dma("y", qfull, qfull.ap[:, hl * S + ch * CH:hl * S + (ch + 1) * CH],
                      q_aout, q_ov[r, hl, hloc, :, lc:lc + CH])
        for qb in range(NQB):
            G0 = qb * CH
            dstart = G0 // 128
            nkt = dstart + KTC
            ob = banks[OB0 + qb % 2]
            sgb = banks[SGB + qb % 2]

            def s_mms(kt):
                sbk = banks[SB0 + kt % 4]
                qcol0 = max(0, 128 * (kt - dstart))
                N = CH - qcol0
                combos = ((0, 0), (0, 1), (1, 0))
                for mi, (khl, qhl) in enumerate(combos):
                    kap = kfull.ap[:, khl * S + 128 * kt:khl * S + 128 * (kt + 1)]
                    qap = qfull.ap[:, qhl * S + G0 + qcol0:qhl * S + G0 + CH]
                    b.emit("t", lambda E, kap=kap, qap=qap, sbk=sbk, qcol0=qcol0,
                           st=(mi == 0), sp=(mi == 2), N=N: E.matmul(
                               sbk.ap[:, qcol0:qcol0 + N], kap, qap, start=st, stop=sp),
                           reads=(kfull, qfull), writes=(sbk,), inc=(mi == 2))

            def e_path(kt):
                sbk = banks[SB0 + kt % 4]
                qcol0 = max(0, 128 * (kt - dstart))
                N = CH - qcol0
                e32 = e32b[kt % 4]
                b.emit("s", lambda E, sbk=sbk, e32=e32, qcol0=qcol0, N=N: E.activation(
                    e32.ap[:, 0:N], sbk.ap[:, qcol0:qcol0 + N], AF.Exp,
                    bias=ebias.ap[:, 0:1], scale=sclc.ap[:, 0:1]),
                    reads=(sbk, ebias, sclc), writes=(e32,))
                if kt >= dstart:
                    b.emit("v", lambda E, e32=e32: E.tensor_mul(
                        e32.ap[:, 0:128], e32.ap[:, 0:128], C["tri"].ap),
                        reads=(e32, C["tri"]), writes=(e32,))
                ehi, elo = ehib[kt % 4], elob[kt % 4]
                b.emit("s", lambda E, e32=e32, ehi=ehi, N=N: E.activation(
                    ehi.ap[:, 0:N], e32.ap[:, 0:N], AF.Copy), reads=(e32,), writes=(ehi,))
                b.emit("v", lambda E, e32=e32, ehi=ehi, elo=elo, N=N: E.tensor_sub(
                    elo.ap[:, 0:N], e32.ap[:, 0:N], ehi.ap[:, 0:N]),
                    reads=(e32, ehi), writes=(elo,))

            def sig_pv(kt):
                qcol0 = max(0, 128 * (kt - dstart))
                N = CH - qcol0
                ehi, elo = ehib[kt % 4], elob[kt % 4]
                last = (kt == nkt - 1)
                for pi, et in enumerate((ehi, elo)):
                    b.emit("t", lambda E, et=et, sgb=sgb, qcol0=qcol0, N=N,
                           st=(kt == 0 and pi == 0), sp=(last and pi == 1): E.matmul(
                               sgb.ap[0:1, qcol0:qcol0 + N], C["onesb"].ap, et.ap[:, 0:N],
                               start=st, stop=sp),
                           reads=(et, C["onesb"]), writes=(sgb,), inc=(pi == 1))
                pvcombos = ((0, ehi), (0, elo), (1, ehi))
                for mi, (vhl, et) in enumerate(pvcombos):
                    vap = vfull.ap[:, vhl * S + 128 * kt:vhl * S + 128 * (kt + 1)]
                    b.emit("t", lambda E, vap=vap, et=et, ob=ob, qcol0=qcol0, N=N,
                           st=(kt == 0 and mi == 0), sp=(last and mi == 2): E.matmul(
                               ob.ap[:, qcol0:qcol0 + N], vap, et.ap[:, 0:N],
                               start=st, stop=sp),
                           reads=(vfull, et), writes=(ob,), inc=(mi == 2))

            DEPTH = 3
            for kt in range(nkt):
                s_mms(kt)
                e_path(kt)
                if kt >= DEPTH:
                    sig_pv(kt - DEPTH)
            for kt in range(max(0, nkt - DEPTH), nkt):
                sig_pv(kt)
            b.phase = "p2ep"
            b.emit("v", lambda E, sgb=sgb: E.reciprocal(recrow.ap[0:1, 0:CH],
                                                        sgb.ap[0:1, 0:CH]),
                   reads=(sgb,), writes=(recrow,))
            rb = banks[SB0 + (nkt + 1) % 4]
            b.emit("t", lambda E, rb=rb: E.matmul(rb.ap[:, 0:CH], C["onesfr"].ap,
                                                  recrow.ap[0:1, 0:CH], start=True, stop=True),
                   reads=(C["onesfr"], recrow), writes=(rb,))
            b.emit("v", lambda E, rb=rb: E.tensor_copy(e32b[0].ap[:, 0:CH], rb.ap[:, 0:CH]),
                   reads=(rb,), writes=(e32b[0],))
            b.emit("v", lambda E, ob=ob: E.tensor_mul(o32s.ap[:, 0:CH],
                                                      ob.ap[:, 0:CH], e32b[0].ap[:, 0:CH]),
                   reads=(ob, e32b[0]), writes=(o32s,))
            b.emit("v", lambda E: E.tensor_copy(ohis.ap[:, 0:CH], o32s.ap[:, 0:CH]),
                   reads=(o32s,), writes=(ohis,))
            b.emit("v", lambda E: E.tensor_sub(olos.ap[:, 0:CH], o32s.ap[:, 0:CH],
                                               ohis.ap[:, 0:CH]), reads=(o32s, ohis), writes=(olos,))
            r, lc = chunk_rank(qb), chunk_local(qb)
            o_t, o_ivh = ((o_ain0, o_iv0), (o_ain1, o_iv1))[hloc]
            b.dma("y", o_t, o_ivh[r, 0, :, lc:lc + CH], ohis, ohis.ap[:, 0:CH])
            b.dma("y", o_t, o_ivh[r, 1, :, lc:lc + CH], olos, olos.ap[:, 0:CH])
            b.phase = "p2"
            if hloc == 0:
                b.phase = "p0b"
                jb = banks[SGB + (qb + 1) % 2]
                quant_job(2 * qb, jb)
                quant_job(2 * qb + 1, jb)
                wo_job(qb)
                if qb == 3:
                    ev_ag_g = b.collective(
                        "AllGather", ALU.bypass, tern_in.ap[0:GPART].opt(),
                        tern_outg.ap.opt(), reads=(tern_in,), writes=(tern_outg,))
                if qb == 7:
                    ev_ag_u = b.collective(
                        "AllGather", ALU.bypass, tern_in.ap[GPART:2 * GPART].opt(),
                        tern_outu.ap.opt(), reads=(tern_in,), writes=(tern_outu,))
                b.phase = "p2"

        if hloc == 0:
            ev_o0 = b.collective("AllToAll", ALU.bypass, o_ain0.ap.opt(),
                                 o_aout0.ap.opt(), reads=(o_ain0,), writes=(o_aout0,))
            ev_ag_d = b.collective(
                "AllGather", ALU.bypass, tern_in.ap[2 * GPART:].opt(),
                tern_outd.ap.opt(), reads=(tern_in,), writes=(tern_outd,))

    ev_o1 = b.collective("AllToAll", ALU.bypass, o_ain1.ap.opt(), o_aout1.ap.opt(),
                         reads=(o_ain1,), writes=(o_aout1,))

    b.phase = "p3"
    b.phase_barrier(p2_tts)
    p2.close()
    b.phase_barrier(p0_tts)
    p0.close()
    p3 = ExitStack()
    p3_tts = []

    def sb3(name, shape, dtype):
        t = TT(p3.enter_context(nc.sbuf_tensor(name, shape, dtype)).ap())
        p3_tts.append(t)
        return t

    obuf_h = [sb3(f"obuf_h{i}", [128, 2 * NC * TQ], BF16) for i in range(2)]
    wo_g = [sb3(f"wo_g{i}", [128, 2 * NH * 512], BF16) for i in range(2)]
    xg = [sb3(f"xg{i}", [128, NXT * 512], F32) for i in range(2)]
    x1g = [sb3(f"x1g{i}", [128, NXT * 512], F32) for i in range(2)]

    o_ov0 = o_aout0.ap.rearrange("r (hl d q) -> r hl d q", hl=2, d=DH)
    o_ov1 = o_aout1.ap.rearrange("r (hl d q) -> r hl d q", hl=2, d=DH)
    obq = [obuf_h[i].ap.rearrange("p (b q) -> p b q", b=2 * NC) for i in range(2)]

    def load_obuf(half, ov, ot):
        for srcr in range(NC):
            for hl in range(2):
                qn = ("y", "s", "g")[(2 * srcr + hl) % 3]
                b.dma(qn, obuf_h[half], obq[half][:, hl * NC + srcr, :], ot, ov[srcr, hl])

    load_obuf(0, o_ov0, o_aout0)
    wo_gv = [wo_g[i].ap.rearrange("p (b c) -> p b c", b=2 * NH) for i in range(2)]

    def load_wog(g, slot):
        for hl in range(2):
            for hd in range(NDC):
                b.dma("y" if hd % 2 == 0 else "s", wo_g[slot],
                      wo_gv[slot][:, hl * NH + hd, :], wo_split_d,
                      wo_sv[hl, hd][:, 512 * g:512 * (g + 1)])
        for ti in range(NXT):
            b.dma("g", xg[slot], xg[slot].ap[:, 512 * ti:512 * (ti + 1)],
                  x_tm_d, x_tm_d.ap[ti][:, 512 * g:512 * (g + 1)])

    load_wog(0, 0)
    load_wog(1, 1)
    load_obuf(1, o_ov1, o_aout1)
    combos = ((0, 0), (0, 1), (1, 0))
    for g in range(4):
        slot = g % 2
        if g + 1 < 4 and g >= 1:
            load_wog(g + 1, 1 - slot)
        for ti in range(NXT):
            pb = banks[ti % 2]
            k = 0
            for half in range(2):
                for srcr in range(NC):
                    h = 2 * srcr + half
                    for (ohl, whl) in combos:
                        oap = obq[half][:, ohl * NC + srcr, 128 * ti:128 * (ti + 1)]
                        wap = wo_gv[slot][:, whl * NH + h, :]
                        b.emit("t", lambda E, oap=oap, wap=wap, pb=pb, st=(k == 0),
                               sp=(k == 3 * NH - 1): E.matmul(
                                   pb.ap[:, 0:512], oap, wap, start=st, stop=sp),
                               reads=(obuf_h[half], wo_g[slot]), writes=(pb,),
                               inc=(k == 3 * NH - 1))
                        k += 1
            b.emit("v", lambda E, pb=pb, ti=ti, slot=slot: E.tensor_add(
                x1g[slot].ap[:, 512 * ti:512 * (ti + 1)], pb.ap[:, 0:512],
                xg[slot].ap[:, 512 * ti:512 * (ti + 1)]),
                reads=(pb, xg[slot]), writes=(x1g[slot],))
            b.dma("y", x1_d, x1_d.ap[ti][:, 512 * g:512 * (g + 1)],
                  x1g[slot], x1g[slot].ap[:, 512 * ti:512 * (ti + 1)])

    b.phase = "p4"
    b.phase_barrier(p3_tts)
    p3.close()
    p4 = ExitStack()
    p4_tts = []

    def sb4(name, shape, dtype):
        t = TT(p4.enter_context(nc.sbuf_tensor(name, shape, dtype)).ap())
        p4_tts.append(t)
        return t

    T2 = 256
    NT2 = TQ // T2
    C1 = 1.0 / float(np.sqrt(float(D)))
    C2 = 1.0 / float(np.sqrt(float(FI)))

    x1t4 = [[sb4(f"x1t{p}_{t}", [128, D], F32) for t in range(2)] for p in range(2)]
    st8_b = [sb4(f"st8_{p}", [128, 10], F32) for p in range(2)]
    z1 = sb4("z1", [128, NDC * T2], F32)
    zt = sb4("zt", [128, NDC * T2], F32)
    xq1_b = [sb4(f"xq1_{p}", [128, NDC * T2], BF16) for p in range(2)]
    terng = [sb4(f"terng{i}", [128, NDC * 256], FP8) for i in range(2)]
    ternu = [sb4(f"ternu{i}", [128, NDC * 256], FP8) for i in range(2)]
    z2 = sb4("z2", [128, NIC * T2], F32)
    xnf = sb4("xnf", [128, 2 * NDC * T2], BF16)
    zq1 = sb4("zq1", [128, 512], F32)
    hst = [sb4("hst0", [128, 512], F32)] * 2
    hhi = [sb4("hhi0", [128, 512], BF16)] * 2
    hlo = [sb4("hlo0", [128, 512], BF16)] * 2
    u32 = [sb4("u320", [128, 512], F32)] * 2
    qss1 = sb4("qss1", [128, 256], F32)
    qss2 = sb4("qss2", [128, 256], F32)
    dqs_b = [sb4(f"dqs_{p}", [128, 512], F32) for p in range(2)]
    qrow = sb4("qrow", [1, 1024], F32)
    wdb = [sb4(f"wdb{i}", [128, D], FP8) for i in range(2)]
    rtmp1 = sb4("rtmp1", [128, 8], F32)
    rtmpB = sb4("rtmpB", [128, 4], F32)
    st8b2 = sb4("st8b2", [128, 4], F32)

    class _View:
        __slots__ = ("ap",)

        def __init__(self, ap):
            self.ap = ap
    z2bf = z2.ap.bitcast(BF16)
    xnfm_hi = _View(xnf.ap[:, 0:NDC * T2])
    xnfm_lo = _View(xnf.ap[:, NDC * T2:2 * NDC * T2])
    ztbf = zt.ap.bitcast(BF16)
    xn2lo = _View(ztbf[:, 4096:6144])
    xn2hi = _View(ztbf[:, 6144:8192])

    def butterfly_level(z, nchunk, hsz):
        zv = z.ap.rearrange("p (bq i q) -> p bq i q", bq=nchunk // (2 * hsz), i=2 * hsz)
        a = zv[:, :, 0:hsz, :]
        bb = zv[:, :, hsz:2 * hsz, :]
        b.emit("v", lambda E, a=a, bb=bb: E.tensor_add(a, a, bb), reads=(z,), writes=(z,))
        b.emit("v", lambda E, a=a, bb=bb: E.scalar_tensor_tensor(
            bb, bb, -2.0, a, ALU.mult, ALU.add), reads=(z,), writes=(z,))

    def h1_stage_fns(j):
        p = j % 2
        st8 = st8_b[p]
        xq1 = xq1_b[p]
        dqs = dqs_b[p]
        xn232 = _View(zt.ap[:, 0:D])
        fns = []

        def ln2_stats(stt):
            ti = 2 * j + stt
            xt = x1t4[p][stt]
            b.phase = "p4ln"
            b.dma("y", xt, xt.ap, x1_d, x1_d.ap[ti])
            b.emit("v", lambda E, xt=xt: E.reduce_sum(st8.ap[:, 0:1], xt.ap, AX),
                   reads=(xt,), writes=(st8,))
            b.emit("s", lambda E, xt=xt: E.activation(xn232.ap, xt.ap, AF.Square),
                   reads=(xt,), writes=(zt,))
            b.emit("v", lambda E: E.reduce_sum(st8.ap[:, 1:2], xn232.ap, AX),
                   reads=(zt,), writes=(st8,))
            b.emit("v", lambda E: E.tensor_scalar(st8.ap[:, 2:4], st8.ap[:, 0:2], 1.0 / D,
                                                  None, ALU.mult), reads=(st8,), writes=(st8,))
            b.emit("v", lambda E: E.tensor_scalar(st8.ap[:, 4:5], st8.ap[:, 2:3],
                                                  st8.ap[:, 2:3], None, ALU.mult),
                   reads=(st8,), writes=(st8,))
            b.emit("v", lambda E: E.tensor_sub(st8.ap[:, 5:6], st8.ap[:, 3:4], st8.ap[:, 4:5]),
                   reads=(st8,), writes=(st8,))
            b.emit("v", lambda E: E.tensor_scalar(st8.ap[:, 5:6], st8.ap[:, 5:6], EPS_LN,
                                                  None, ALU.add), reads=(st8,), writes=(st8,))
            b.emit("s", lambda E: E.activation(st8.ap[:, 5:6], st8.ap[:, 5:6], AF.Sqrt),
                   reads=(st8,), writes=(st8,))
            b.emit("v", lambda E: E.reciprocal(st8.ap[:, 6:7], st8.ap[:, 5:6]),
                   reads=(st8,), writes=(st8,))
            b.emit("v", lambda E: E.tensor_scalar(st8.ap[:, 7:8], st8.ap[:, 2:3],
                                                  st8.ap[:, 6:7], -1.0, ALU.mult, ALU.mult),
                   reads=(st8,), writes=(st8,))
            b.emit("v", lambda E, xt=xt: E.tensor_scalar(
                xn2hi.ap, xt.ap, st8.ap[:, 6:7], st8.ap[:, 7:8], ALU.mult, ALU.add),
                reads=(xt, st8), writes=(zt,))
            b.emit("v", lambda E, xt=xt: E.tensor_scalar(
                xn232.ap, xt.ap, st8.ap[:, 6:7], st8.ap[:, 7:8], ALU.mult, ALU.add),
                reads=(xt, st8), writes=(zt,))
            b.emit("v", lambda E: E.tensor_sub(xn2lo.ap, xn232.ap, xn2hi.ap),
                   reads=(zt,), writes=(zt,))

        def ln2_tr(stt, half, grp):
            srcT, dstT = ((xn2hi, xnfm_hi), (xn2lo, xnfm_lo))[half]
            b.phase = "p4ln"
            bk = banks[grp % 2]
            bkb = bk.ap.bitcast(BF16)
            for pp in range(8):
                cc = 8 * grp + pp
                b.emit("t", lambda E, bkb=bkb, pp=pp, srcT=srcT, cc=cc: E.transpose(
                    bkb[:, 128 * pp:128 * (pp + 1)], srcT.ap[:, 128 * cc:128 * (cc + 1)],
                    C["identb"].ap), reads=(zt, C["identb"]), writes=(bk,))

            def evac2(E, bkb=bkb, dstT=dstT, grp=grp, stt=stt):
                dst = dstT.ap.rearrange("p (c q) -> p c q", c=NDC)[
                    :, 8 * grp:8 * grp + 8, 128 * stt:128 * (stt + 1)]
                return E.tensor_copy(dst, bkb.rearrange("p (c q) -> p c q", c=8))
            b.emit("v", evac2, reads=(bk,), writes=(xnf,))

        def fw1(c4):
            b.phase = "p4fw1"
            for cc in range(4 * c4, 4 * c4 + 4):
                bk = banks[cc % 2]
                for pi, srcT in enumerate((xnfm_hi, xnfm_lo)):
                    b.emit("t", lambda E, srcT=srcT, cc=cc, bk=bk, st=(pi == 0),
                           sp=(pi == 1): E.matmul(
                        bk.ap[:, 0:T2], C["h128"].ap, srcT.ap[:, T2 * cc:T2 * (cc + 1)],
                        start=st, stop=sp), reads=(xnf, C["h128"]), writes=(bk,), inc=(pi == 1))
                b.emit("s", lambda E, cc=cc, bk=bk: E.activation(
                    z1.ap[:, T2 * cc:T2 * (cc + 1)], bk.ap[:, 0:T2], AF.Copy),
                    reads=(bk,), writes=(z1,))

        def bf1(lev):
            b.phase = "p4fw1"
            butterfly_level(z1, NDC, NDC // (2 ** (lev + 1)))

        def q1max():
            b.phase = "p4q1"
            b.emit("s", lambda E: E.activation(zt.ap, z1.ap, AF.Abs), reads=(z1,), writes=(zt,))
            cv = zt.ap.rearrange("p (c q) -> p c q", c=NDC)
            b.emit("v", lambda E, cv=cv: E.tensor_tensor(
                cv[:, 0:8, :], cv[:, 0:8, :], cv[:, 8:16, :], ALU.max),
                reads=(zt,), writes=(zt,))
            for nn in (4, 2, 1):
                b.emit("v", lambda E, nn=nn, cv=cv: E.tensor_tensor(
                    cv[:, 0:nn, :], cv[:, 0:nn, :], cv[:, nn:2 * nn, :], ALU.max),
                    reads=(zt,), writes=(zt,))
            for pp in range(2):
                b.emit("t", lambda E, pp=pp: E.transpose(
                    banks[0].ap[:, 128 * pp:128 * (pp + 1)], zt.ap[:, 128 * pp:128 * (pp + 1)],
                    C["identf"].ap), reads=(zt, C["identf"]), writes=(banks[0],))
            b.emit("v", lambda E: E.tensor_copy(zq1.ap[:, 256:512], banks[0].ap[:, 0:256]),
                   reads=(banks[0],), writes=(zq1,))
            b.emit("v", lambda E: E.reduce_max(st8.ap[:, 8:9], zq1.ap[:, 256:384], AX),
                   reads=(zq1,), writes=(st8,))
            b.emit("v", lambda E: E.reduce_max(st8.ap[:, 9:10], zq1.ap[:, 384:512], AX),
                   reads=(zq1,), writes=(st8,))

        def q1scales():
            b.phase = "p4q1"
            b.emit("v", lambda E: E.tensor_scalar(st8.ap[:, 8:10], st8.ap[:, 8:10], C1, EPS_Q,
                                                  ALU.mult, ALU.max), reads=(st8,), writes=(st8,))
            b.emit("v", lambda E: E.reciprocal(rtmp1.ap[:, 0:2], st8.ap[:, 8:10]),
                   reads=(st8,), writes=(rtmp1,))
            b.emit("v", lambda E: E.tensor_scalar(rtmp1.ap[:, 0:2], rtmp1.ap[:, 0:2], 127.0 * C1,
                                                  None, ALU.mult), reads=(rtmp1,), writes=(rtmp1,))
            b.emit("v", lambda E: E.tensor_scalar(rtmp1.ap[:, 2:3], st8.ap[:, 8:9],
                                                  shalfB.ap[:, 0:1], 2.0 / 127.0, ALU.mult, ALU.mult),
                   reads=(st8, shalfB), writes=(rtmp1,))
            b.emit("v", lambda E: E.tensor_scalar(rtmp1.ap[:, 3:4], st8.ap[:, 9:10],
                                                  shalfB.ap[:, 0:1], 2.0 / 127.0, ALU.mult, ALU.mult),
                   reads=(st8, shalfB), writes=(rtmp1,))
            b.emit("v", lambda E: E.tensor_scalar(rtmp1.ap[:, 4:5], st8.ap[:, 8:9],
                                                  shalfB.ap[:, 1:2], 2.0 / 127.0, ALU.mult, ALU.mult),
                   reads=(st8, shalfB), writes=(rtmp1,))
            b.emit("v", lambda E: E.tensor_scalar(rtmp1.ap[:, 5:6], st8.ap[:, 9:10],
                                                  shalfB.ap[:, 1:2], 2.0 / 127.0, ALU.mult, ALU.mult),
                   reads=(st8, shalfB), writes=(rtmp1,))

        def q1rows():
            b.phase = "p4q1"
            for pp in range(4):
                b.emit("t", lambda E, pp=pp: E.transpose(
                    banks[0].ap[0:1, 128 * pp:128 * (pp + 1)], rtmp1.ap[:, pp:pp + 1],
                    C["identf"].ap), reads=(rtmp1, C["identf"]), writes=(banks[0],))
            b.emit("v", lambda E: E.tensor_copy(qrow.ap[0:1, 0:512], banks[0].ap[0:1, 0:512]),
                   reads=(banks[0],), writes=(qrow,))
            for pp in range(2):
                b.emit("t", lambda E, pp=pp: E.transpose(
                    banks[0].ap[0:1, 128 * pp:128 * (pp + 1)], rtmp1.ap[:, 4 + pp:5 + pp],
                    C["identf"].ap), reads=(rtmp1, C["identf"]), writes=(banks[0],))
            b.emit("v", lambda E: E.tensor_copy(qrow.ap[0:1, 512:768], banks[0].ap[0:1, 0:256]),
                   reads=(banks[0],), writes=(qrow,))
            b.emit("t", lambda E: E.matmul(banks[1].ap[:, 0:256], C["onesfr"].ap,
                                           qrow.ap[0:1, 0:256], start=True, stop=True),
                   reads=(C["onesfr"], qrow), writes=(banks[1],))
            b.emit("v", lambda E: E.tensor_copy(qss1.ap[:, 0:256], banks[1].ap[:, 0:256]),
                   reads=(banks[1],), writes=(qss1,))
            b.emit("t", lambda E: E.matmul(banks[1].ap[:, 0:512], C["onesfr"].ap,
                                           qrow.ap[0:1, 256:768], start=True, stop=True),
                   reads=(C["onesfr"], qrow), writes=(banks[1],))
            b.emit("v", lambda E, dqs=dqs: E.tensor_copy(dqs.ap[:, 0:512], banks[1].ap[:, 0:512]),
                   reads=(banks[1],), writes=(dqs,))

        def q1big(hf):
            b.phase = "p4q1"
            HC = NDC // 2
            sl = slice(hf * HC * T2, (hf + 1) * HC * T2)
            b.emit("v", lambda E, sl=sl: E.tensor_tensor(
                zt.ap[:, sl].rearrange("p (c q) -> p c q", c=HC),
                z1.ap[:, sl].rearrange("p (c q) -> p c q", c=HC),
                qss1.ap[:, 0:256].unsqueeze(1).broadcast_to([128, HC, 256]), ALU.mult),
                reads=(z1, qss1), writes=(zt,))
            b.emit("v", lambda E, sl=sl: E.tensor_scalar(zt.ap[:, sl], zt.ap[:, sl], C32, C32,
                                                         ALU.add, ALU.subtract),
                   reads=(zt,), writes=(zt,))
            b.emit("v", lambda E, sl=sl: E.tensor_scalar(z1.ap[:, sl], zt.ap[:, sl], C52, C52,
                                                         ALU.add, ALU.subtract),
                   reads=(zt,), writes=(z1,))
            b.emit("v", lambda E, sl=sl, xq1=xq1: E.scalar_tensor_tensor(
                xq1.ap[:, sl], zt.ap[:, sl], blend.ap[:, 1:2], z1.ap[:, sl],
                ALU.mult, ALU.add), reads=(zt, z1, blend), writes=(xq1,))

        for stt in range(2):
            fns.append(lambda stt=stt: ln2_stats(stt))
            for half in range(2):
                for grp in range(2):
                    fns.append(lambda stt=stt, half=half, grp=grp: ln2_tr(stt, half, grp))
        for c4 in range(4):
            fns.append(lambda c4=c4: fw1(c4))
        for lev in range(4):
            fns.append(lambda lev=lev: bf1(lev))
        fns.append(q1max)
        fns.append(q1scales)
        fns.append(q1rows)
        fns.append(lambda: q1big(0))
        fns.append(lambda: q1big(1))
        return fns

    def h2_emit(j, drain):
        p = j % 2
        xq1 = xq1_b[p]
        dqs = dqs_b[p]
        zqa = _View(xq1.ap.bitcast(F32)[:, 0:1024])
        zqb = _View(xq1.ap.bitcast(F32)[:, 1024:2048])
        b.phase = "p4gu"
        for hs in range(NIC // 2):
            tg, tu = terng[hs % 2], ternu[hs % 2]
            r = (2 * hs) // (WGS // 128)
            col0 = (2 * hs) % (WGS // 128) * 128
            for m, dstT in ((0, tg), (1, tu)):
                tsrc = (tern_outg, tern_outu)[m]
                srcflat = tsrc.ap[r * GPART:(r + 1) * GPART]
                srcv = srcflat.rearrange("(c p o) -> c p o", c=NDC, p=128)[
                    :, :, col0:col0 + 256].transpose([1, 0, 2])
                b.dma(("s", "g")[m], dstT, dstT.ap.rearrange("p (c f) -> p c f", c=NDC),
                      tsrc, srcv)
            par = hs % 2
            gb, ub = (banks[2], banks[3]) if par == 0 else (banks[5], banks[6])
            hstp, u32p, hhip, hlop = hst[par], u32[par], hhi[par], hlo[par]
            for m, (wt, pb) in enumerate(((tg, gb), (tu, ub))):
                for dtl in range(2):
                    for cc in range(NDC):
                        b.emit("t", lambda E, wt=wt, pb=pb, cc=cc, dtl=dtl,
                               st=(cc == 0),
                               sp=(cc == NDC - 1): E.matmul(
                                   pb.ap[:, 256 * dtl:256 * (dtl + 1)],
                                   wt.ap[:, 256 * cc + 128 * dtl:256 * cc + 128 * (dtl + 1)],
                                   xq1.ap[:, T2 * cc:T2 * (cc + 1)], start=st, stop=sp),
                               reads=(wt, xq1), writes=(pb,),
                               inc=(dtl == 1 and cc == NDC - 1))
            dqv = dqs.ap[:, 0:256].unsqueeze(1).broadcast_to([128, 2, 256])
            dqv_u = dqs.ap[:, 256:512].unsqueeze(1).broadcast_to([128, 2, 256])
            b.emit("v", lambda E, gb=gb, hstp=hstp, dqv=dqv, dqs=dqs: E.tensor_tensor(
                hstp.ap.rearrange("p (c q) -> p c q", c=2),
                gb.ap[:, 0:512].rearrange("p (c q) -> p c q", c=2), dqv, ALU.mult),
                reads=(gb, dqs), writes=(hstp,))
            b.emit("s", lambda E, hstp=hstp: E.activation(hstp.ap, hstp.ap, AF.Silu),
                   reads=(hstp,), writes=(hstp,))
            b.emit("v", lambda E, ub=ub, u32p=u32p, dqv_u=dqv_u, dqs=dqs: E.tensor_tensor(
                u32p.ap.rearrange("p (c q) -> p c q", c=2),
                ub.ap[:, 0:512].rearrange("p (c q) -> p c q", c=2), dqv_u, ALU.mult),
                reads=(ub, dqs), writes=(u32p,))
            b.emit("v", lambda E, hstp=hstp, u32p=u32p: E.tensor_mul(
                hstp.ap, hstp.ap, u32p.ap), reads=(hstp, u32p), writes=(hstp,))
            b.emit("v", lambda E, hstp=hstp, hhip=hhip: E.tensor_copy(hhip.ap, hstp.ap),
                   reads=(hstp,), writes=(hhip,))
            b.emit("v", lambda E, hstp=hstp, hhip=hhip, hlop=hlop: E.tensor_sub(
                hlop.ap, hstp.ap, hhip.ap), reads=(hstp, hhip), writes=(hlop,))
            hbk = banks[4] if par == 0 else banks[7]
            for pi, srcT in enumerate((hhip, hlop)):
                b.emit("t", lambda E, srcT=srcT, st=(pi == 0), sp=(pi == 1), hbk=hbk:
                       E.matmul(hbk.ap[:, 0:512], C["h128"].ap, srcT.ap, start=st,
                                stop=sp), reads=(srcT, C["h128"]), writes=(hbk,),
                       inc=(pi == 1))
            b.emit("s", lambda E, hs=hs, hbk=hbk: E.activation(
                z2.ap[:, 512 * hs:512 * (hs + 1)], hbk.ap[:, 0:512], AF.Copy),
                reads=(hbk,), writes=(z2,))
            drain(1)
        drain(-1)
        b.phase = "p4fw2"
        for lev in range(6):
            butterfly_level(z2, NIC, NIC // (2 ** (lev + 1)))
        b.phase = "p4q2"
        for seg in range(16):
            src = z2.ap[:, 1024 * seg:1024 * (seg + 1)]
            if seg == 0:
                b.emit("s", lambda E, src=src: E.activation(zqa.ap, src, AF.Abs),
                       reads=(z2,), writes=(xq1,))
            else:
                b.emit("s", lambda E, src=src: E.activation(zqb.ap, src, AF.Abs),
                       reads=(z2,), writes=(xq1,))
                b.emit("v", lambda E: E.tensor_tensor(zqa.ap, zqa.ap, zqb.ap, ALU.max),
                       reads=(xq1,), writes=(xq1,))
        b.emit("v", lambda E: E.tensor_tensor(zqa.ap[:, 0:512], zqa.ap[:, 0:512],
                                              zqa.ap[:, 512:1024], ALU.max),
               reads=(xq1,), writes=(xq1,))
        b.emit("v", lambda E: E.tensor_tensor(zqa.ap[:, 0:256], zqa.ap[:, 0:256],
                                              zqa.ap[:, 256:512], ALU.max),
               reads=(xq1,), writes=(xq1,))
        for pp in range(2):
            b.emit("t", lambda E, pp=pp: E.transpose(
                banks[2].ap[:, 128 * pp:128 * (pp + 1)], zqa.ap[:, 128 * pp:128 * (pp + 1)],
                C["identf"].ap), reads=(xq1, C["identf"]), writes=(banks[2],))
        b.emit("v", lambda E: E.tensor_copy(zqb.ap[:, 0:256], banks[2].ap[:, 0:256]),
               reads=(banks[2],), writes=(xq1,))
        b.emit("v", lambda E: E.reduce_max(st8b2.ap[:, 0:1], zqb.ap[:, 0:128], AX),
               reads=(xq1,), writes=(st8b2,))
        b.emit("v", lambda E: E.reduce_max(st8b2.ap[:, 1:2], zqb.ap[:, 128:256], AX),
               reads=(xq1,), writes=(st8b2,))
        b.emit("v", lambda E: E.tensor_scalar(st8b2.ap[:, 0:2], st8b2.ap[:, 0:2], C2, EPS_Q,
                                              ALU.mult, ALU.max), reads=(st8b2,), writes=(st8b2,))
        b.emit("v", lambda E: E.reciprocal(rtmpB.ap[:, 0:2], st8b2.ap[:, 0:2]),
               reads=(st8b2,), writes=(rtmpB,))
        b.emit("v", lambda E: E.tensor_scalar(rtmpB.ap[:, 0:2], rtmpB.ap[:, 0:2], 127.0 * C2,
                                              None, ALU.mult), reads=(rtmpB,), writes=(rtmpB,))
        b.emit("v", lambda E: E.tensor_scalar(st8b2.ap[:, 0:2], st8b2.ap[:, 0:2],
                                              shalfB.ap[:, 2:3], 2.0 / 127.0, ALU.mult, ALU.mult),
               reads=(st8b2, shalfB), writes=(st8b2,))
        for pp in range(2):
            b.emit("t", lambda E, pp=pp: E.transpose(
                banks[2].ap[0:1, 128 * pp:128 * (pp + 1)], rtmpB.ap[:, pp:pp + 1],
                C["identf"].ap), reads=(rtmpB, C["identf"]), writes=(banks[2],))
        b.emit("v", lambda E: E.tensor_copy(qrow.ap[0:1, 768:1024], banks[2].ap[0:1, 0:256]),
               reads=(banks[2],), writes=(qrow,))
        b.emit("t", lambda E: E.matmul(banks[3].ap[:, 0:256], C["onesfr"].ap,
                                       qrow.ap[0:1, 768:1024], start=True, stop=True),
               reads=(C["onesfr"], qrow), writes=(banks[3],))
        b.emit("v", lambda E: E.tensor_copy(qss2.ap[:, 0:256], banks[3].ap[:, 0:256]),
               reads=(banks[3],), writes=(qss2,))
        b.phase = "p4dn"
        for seg in range(16):
            sl = slice(1024 * seg, 1024 * (seg + 1))
            b.emit("v", lambda E, sl=sl: E.tensor_tensor(
                zqa.ap.rearrange("p (c q) -> p c q", c=4),
                z2.ap[:, sl].rearrange("p (c q) -> p c q", c=4),
                qss2.ap[:, 0:256].unsqueeze(1).broadcast_to([128, 4, 256]), ALU.mult),
                reads=(z2, qss2), writes=(xq1,))
            b.emit("v", lambda E: E.tensor_scalar(zqa.ap, zqa.ap, C32, C32,
                                                  ALU.add, ALU.subtract),
                   reads=(xq1,), writes=(xq1,))
            b.emit("v", lambda E: E.tensor_scalar(zqb.ap, zqa.ap, C52, C52,
                                                  ALU.add, ALU.subtract),
                   reads=(xq1,), writes=(xq1,))
            b.emit("v", lambda E, seg=seg: E.scalar_tensor_tensor(
                z2bf[:, 1024 * seg:1024 * (seg + 1)], zqa.ap, blend.ap[:, 1:2],
                zqb.ap, ALU.mult, ALU.add),
                reads=(xq1, blend), writes=(z2,))
        for cc in range(NIC):
            wb_ = wdb[cc % 2]
            rr = cc // (WDS // 128)
            blk = cc % (WDS // 128)
            srcw = tern_outd.ap[rr * DPART + blk * 128 * D:
                                rr * DPART + (blk + 1) * 128 * D]
            b.dma(("s", "g", "y")[cc % 3], wb_, wb_.ap, tern_outd,
                  srcw.rearrange("(p o) -> p o", p=128))
            kk = 0
            for th in range(2):
                for g in range(4):
                    pb = banks[th * 4 + g]
                    xap = z2bf[:, T2 * cc + 128 * th:T2 * cc + 128 * (th + 1)]
                    b.emit("t", lambda E, pb=pb, g=g, wb_=wb_, xap=xap,
                           st=(cc == 0), sp=(cc == NIC - 1): E.matmul(
                               pb.ap[:, 0:512], xap,
                               wb_.ap[:, 512 * g:512 * (g + 1)], start=st, stop=sp),
                           reads=(z2, wb_), writes=(pb,), inc=(kk == 7))
                    kk += 1
        for th in range(2):
            for g in range(4):
                pb = banks[th * 4 + g]
                xt = x1t4[p][th]
                b.emit("v", lambda E, pb=pb, th=th, g=g, xt=xt: E.scalar_tensor_tensor(
                    xt.ap[:, 512 * g:512 * (g + 1)], pb.ap[:, 0:512],
                    st8b2.ap[:, th:th + 1], xt.ap[:, 512 * g:512 * (g + 1)],
                    ALU.mult, ALU.add), reads=(pb, st8b2, xt), writes=(xt,))
            ti = 2 * j + th
            b.dma("y", out_d, out_d.ap[128 * ti:128 * (ti + 1), :],
                  x1t4[p][th], x1t4[p][th].ap)

    pend = h1_stage_fns(0)
    while pend:
        pend.pop(0)()
    for j in range(NT2):
        pend = h1_stage_fns(j + 1) if j + 1 < NT2 else []

        def drain(k, pend=pend):
            if k < 0:
                while pend:
                    pend.pop(0)()
            else:
                for _ in range(k):
                    if pend:
                        pend.pop(0)()
        h2_emit(j, drain)


    tap_outs = {}
    if "kv_in" in taps:
        tap_outs["t_kv"] = dram("t_kv", [NC, KVP], BF16, "ExternalOutput")
        b.dma("y", tap_outs["t_kv"], tap_outs["t_kv"].ap[:, :], kv_in, kv_in.ap[:, :])
    if "x_tm" in taps:
        tap_outs["t_x"] = dram("t_x", [NXT, 128, D], F32, "ExternalOutput")
        b.dma("y", tap_outs["t_x"], tap_outs["t_x"].ap[:], x_tm_d, x_tm_d.ap[:])
    if "svals" in taps:
        tap_outs["t_sv"] = dram("t_sv", [1, 3], F32, "ExternalOutput")
        b.emit("y", lambda E: E.dma_start(out=tap_outs["t_sv"].ap, in_=svals.ap[0:1, 4:7]),
               reads=(svals,), writes=(tap_outs["t_sv"],), dma=True)
    if "q" in taps:
        tap_outs["t_q"] = dram("t_q", [NC, QOP], BF16, "ExternalOutput")
        b.dma("y", tap_outs["t_q"], tap_outs["t_q"].ap[:, :], q_aout, q_aout.ap[:, :])
    if "oa" in taps:
        tap_outs["t_oa"] = dram("t_oa", [2, NC, OHP], BF16, "ExternalOutput")
        b.dma("y", tap_outs["t_oa"], tap_outs["t_oa"].ap[0], o_aout0, o_aout0.ap[:, :])
        b.dma("y", tap_outs["t_oa"], tap_outs["t_oa"].ap[1], o_aout1, o_aout1.ap[:, :])
    if "x1" in taps:
        tap_outs["t_x1"] = dram("t_x1", [NXT, 128, D], F32, "ExternalOutput")
        b.dma("y", tap_outs["t_x1"], tap_outs["t_x1"].ap[:], x1_d, x1_d.ap[:])
    if "tern" in taps:
        tap_outs["t_tern"] = dram("t_tern", [TERN_STRIDE], FP8, "ExternalOutput")
        b.dma("y", tap_outs["t_tern"], tap_outs["t_tern"].ap[:], tern_in, tern_in.ap[:])

    for q in ("DY", "DS", "DG"):
        for i in range(NQSEM):
            b.wait("y", (f"{q}{i}", b.cnt[f"{q}{i}"]))
    for e in ("G", "V", "S", "T"):
        b.wait("y", (e, b.cnt[e]))

    b.finalize()
    return nc


def kernel(L0_attn_x, L0_attn_Wq, L0_attn_Wk, L0_attn_Wv, L0_attn_Wo,
           L0_attn_gq, L0_attn_gk, L0_ffn_w_gate, L0_ffn_w_up, L0_ffn_w_down,
           _taps=(), _return_raw=False):
    x = np.ascontiguousarray(np.asarray(L0_attn_x, dtype=np.float32).reshape(S, D))
    nc = build_program(taps=_taps)
    in_maps = []
    for c in range(NC):
        lo = c * CH
        hi = (15 - c) * CH
        x_own = np.ascontiguousarray(np.concatenate([x[lo:lo + CH], x[hi:hi + CH]], axis=0))
        in_maps.append({
            "x_own": x_own,
            "Wq": np.ascontiguousarray(L0_attn_Wq, dtype=np.float32),
            "Wk": np.ascontiguousarray(L0_attn_Wk, dtype=np.float32),
            "Wv": np.ascontiguousarray(L0_attn_Wv, dtype=np.float32),
            "Wo": np.ascontiguousarray(L0_attn_Wo, dtype=np.float32),
            "gq": np.ascontiguousarray(np.asarray(L0_attn_gq, dtype=np.float32).reshape(1, DH)),
            "gk": np.ascontiguousarray(np.asarray(L0_attn_gk, dtype=np.float32).reshape(1, DH)),
            "wg_sl": np.ascontiguousarray(np.asarray(L0_ffn_w_gate, dtype=np.float32)[c * WGS:(c + 1) * WGS, :]),
            "wu_sl": np.ascontiguousarray(np.asarray(L0_ffn_w_up, dtype=np.float32)[c * WGS:(c + 1) * WGS, :]),
            "wd_sl": np.ascontiguousarray(np.asarray(L0_ffn_w_down, dtype=np.float32)[:, c * WDS:(c + 1) * WDS]),
        })
    res = run_bass_kernel_spmd(nc, in_maps, core_ids=list(range(NC)))
    if _return_raw:
        return res
    full = np.zeros((S, D), dtype=np.float32)
    for c in range(NC):
        o = res.results[c]["out"]
        full[c * CH:(c + 1) * CH] = o[:CH]
        full[(15 - c) * CH:(16 - c) * CH] = o[CH:]
    return full.reshape(1, S, D)


# version 25
# speedup vs baseline: 1.1088x; 1.1088x over previous
from contextlib import ExitStack

import numpy as np
import ml_dtypes

import concourse.bass as bass
import concourse.mybir as mybir
from concourse.bass_utils import run_bass_kernel_spmd
from concourse.aot_env import allow_ici_all_to_all

F32 = mybir.dt.float32
FP8 = mybir.dt.float8e4
BF16 = mybir.dt.bfloat16
ALU = mybir.AluOpType
AF = mybir.ActivationFunctionType
AX = mybir.AxisListType.X

C32 = float(np.float32(1.5 * 2.0 ** 23))
C52 = float(np.float32(1.5 * 2.0 ** 52))

S = 8192
D = 2048
NH = 16
NKV = 4
DH = 128
DKV = NKV * DH
FI = 8192
NC = 8
EPS_LN = 1e-5
EPS_RMS = 1e-6
EPS_Q = 1e-5

CH = S // 16
TQ = 2 * CH
NXT = TQ // 128
NKT = S // 128
NDC = D // 128
NIC = FI // 128
WGS = FI // NC
WDS = FI // NC

STOP_AFTER = None
ENGS = ("y", "t", "v", "s", "g")
DMAQ = {"y": "DY", "s": "DS", "g": "DG"}
ESEM = {"t": "T", "v": "V", "s": "S", "g": "G", "y": "Y"}


class TT:
    __slots__ = ("ap", "w", "r")

    def __init__(self, ap):
        self.ap = ap
        self.w = {}
        self.r = {}

    def __getitem__(self, k):
        return self.ap[k]


NQSEM = 8


class Bld:
    def __init__(self, nc):
        self.nc = nc
        self.phase = "init"
        self.pend = {e: [] for e in ENGS}
        self.ops = {e: [] for e in ENGS}
        self.semh = {}
        self.cnt = {}
        for nm in ["T", "V", "S", "G", "Y"] + [f"{q}{i}" for q in ("DY", "DS", "DG")
                                              for i in range(NQSEM)]:
            self.semh[nm] = nc.alloc_semaphore(name="sem" + nm)
            self.cnt[nm] = 0
        self.qidx = {"DY": 0, "DS": 0, "DG": 0}
        self.ncc = 0
        self.hwm = {e: {} for e in ENGS}

    def wait(self, eng, ev):
        if ev is None:
            return
        sem, val = ev
        if self.hwm[eng].get(sem, 0) >= val:
            return
        self.hwm[eng][sem] = val
        h = self.semh[sem]
        self.ops[eng].append(lambda E, h=h, val=val: E.wait_ge(h, val))

    def waits(self, eng, evs):
        best = {}
        for ev in evs:
            if ev is None:
                continue
            sem, val = ev
            if best.get(sem, -1) < val:
                best[sem] = val
        for sem, val in best.items():
            self.wait(eng, (sem, val))

    def raw(self, eng, fn, inc):
        step = 16 if inc.startswith("D") else 1
        self.cnt[inc] += step
        val = self.cnt[inc]
        h = self.semh[inc]
        ph = self.phase

        def f(E, fn=fn, h=h, step=step, ph=ph, nc=self.nc):
            nc.push_named_scope(ph)
            try:
                fn(E).then_inc(h, step)
            finally:
                nc.pop_named_scope(ph)
        self.ops[eng].append(f)
        return (inc, val)

    def emit(self, eng, fn, reads=(), writes=(), dma=False, sem=None, inc=True):
        evs = []
        for t in reads:
            if t is not None:
                evs.extend(t.w.items())
        for t in writes:
            if t is not None:
                evs.extend(t.w.items())
                evs.extend(t.r.items())
        self.waits(eng, evs)
        if not inc:
            ph = self.phase

            def f(E, fn=fn, ph=ph, nc=self.nc):
                nc.push_named_scope(ph)
                try:
                    fn(E)
                finally:
                    nc.pop_named_scope(ph)
            self.ops[eng].append(f)
            for t in reads:
                if t is not None:
                    self.pend[eng].append((t, False))
            for t in writes:
                if t is not None:
                    self.pend[eng].append((t, True))
            return None
        if sem is None:
            if dma:
                q = DMAQ[eng]
                sem = f"{q}{self.qidx[q]}"
                self.qidx[q] = (self.qidx[q] + 1) % NQSEM
                self.wait(eng, (sem, self.cnt[sem]))
            else:
                sem = ESEM[eng]
        ev = self.raw(eng, fn, sem)
        s, v = ev
        for t in reads:
            if t is not None and t.r.get(s, 0) < v:
                t.r[s] = v
        for t in writes:
            if t is not None:
                t.w = {s: v}
                t.r = {}
        if self.pend[eng] and not sem.startswith("D") and sem in ESEM.values():
            for t, is_w in self.pend[eng]:
                if is_w:
                    t.w[s] = v
                else:
                    if t.r.get(s, 0) < v:
                        t.r[s] = v
            self.pend[eng] = []
        return ev

    def phase_barrier(self, tts):
        evs = []
        for t in tts:
            evs.extend(t.w.items())
            evs.extend(t.r.items())
        for e in ENGS:
            self.waits(e, evs)

    def collective(self, kind, op, ins_ap, outs_ap, reads=(), writes=()):
        nm = f"CC{self.ncc}"
        self.ncc += 1
        self.semh[nm] = self.nc.alloc_semaphore(name="sem" + nm)
        self.cnt[nm] = 0
        return self.emit("g", lambda E: E.collective_compute(
            kind, op, replica_groups=[list(range(NC))], ins=[ins_ap], outs=[outs_ap]),
            reads=reads, writes=writes, sem=nm)

    def dma(self, eng, dst, dst_ap, src, src_ap):
        return self.emit(eng, lambda E, o=dst_ap, i=src_ap: E.dma_start(out=o, in_=i),
                         reads=(src,), writes=(dst,), dma=True)

    def finalize(self):
        with allow_ici_all_to_all(), self.nc.Block() as block:
            for ename, attr in (("y", "sync"), ("t", "tensor"), ("v", "vector"),
                                ("s", "scalar"), ("g", "gpsimd")):
                def body(E, ops=self.ops[ename]):
                    for f in ops:
                        f(E)
                getattr(block, attr)(body)


def build_program(taps=()):
    nc = bass.Bass("TRN2", target_bir_lowering=False)
    b = Bld(nc)
    ctx = ExitStack()

    def sb(name, shape, dtype):
        return TT(ctx.enter_context(nc.sbuf_tensor(name, shape, dtype)).ap())

    def dram(name, shape, dtype, kind="Internal"):
        return TT(nc.dram_tensor(name, shape, dtype, kind=kind).ap())

    x_in = dram("x_own", [TQ, D], F32, "ExternalInput")
    wq_in = dram("Wq", [D, D], F32, "ExternalInput")
    wk_in = dram("Wk", [D, DKV], F32, "ExternalInput")
    wv_in = dram("Wv", [D, DKV], F32, "ExternalInput")
    wo_in = dram("Wo", [D, D], F32, "ExternalInput")
    gq_in = dram("gq", [1, DH], F32, "ExternalInput")
    gk_in = dram("gk", [1, DH], F32, "ExternalInput")
    wg_in = dram("wg_sl", [WGS, D], F32, "ExternalInput")
    wu_in = dram("wu_sl", [WGS, D], F32, "ExternalInput")
    wd_in = dram("wd_sl", [D, WDS], F32, "ExternalInput")
    out_d = dram("out", [TQ, D], F32, "ExternalOutput")

    KVP = 4 * DH * TQ
    kv_in = dram("kv_in", [NC, KVP], BF16)
    kv_out = dram("kv_out", [NC, KVP], BF16)
    QOP = 4 * DH * TQ
    q_ain = dram("q_ain", [NC, QOP], BF16)
    q_aout = dram("q_aout", [NC, QOP], BF16)
    OHP = 2 * DH * TQ
    o_ain0 = dram("o_ain0", [NC, OHP], BF16)
    o_aout0 = dram("o_aout0", [NC, OHP], BF16)
    o_ain1 = dram("o_ain1", [NC, OHP], BF16)
    o_aout1 = dram("o_aout1", [NC, OHP], BF16)
    wo_split_d = dram("wo_split_d", [2, NDC, 128, D], BF16)

    GPART = D * WGS
    DPART = WDS * D
    tern_in = dram("tern_in", [2 * GPART + DPART], FP8)
    tern_outg = dram("tern_outg", [NC * GPART], FP8)
    tern_outu = dram("tern_outu", [NC * GPART], FP8)
    tern_outd = dram("tern_outd", [NC * DPART], FP8)
    TERN_STRIDE = 2 * GPART + DPART

    ar_in = dram("ar_in", [1, 3], F32)
    ar_out = dram("ar_out", [1, 3], F32)

    x_tm_d = dram("x_tm_d", [NXT, 128, D], F32)
    x1_d = dram("x1_d", [NXT, 128, D], F32)

    Hm = np.ones((1, 1), dtype=np.float64)
    while Hm.shape[0] < 128:
        Hm = np.block([[Hm, Hm], [Hm, -Hm]])
    consts = {
        "h128": (np.asarray(Hm, dtype=ml_dtypes.bfloat16), BF16, [128, 128]),
        "identb": (np.eye(128, dtype=ml_dtypes.bfloat16), BF16, [128, 128]),
        "identf": (np.eye(128, dtype=np.float32), F32, [128, 128]),
        "onesb": (np.ones((128, 1), dtype=ml_dtypes.bfloat16), BF16, [128, 1]),
        "onesfr": (np.ones((1, 128), dtype=np.float32), F32, [1, 128]),
        "onesfc": (np.ones((128, 1), dtype=np.float32), F32, [128, 1]),
        "tri": (np.tril(np.ones((128, 128), dtype=np.float32)).T.copy(), F32, [128, 128]),
    }
    C = {}
    for nm, (arr, dt, shape) in consts.items():
        h = nc.inline_tensor(np.ascontiguousarray(arr), name=nm + "_c")
        t = sb(nm, shape, dt)
        b.dma("y", t, t.ap[:, :], TT(h.ap()), h.ap())
        C[nm] = t

    gq_r = sb("gq_r", [1, 128], F32)
    gk_r = sb("gk_r", [1, 128], F32)
    gq_c = sb("gq_c", [128, 1], F32)
    gk_c = sb("gk_c", [128, 1], F32)
    b.dma("y", gq_r, gq_r.ap[:, :], gq_in, gq_in.ap)
    b.dma("y", gk_r, gk_r.ap[:, :], gk_in, gk_in.ap)
    b.dma("y", gq_c, gq_c.ap[:, :], gq_in, gq_in.ap.rearrange("a d -> d a"))
    b.dma("y", gk_c, gk_c.ap[:, :], gk_in, gk_in.ap.rearrange("a d -> d a"))

    banks = [TT(ctx.enter_context(nc.psum_tensor(f"bank{i}", [128, 512], F32)).ap())
             for i in range(8)]

    blend = sb("blend", [128, 2], F32)
    mtmp = sb("mtmp", [1, 8], F32)
    ebias = sb("ebias", [128, 1], F32)
    gabs = sb("gabs", [1, 256], F32)
    svals = sb("svals", [1, 8], F32)
    shalfB = sb("shalfB", [128, 8], F32)
    b.emit("v", lambda E: E.memset(blend.ap[:, 0:1], 1.0), writes=(blend,))
    b.emit("v", lambda E: E.tensor_scalar(blend.ap[:, 0:1], blend.ap[:, 0:1], C52, C52,
                                          ALU.add, ALU.subtract), reads=(blend,), writes=(blend,))
    b.emit("v", lambda E: E.tensor_scalar(blend.ap[:, 1:2], blend.ap[:, 0:1], -1.0, 1.0,
                                          ALU.mult, ALU.add), reads=(blend,), writes=(blend,))

    def emit_rne(eng, out_t, out_ap, in_t, in_ap, scale1, t1, t2):
        sc_reads = (in_t,)
        if isinstance(scale1, tuple):
            sc_t, sc_ap = scale1
            sc_reads = (in_t, sc_t)
        else:
            sc_ap = scale1
        b.emit(eng, lambda E: E.tensor_scalar(t1.ap, in_ap, sc_ap, C32, ALU.mult, ALU.add),
               reads=sc_reads, writes=(t1,))
        b.emit(eng, lambda E: E.tensor_scalar(t2.ap, t1.ap, C32, C52, ALU.subtract, ALU.add),
               reads=(t1,), writes=(t2,))
        b.emit(eng, lambda E: E.tensor_scalar(t2.ap, t2.ap, C52, None, ALU.subtract),
               reads=(t2,), writes=(t2,))
        b.emit(eng, lambda E: E.tensor_scalar(t1.ap, t1.ap, C32, None, ALU.subtract),
               reads=(t1,), writes=(t1,))
        return b.emit(eng, lambda E: E.scalar_tensor_tensor(out_ap, t1.ap, blend.ap[:, 1:2],
                                                            t2.ap, ALU.mult, ALU.add),
                      reads=(t1, t2, blend), writes=(out_t,))

    b.phase = "p0a"

    p0 = ExitStack()
    p0_tts = []

    def sb0(name, shape, dtype):
        t = TT(p0.enter_context(nc.sbuf_tensor(name, shape, dtype)).ap())
        p0_tts.append(t)
        return t

    wstage = [sb0(f"wstage{i}", [128, D], F32) for i in range(2)]
    q1 = sb0("q1", [128, D], F32)
    q2 = sb0("q2", [128, D], F32)
    q3 = sb0("q3", [128, D], BF16)
    q4 = sb0("q4", [128, 1024], FP8)
    wacc = sb0("wacc", [128, 4], F32)
    wred = sb0("wred", [128, 2], F32)

    wtiles = []
    for j in range(WGS // 128):
        wtiles.append((0, wg_in, wg_in.ap[128 * j:128 * (j + 1), :], D))
    for j in range(WGS // 128):
        wtiles.append((1, wu_in, wu_in.ap[128 * j:128 * (j + 1), :], D))
    for j in range(D // 128):
        wtiles.append((2, wd_in, wd_in.ap[128 * j:128 * (j + 1), :], WDS))

    b.phase = "p1ln"
    p1 = ExitStack()
    p1_tts = []

    def sb1(name, shape, dtype):
        t = TT(p1.enter_context(nc.sbuf_tensor(name, shape, dtype)).ap())
        p1_tts.append(t)
        return t

    ln_hi = sb1("ln_hi", [128, NDC * TQ], BF16)
    ln_lo = sb1("ln_lo", [128, NDC * TQ], BF16)
    p1a = ExitStack()
    p1a_tts = []

    def sb1a(name, shape, dtype):
        t = TT(p1a.enter_context(nc.sbuf_tensor(name, shape, dtype)).ap())
        p1a_tts.append(t)
        return t

    xstage = [sb1a(f"xstage{i}", [128, D], F32) for i in range(2)]
    xsq_b = [sb1a("xsq0", [128, D], F32)] * 2
    xn32_b = [sb1a("xn320", [128, D], F32)] * 2
    xnhi_b = [sb1a(f"xnhi{i}", [128, D], BF16) for i in range(2)]
    xnlo_b = [sb1a(f"xnlo{i}", [128, D], BF16) for i in range(2)]
    lnst_b = [sb1a(f"lnst{i}", [128, 8], F32) for i in range(2)]

    b.emit("v", lambda E: E.memset(wacc.ap[:, 0:3], 0.0), writes=(wacc,))

    def wsum_job(i):
        mat, src_t, src_ap, cols = wtiles[i]
        stw = wstage[i % 2]
        b.dma("s", stw, stw.ap[:, 0:cols], src_t, src_ap)
        b.emit("v", lambda E, stw=stw, cols=cols: E.reduce_sum(
            wred.ap[:, 0:1], stw.ap[:, 0:cols], AX, apply_absolute_value=True),
            reads=(stw,), writes=(wred,))
        b.emit("v", lambda E, mat=mat: E.tensor_add(
            wacc.ap[:, mat:mat + 1], wacc.ap[:, mat:mat + 1], wred.ap[:, 0:1]),
            reads=(wacc, wred), writes=(wacc,))

    NWJ = len(wtiles)
    wj = [0]
    for i in range(NXT):
        st = xstage[i % 2]
        xsq = xsq_b[i % 2]
        xn32 = xn32_b[i % 2]
        xnhi_tm = xnhi_b[i % 2]
        xnlo_tm = xnlo_b[i % 2]
        lnst = lnst_b[i % 2]
        b.dma("y", st, st.ap[:, :], x_in, x_in.ap[128 * i:128 * (i + 1), :])
        b.emit("v", lambda E, st=st, lnst=lnst: E.reduce_sum(lnst.ap[:, 0:1], st.ap, AX),
               reads=(st,), writes=(lnst,))
        b.emit("s", lambda E, st=st, xsq=xsq: E.activation(xsq.ap, st.ap, AF.Square),
               reads=(st,), writes=(xsq,))
        b.emit("v", lambda E, lnst=lnst, xsq=xsq: E.reduce_sum(lnst.ap[:, 1:2], xsq.ap, AX),
               reads=(xsq,), writes=(lnst,))
        b.emit("v", lambda E, lnst=lnst: E.tensor_scalar(lnst.ap[:, 2:4], lnst.ap[:, 0:2], 1.0 / D,
                                              None, ALU.mult), reads=(lnst,), writes=(lnst,))
        b.emit("v", lambda E, lnst=lnst: E.tensor_scalar(lnst.ap[:, 4:5], lnst.ap[:, 2:3],
                                              lnst.ap[:, 2:3], None, ALU.mult),
               reads=(lnst,), writes=(lnst,))
        b.emit("v", lambda E, lnst=lnst: E.tensor_sub(lnst.ap[:, 5:6], lnst.ap[:, 3:4], lnst.ap[:, 4:5]),
               reads=(lnst,), writes=(lnst,))
        b.emit("v", lambda E, lnst=lnst: E.tensor_scalar(lnst.ap[:, 5:6], lnst.ap[:, 5:6], EPS_LN,
                                              None, ALU.add), reads=(lnst,), writes=(lnst,))
        b.emit("s", lambda E, lnst=lnst: E.activation(lnst.ap[:, 5:6], lnst.ap[:, 5:6], AF.Sqrt),
               reads=(lnst,), writes=(lnst,))
        b.emit("v", lambda E, lnst=lnst: E.reciprocal(lnst.ap[:, 6:7], lnst.ap[:, 5:6]),
               reads=(lnst,), writes=(lnst,))
        b.emit("v", lambda E, lnst=lnst: E.tensor_scalar(lnst.ap[:, 7:8], lnst.ap[:, 2:3],
                                              lnst.ap[:, 6:7], -1.0, ALU.mult, ALU.mult),
               reads=(lnst,), writes=(lnst,))
        b.emit("v", lambda E, st=st, lnst=lnst, xnhi_tm=xnhi_tm: E.tensor_scalar(
            xnhi_tm.ap, st.ap, lnst.ap[:, 6:7], lnst.ap[:, 7:8], ALU.mult, ALU.add),
            reads=(st, lnst), writes=(xnhi_tm,))
        b.emit("v", lambda E, st=st, lnst=lnst, xn32=xn32: E.tensor_scalar(
            xn32.ap, st.ap, lnst.ap[:, 6:7], lnst.ap[:, 7:8], ALU.mult, ALU.add),
            reads=(st, lnst), writes=(xn32,))
        b.emit("v", lambda E, xn32=xn32, xnhi_tm=xnhi_tm, xnlo_tm=xnlo_tm: E.tensor_sub(
            xnlo_tm.ap, xn32.ap, xnhi_tm.ap),
               reads=(xn32, xnhi_tm), writes=(xnlo_tm,))
        b.dma("g", x_tm_d, x_tm_d.ap[i], st, st.ap)
        for half, (srcT, dstT) in enumerate(((xnhi_tm, ln_hi), (xnlo_tm, ln_lo))):
            for grp in range(2):
                bk = banks[2 + (half * 2 + grp) % 2]
                bkb = bk.ap.bitcast(BF16)
                for p in range(8):
                    cch = 8 * grp + p
                    b.emit("t", lambda E, bkb=bkb, p=p, srcT=srcT, cch=cch: E.transpose(
                        bkb[:, 128 * p:128 * (p + 1)], srcT.ap[:, 128 * cch:128 * (cch + 1)],
                        C["identb"].ap), reads=(srcT, C["identb"]), writes=(bk,))
                def evac(E, bkb=bkb, dstT=dstT, grp=grp, i=i):
                    dst = dstT.ap.rearrange("p (c q) -> p c q", c=NDC)[
                        :, 8 * grp:8 * grp + 8, 128 * i:128 * (i + 1)]
                    return E.tensor_copy(dst, bkb.rearrange("p (c q) -> p c q", c=8))
                b.emit("v", evac, reads=(bk,), writes=(dstT,))
        b.phase = "p0a"
        for _ in range(2):
            if wj[0] < NWJ:
                wsum_job(wj[0])
                wj[0] += 1
        b.phase = "p1ln"

    b.phase_barrier(p1a_tts)
    p1a.close()

    b.phase = "p1m"

    b.emit("s", lambda E: E.activation(gabs.ap[0:1, 0:128], gq_r.ap, AF.Abs),
           reads=(gq_r,), writes=(gabs,))
    b.emit("s", lambda E: E.activation(gabs.ap[0:1, 128:256], gk_r.ap, AF.Abs),
           reads=(gk_r,), writes=(gabs,))
    b.emit("v", lambda E: E.reduce_max(mtmp.ap[0:1, 0:1], gabs.ap[0:1, 0:128], AX),
           reads=(gabs,), writes=(mtmp,))
    b.emit("v", lambda E: E.reduce_max(mtmp.ap[0:1, 1:2], gabs.ap[0:1, 128:256], AX),
           reads=(gabs,), writes=(mtmp,))
    b.emit("v", lambda E: E.tensor_scalar(mtmp.ap[0:1, 2:3], mtmp.ap[0:1, 0:1],
                                          mtmp.ap[0:1, 1:2], -float(np.sqrt(128.0)),
                                          ALU.mult, ALU.mult), reads=(mtmp,), writes=(mtmp,))
    b.emit("t", lambda E: E.matmul(banks[4].ap[:, 0:1], C["onesfr"].ap, mtmp.ap[0:1, 2:3],
                                   start=True, stop=True), reads=(C["onesfr"], mtmp), writes=(banks[4],))
    b.emit("v", lambda E: E.tensor_copy(ebias.ap, banks[4].ap[:, 0:1]),
           reads=(banks[4],), writes=(ebias,))

    def split_weight(src_t, src_ap_fn, ntiles, cols, hi_t, lo_t, stage, w32, q="y"):
        w32b = w32 if isinstance(w32, list) else [w32, w32]
        for j in range(ntiles):
            w32 = w32b[j % 2]
            b.dma(q, w32, w32.ap[:, 0:cols], src_t, src_ap_fn(j))
            b.emit("s", lambda E, j=j, w32=w32: E.activation(
                hi_t.ap[:, cols * j:cols * (j + 1)], w32.ap[:, 0:cols], AF.Copy),
                   reads=(w32,), writes=(hi_t,))
            b.emit("v", lambda E, j=j, w32=w32: E.tensor_sub(
                lo_t.ap[:, cols * j:cols * (j + 1)], w32.ap[:, 0:cols],
                hi_t.ap[:, cols * j:cols * (j + 1)]),
                   reads=(w32, hi_t), writes=(lo_t,))

    b.phase = "p1kv"
    kvsq_b = [sb1(f"kvsq{i}", [128, 512], F32) for i in range(2)]
    kvn32_b = [sb1(f"kvn32{i}", [128, 512], F32) for i in range(2)]
    kvhi_b = [sb1(f"kvhi{i}", [128, 512], BF16) for i in range(2)]
    kvlo_b = [sb1(f"kvlo{i}", [128, 512], BF16) for i in range(2)]
    rmsrow_b = [sb1(f"rmsrow{i}", [1, 512], F32) for i in range(2)]

    p1b = ExitStack()
    p1b_tts = []

    def sb1b(name, shape, dtype):
        t = TT(p1b.enter_context(nc.sbuf_tensor(name, shape, dtype)).ap())
        p1b_tts.append(t)
        return t

    wkv32 = [sb1b(f"wkv32_{i}", [128, DKV], F32) for i in range(2)]
    wk_hi = sb1b("wk_hi", [128, NDC * DKV], BF16)
    wk_lo = sb1b("wk_lo", [128, NDC * DKV], BF16)
    wv_hi = sb1b("wv_hi", [128, NDC * DKV], BF16)
    wv_lo = sb1b("wv_lo", [128, NDC * DKV], BF16)
    split_weight(wk_in, lambda j: wk_in.ap[128 * j:128 * (j + 1), :], NDC, DKV, wk_hi, wk_lo, None, wkv32)
    split_weight(wv_in, lambda j: wv_in.ap[128 * j:128 * (j + 1), :], NDC, DKV, wv_hi, wv_lo, None, wkv32)

    def proj_head(w_hi, w_lo, hd, qh, pb):
        combos = ((w_hi, ln_hi), (w_hi, ln_lo), (w_lo, ln_hi))
        n_mm = 3 * NDC
        k = 0
        for (wt, lt) in combos:
            for cchunk in range(NDC):
                wap = wt.ap[:, DKV * cchunk + DH * hd:DKV * cchunk + DH * (hd + 1)]
                lap = lt.ap.rearrange("p (c q) -> p c q", c=NDC)[
                    :, cchunk, 512 * qh:512 * (qh + 1)]
                b.emit("t", lambda E, wap=wap, lap=lap, st=(k == 0), sp=(k == n_mm - 1),
                       pb=pb: E.matmul(pb.ap[:, 0:512], wap, lap, start=st, stop=sp),
                       reads=(wt, lt), writes=(pb,), inc=(k == n_mm - 1))
                k += 1

    def rms_split_store(pb, g_col, eps, dst_t, dst_pairs, do_rms, sbk2=None, par=0):
        kvsq, kvn32, kvhi, kvlo, rmsrow = (kvsq_b[par], kvn32_b[par], kvhi_b[par],
                                           kvlo_b[par], rmsrow_b[par])
        if sbk2 is None:
            sbk2 = banks[6]
        if do_rms:
            b.emit("s", lambda E: E.activation(kvsq.ap, pb.ap[:, 0:512], AF.Square),
                   reads=(pb,), writes=(kvsq,))
            b.emit("t", lambda E, sbk2=sbk2: E.matmul(sbk2.ap[0:1, 0:512], C["onesfc"].ap,
                                           kvsq.ap, start=True, stop=True),
                   reads=(C["onesfc"], kvsq), writes=(sbk2,))
            b.emit("v", lambda E, sbk2=sbk2: E.tensor_scalar(rmsrow.ap, sbk2.ap[0:1, 0:512],
                                                  1.0 / DH, eps, ALU.mult, ALU.add),
                   reads=(sbk2,), writes=(rmsrow,))
            b.emit("s", lambda E: E.activation(rmsrow.ap, rmsrow.ap, AF.Sqrt),
                   reads=(rmsrow,), writes=(rmsrow,))
            b.emit("v", lambda E: E.reciprocal(rmsrow.ap, rmsrow.ap),
                   reads=(rmsrow,), writes=(rmsrow,))
            b.emit("t", lambda E, sbk2=sbk2: E.matmul(sbk2.ap[:, 0:512], C["onesfr"].ap,
                                           rmsrow.ap, start=True, stop=True),
                   reads=(C["onesfr"], rmsrow), writes=(sbk2,))
            b.emit("v", lambda E, sbk2=sbk2: E.tensor_copy(kvsq.ap, sbk2.ap[:, 0:512]),
                   reads=(sbk2,), writes=(kvsq,))
            b.emit("v", lambda E: E.scalar_tensor_tensor(kvn32.ap, pb.ap[:, 0:512],
                                                         g_col.ap, kvsq.ap,
                                                         ALU.mult, ALU.mult),
                   reads=(pb, g_col, kvsq), writes=(kvn32,))
        else:
            b.emit("v", lambda E: E.tensor_copy(kvn32.ap, pb.ap[:, 0:512]),
                   reads=(pb,), writes=(kvn32,))
        b.emit("v", lambda E: E.tensor_copy(kvhi.ap, kvn32.ap), reads=(kvn32,), writes=(kvhi,))
        b.emit("v", lambda E: E.tensor_sub(kvlo.ap, kvn32.ap, kvhi.ap),
               reads=(kvn32, kvhi), writes=(kvlo,))
        for dst_hi_ap, dst_lo_ap in dst_pairs:
            b.dma("g", dst_t, dst_hi_ap, kvhi, kvhi.ap)
            b.dma("g", dst_t, dst_lo_ap, kvlo, kvlo.ap)

    kv_inv = kv_in.ap.rearrange("r (s hl d q) -> r s hl d q", s=2, hl=2, d=DH)
    pcall = [0]
    for hd in range(NKV):
        for qh in range(TQ // 512):
            pb = banks[5 + 2 * (pcall[0] % 2)]
            sbk2 = banks[6 - 2 * (pcall[0] % 2)]
            pcall[0] += 1
            proj_head(wk_hi, wk_lo, hd, qh, pb)
            dsts = []
            for dr in (2 * hd, 2 * hd + 1):
                dsts.append((kv_inv[dr, 0, 0, :, 512 * qh:512 * (qh + 1)],
                             kv_inv[dr, 0, 1, :, 512 * qh:512 * (qh + 1)]))
            rms_split_store(pb, gk_c, EPS_RMS, kv_in, dsts, True, sbk2,
                            par=(pcall[0] % 2))
            if wj[0] < NWJ:
                b.phase = "p0a"
                wsum_job(wj[0])
                wj[0] += 1
                b.phase = "p1kv"

    v32sb_b = kvn32_b
    v32p_b = kvsq_b
    vhip_b = kvhi_b
    vlop_b = kvlo_b
    for hd in range(NKV):
        for qh in range(TQ // 512):
            pb = banks[5 + 2 * (pcall[0] % 2)]
            sbk2 = banks[6 - 2 * (pcall[0] % 2)]
            pcall[0] += 1
            par = pcall[0] % 2
            v32sb, v32p, vhip, vlop = (v32sb_b[par], v32p_b[par], vhip_b[par],
                                       vlop_b[par])
            proj_head(wv_hi, wv_lo, hd, qh, pb)
            b.emit("v", lambda E, pb=pb, v32sb=v32sb: E.tensor_copy(v32sb.ap, pb.ap[:, 0:512]),
                   reads=(pb,), writes=(v32sb,))
            for p in range(4):
                b.emit("t", lambda E, p=p, sbk2=sbk2, v32sb=v32sb: E.transpose(
                    sbk2.ap[:, 128 * p:128 * (p + 1)],
                    v32sb.ap[:, 128 * p:128 * (p + 1)], C["identf"].ap),
                    reads=(v32sb, C["identf"]), writes=(sbk2,), inc=(p == 3))
            b.emit("v", lambda E, sbk2=sbk2, v32p=v32p: E.tensor_copy(v32p.ap, sbk2.ap),
                   reads=(sbk2,), writes=(v32p,))
            b.emit("v", lambda E, v32p=v32p, vhip=vhip: E.tensor_copy(vhip.ap, v32p.ap),
                   reads=(v32p,), writes=(vhip,))
            b.emit("v", lambda E, v32p=v32p, vhip=vhip, vlop=vlop: E.tensor_sub(
                vlop.ap, v32p.ap, vhip.ap), reads=(v32p, vhip), writes=(vlop,))
            for p in range(4):
                tok0 = 512 * qh + 128 * p
                for dr in (2 * hd, 2 * hd + 1):
                    vv = kv_inv[dr, 1].rearrange("hl d q -> hl (d q)").rearrange(
                        "hl (q d) -> hl q d", d=DH)
                    b.dma("g", kv_in, vv[0, tok0:tok0 + 128, :],
                          vhip, vhip.ap[:, 128 * p:128 * (p + 1)])
                    b.dma("g", kv_in, vv[1, tok0:tok0 + 128, :],
                          vlop, vlop.ap[:, 128 * p:128 * (p + 1)])
            if wj[0] < NWJ:
                b.phase = "p0a"
                wsum_job(wj[0])
                wj[0] += 1
                b.phase = "p1kv"

    ev_kv = b.collective("AllToAll", ALU.bypass, kv_in.ap.opt(), kv_out.ap.opt(),
                         reads=(kv_in,), writes=(kv_out,))

    b.phase = "p0a"
    while wj[0] < NWJ:
        wsum_job(wj[0])
        wj[0] += 1
    b.emit("t", lambda E: E.matmul(banks[0].ap[0:1, 0:3], C["onesfc"].ap, wacc.ap[:, 0:3],
                                   start=True, stop=True),
           reads=(C["onesfc"], wacc), writes=(banks[0],))
    b.emit("v", lambda E: E.tensor_copy(svals.ap[0:1, 0:3], banks[0].ap[0:1, 0:3]),
           reads=(banks[0],), writes=(svals,))
    b.dma("g", ar_in, ar_in.ap[0:1, 0:3], svals, svals.ap[0:1, 0:3])
    ev_ar = b.collective("AllReduce", ALU.add, ar_in.ap.opt(), ar_out.ap.opt(),
                         reads=(ar_in,), writes=(ar_out,))

    b.phase_barrier(p1b_tts)
    p1b.close()

    b.phase = "p1q"
    HD2 = D // 2
    wq_hi = sb1("wq_hi", [128, NDC * HD2], BF16)
    wq_lo = sb1("wq_lo", [128, NDC * HD2], BF16)
    wq32 = [sb1(f"wq32_{i}", [128, HD2], F32) for i in range(2)]
    for wh in range(2):
        split_weight(wq_in, lambda j: wq_in.ap[128 * j:128 * (j + 1), HD2 * wh:HD2 * (wh + 1)],
                     NDC, HD2, wq_hi, wq_lo, None, wq32, q=("y" if wh == 0 else "s"))
        for hd8 in range(NH // 2):
            hd = 8 * wh + hd8
            for qh in range(TQ // 512):
                pb = banks[5 + 2 * (pcall[0] % 2)]
                sbk2 = banks[6 - 2 * (pcall[0] % 2)]
                pcall[0] += 1
                combos = ((wq_hi, ln_hi), (wq_hi, ln_lo), (wq_lo, ln_hi))
                n_mm = 3 * NDC
                k = 0
                for (wt, lt) in combos:
                    for cchunk in range(NDC):
                        wap = wt.ap[:, HD2 * cchunk + DH * hd8:HD2 * cchunk + DH * (hd8 + 1)]
                        lap = lt.ap.rearrange("p (c q) -> p c q", c=NDC)[
                            :, cchunk, 512 * qh:512 * (qh + 1)]
                        b.emit("t", lambda E, wap=wap, lap=lap, st=(k == 0), sp=(k == n_mm - 1),
                               pb=pb: E.matmul(pb.ap[:, 0:512], wap, lap, start=st, stop=sp),
                               reads=(wt, lt), writes=(pb,), inc=(k == n_mm - 1))
                        k += 1
                qv = q_ain.ap.rearrange("r (hl g d q) -> r hl g d q", hl=2, g=2, d=DH)
                dst_hi = qv[hd // 2, 0, hd % 2, :, 512 * qh:512 * (qh + 1)]
                dst_lo = qv[hd // 2, 1, hd % 2, :, 512 * qh:512 * (qh + 1)]
                rms_split_store(pb, gq_c, EPS_RMS, q_ain, [(dst_hi, dst_lo)], True,
                                sbk2, par=(pcall[0] % 2))


    ev_q = b.collective("AllToAll", ALU.bypass, q_ain.ap.opt(), q_aout.ap.opt(),
                        reads=(q_ain,), writes=(q_aout,))

    b.phase = "p0a"
    b.dma("g", svals, svals.ap[0:1, 4:7], ar_out, ar_out.ap[0:1, 0:3])
    b.emit("v", lambda E: E.tensor_scalar(svals.ap[0:1, 4:7], svals.ap[0:1, 4:7],
                                          1.0 / (FI * D), EPS_Q, ALU.mult, ALU.max),
           reads=(svals,), writes=(svals,))
    b.emit("v", lambda E: E.tensor_scalar(svals.ap[0:1, 0:3], svals.ap[0:1, 4:7],
                                          0.5, None, ALU.mult), reads=(svals,), writes=(svals,))
    b.emit("t", lambda E: E.matmul(banks[0].ap[:, 0:3], C["onesfr"].ap, svals.ap[0:1, 0:3],
                                   start=True, stop=True),
           reads=(C["onesfr"], svals), writes=(banks[0],))
    b.emit("v", lambda E: E.tensor_copy(shalfB.ap[:, 0:3], banks[0].ap[:, 0:3]),
           reads=(banks[0],), writes=(shalfB,))
    b.emit("v", lambda E: E.tensor_scalar(shalfB.ap[:, 4:7], shalfB.ap[:, 0:3], -1.0,
                                          None, ALU.mult), reads=(shalfB,), writes=(shalfB,))

    b.phase = "p0b"
    tern_g = tern_in.ap[0:2 * GPART].rearrange("(m c p o) -> m c p o", m=2, c=NDC, p=128)
    tern_dv = tern_in.ap[2 * GPART:].rearrange("(bq p o) -> bq p o", bq=WDS // 128, p=128)

    def quant_job(i, jb):
        mat, src_t, src_ap, cols = wtiles[i]
        st = wstage[i % 2]
        b.dma("g", st, st.ap[:, 0:cols], src_t, src_ap)
        b.emit("s", lambda E, st=st, cols=cols: E.activation(q1.ap[:, 0:cols],
                                                             st.ap[:, 0:cols], AF.Abs),
               reads=(st,), writes=(q1,))
        b.emit("s", lambda E, st=st, cols=cols: E.sign(q2.ap[:, 0:cols], st.ap[:, 0:cols]),
               reads=(st,), writes=(q2,))
        b.emit("v", lambda E, cols=cols, mat=mat: E.tensor_single_scalar(
            q1.ap[:, 0:cols], q1.ap[:, 0:cols], shalfB.ap[:, mat:mat + 1], ALU.is_gt),
            reads=(q1, shalfB), writes=(q1,))
        b.emit("v", lambda E, cols=cols: E.tensor_mul(q3.ap[:, 0:cols], q1.ap[:, 0:cols],
                                                      q2.ap[:, 0:cols]),
            reads=(q1, q2), writes=(q3,))
        j = i % (WGS // 128) if mat < 2 else i - 2 * (WGS // 128)
        for grp in range(cols // 1024):
            bk = jb
            bkb = bk.ap.bitcast(BF16)
            for p in range(8):
                cc = 8 * grp + p
                b.emit("t", lambda E, bkb=bkb, p=p, cc=cc, bk=bk: E.transpose(
                    bkb[:, 128 * p:128 * (p + 1)], q3.ap[:, 128 * cc:128 * (cc + 1)],
                    C["identb"].ap), reads=(q3, C["identb"]), writes=(bk,), inc=(p == 7))
            b.emit("v", lambda E, bkb=bkb, bk=bk: E.tensor_copy(q4.ap, bkb[:, 0:1024]),
                   reads=(bk,), writes=(q4,))
            if mat < 2:
                dst = tern_g[mat, 8 * grp:8 * grp + 8, :, 128 * j:128 * (j + 1)]
            else:
                dst = tern_dv[8 * grp:8 * grp + 8, :, 128 * j:128 * (j + 1)]
            b.dma("g", tern_in, dst.transpose([1, 0, 2]),
                  q4, q4.ap.rearrange("p (c f) -> p c f", c=8))

    b.phase = "p2"
    b.phase_barrier(p1_tts)
    p1.close()
    p2 = ExitStack()
    p2_tts = []

    def sb2(name, shape, dtype):
        t = TT(p2.enter_context(nc.sbuf_tensor(name, shape, dtype)).ap())
        p2_tts.append(t)
        return t

    kfull = sb2("kfull", [128, 2 * S], BF16)
    vfull = sb2("vfull", [128, 2 * S], BF16)
    qfull = sb2("qfull", [128, 2 * S], BF16)
    e32b = [sb2(f"e32b{i}", [128, 512], F32) for i in range(4)]
    ehib = [sb2(f"ehib{i}", [128, 512], BF16) for i in range(4)]
    elob = [sb2(f"elob{i}", [128, 512], BF16) for i in range(4)]
    o32s = sb2("o32s", [128, 512], F32)
    ohis = sb2("ohis", [128, 512], BF16)
    olos = sb2("olos", [128, 512], BF16)
    recrow = sb2("recrow", [1, 512], F32)
    sclc = sb2("sclc", [128, 1], F32)
    b.emit("v", lambda E: E.memset(sclc.ap, 1.0 / float(np.sqrt(128.0))), writes=(sclc,))
    wos32 = [sb2(f"wos32_{i}", [128, D], F32) for i in range(2)]
    woh = [sb2(f"woh{i}", [128, D], BF16) for i in range(2)]
    wol = [sb2(f"wol{i}", [128, D], BF16) for i in range(2)]
    wo_sv = wo_split_d.ap.rearrange("hl j p d -> hl j p d")

    def wo_job(j):
        w32 = wos32[j % 2]
        wh, wl = woh[j % 2], wol[j % 2]
        b.dma("s", w32, w32.ap, wo_in, wo_in.ap[128 * j:128 * (j + 1), :])
        b.emit("s", lambda E, w32=w32, wh=wh: E.activation(wh.ap, w32.ap, AF.Copy),
               reads=(w32,), writes=(wh,))
        b.emit("v", lambda E, w32=w32, wh=wh, wl=wl: E.tensor_sub(wl.ap, w32.ap, wh.ap),
               reads=(w32, wh), writes=(wl,))
        b.dma("s", wo_split_d, wo_sv[0, j], wh, wh.ap)
        b.dma("s", wo_split_d, wo_sv[1, j], wl, wl.ap)

    def chunk_rank(ch):
        return ch if ch < NC else 15 - ch

    def chunk_local(ch):
        return 0 if ch < NC else CH

    kv_ov = kv_out.ap.rearrange("r (s hl d q) -> r s hl d q", s=2, hl=2, d=DH)
    q_ov = q_aout.ap.rearrange("r (hl g d q) -> r hl g d q", hl=2, g=2, d=DH)
    o_iv0 = o_ain0.ap.rearrange("r (hl d q) -> r hl d q", hl=2, d=DH)
    o_iv1 = o_ain1.ap.rearrange("r (hl d q) -> r hl d q", hl=2, d=DH)

    for ch in range(16):
        r, lc = chunk_rank(ch), chunk_local(ch)
        for hl in range(2):
            b.dma("y", kfull, kfull.ap[:, hl * S + ch * CH:hl * S + (ch + 1) * CH],
                  kv_out, kv_ov[r, 0, hl, :, lc:lc + CH])
            vsrc = kv_ov[r, 1, hl].rearrange("d q -> (d q)").rearrange(
                "(q d) -> q d", d=DH)[lc:lc + CH, :]
            vsrc = vsrc.rearrange("(sg p) d -> p sg d", p=128)
            vdst = vfull.ap.rearrange("p (hl f) -> p hl f", hl=2)[
                :, hl, ch * CH:(ch + 1) * CH].rearrange("p (sg d) -> p sg d", d=128)
            b.dma("y", vfull, vdst, kv_out, vsrc)

    NQB = S // CH
    KTC = CH // 128
    SB0, SGB, OB0 = 0, 4, 6
    for hloc in range(2):
        for ch in range(16):
            r, lc = chunk_rank(ch), chunk_local(ch)
            for hl in range(2):
                b.dma("y", qfull, qfull.ap[:, hl * S + ch * CH:hl * S + (ch + 1) * CH],
                      q_aout, q_ov[r, hl, hloc, :, lc:lc + CH])
        for qb in range(NQB):
            G0 = qb * CH
            dstart = G0 // 128
            nkt = dstart + KTC
            ob = banks[OB0 + qb % 2]
            sgb = banks[SGB + qb % 2]

            def s_mms(kt):
                sbk = banks[SB0 + kt % 4]
                qcol0 = max(0, 128 * (kt - dstart))
                N = CH - qcol0
                combos = ((0, 0), (0, 1), (1, 0))
                for mi, (khl, qhl) in enumerate(combos):
                    kap = kfull.ap[:, khl * S + 128 * kt:khl * S + 128 * (kt + 1)]
                    qap = qfull.ap[:, qhl * S + G0 + qcol0:qhl * S + G0 + CH]
                    b.emit("t", lambda E, kap=kap, qap=qap, sbk=sbk, qcol0=qcol0,
                           st=(mi == 0), sp=(mi == 2), N=N: E.matmul(
                               sbk.ap[:, qcol0:qcol0 + N], kap, qap, start=st, stop=sp),
                           reads=(kfull, qfull), writes=(sbk,), inc=(mi == 2))

            def e_path(kt):
                sbk = banks[SB0 + kt % 4]
                qcol0 = max(0, 128 * (kt - dstart))
                N = CH - qcol0
                e32 = e32b[kt % 4]
                b.emit("s", lambda E, sbk=sbk, e32=e32, qcol0=qcol0, N=N: E.activation(
                    e32.ap[:, 0:N], sbk.ap[:, qcol0:qcol0 + N], AF.Exp,
                    bias=ebias.ap[:, 0:1], scale=sclc.ap[:, 0:1]),
                    reads=(sbk, ebias, sclc), writes=(e32,))
                if kt >= dstart:
                    b.emit("v", lambda E, e32=e32: E.tensor_mul(
                        e32.ap[:, 0:128], e32.ap[:, 0:128], C["tri"].ap),
                        reads=(e32, C["tri"]), writes=(e32,))
                ehi, elo = ehib[kt % 4], elob[kt % 4]
                b.emit("s", lambda E, e32=e32, ehi=ehi, N=N: E.activation(
                    ehi.ap[:, 0:N], e32.ap[:, 0:N], AF.Copy), reads=(e32,), writes=(ehi,))
                b.emit("v", lambda E, e32=e32, ehi=ehi, elo=elo, N=N: E.tensor_sub(
                    elo.ap[:, 0:N], e32.ap[:, 0:N], ehi.ap[:, 0:N]),
                    reads=(e32, ehi), writes=(elo,))

            def sig_pv(kt):
                qcol0 = max(0, 128 * (kt - dstart))
                N = CH - qcol0
                ehi, elo = ehib[kt % 4], elob[kt % 4]
                last = (kt == nkt - 1)
                for pi, et in enumerate((ehi, elo)):
                    b.emit("t", lambda E, et=et, sgb=sgb, qcol0=qcol0, N=N,
                           st=(kt == 0 and pi == 0), sp=(last and pi == 1): E.matmul(
                               sgb.ap[0:1, qcol0:qcol0 + N], C["onesb"].ap, et.ap[:, 0:N],
                               start=st, stop=sp),
                           reads=(et, C["onesb"]), writes=(sgb,), inc=(pi == 1))
                pvcombos = ((0, ehi), (0, elo), (1, ehi))
                for mi, (vhl, et) in enumerate(pvcombos):
                    vap = vfull.ap[:, vhl * S + 128 * kt:vhl * S + 128 * (kt + 1)]
                    b.emit("t", lambda E, vap=vap, et=et, ob=ob, qcol0=qcol0, N=N,
                           st=(kt == 0 and mi == 0), sp=(last and mi == 2): E.matmul(
                               ob.ap[:, qcol0:qcol0 + N], vap, et.ap[:, 0:N],
                               start=st, stop=sp),
                           reads=(vfull, et), writes=(ob,), inc=(mi == 2))

            DEPTH = 3
            for kt in range(nkt):
                s_mms(kt)
                e_path(kt)
                if kt >= DEPTH:
                    sig_pv(kt - DEPTH)
            for kt in range(max(0, nkt - DEPTH), nkt):
                sig_pv(kt)
            b.phase = "p2ep"
            b.emit("v", lambda E, sgb=sgb: E.reciprocal(recrow.ap[0:1, 0:CH],
                                                        sgb.ap[0:1, 0:CH]),
                   reads=(sgb,), writes=(recrow,))
            rb = banks[SB0 + (nkt + 1) % 4]
            b.emit("t", lambda E, rb=rb: E.matmul(rb.ap[:, 0:CH], C["onesfr"].ap,
                                                  recrow.ap[0:1, 0:CH], start=True, stop=True),
                   reads=(C["onesfr"], recrow), writes=(rb,))
            b.emit("v", lambda E, rb=rb: E.tensor_copy(e32b[0].ap[:, 0:CH], rb.ap[:, 0:CH]),
                   reads=(rb,), writes=(e32b[0],))
            b.emit("v", lambda E, ob=ob: E.tensor_mul(o32s.ap[:, 0:CH],
                                                      ob.ap[:, 0:CH], e32b[0].ap[:, 0:CH]),
                   reads=(ob, e32b[0]), writes=(o32s,))
            b.emit("v", lambda E: E.tensor_copy(ohis.ap[:, 0:CH], o32s.ap[:, 0:CH]),
                   reads=(o32s,), writes=(ohis,))
            b.emit("v", lambda E: E.tensor_sub(olos.ap[:, 0:CH], o32s.ap[:, 0:CH],
                                               ohis.ap[:, 0:CH]), reads=(o32s, ohis), writes=(olos,))
            r, lc = chunk_rank(qb), chunk_local(qb)
            o_t, o_ivh = ((o_ain0, o_iv0), (o_ain1, o_iv1))[hloc]
            b.dma("y", o_t, o_ivh[r, 0, :, lc:lc + CH], ohis, ohis.ap[:, 0:CH])
            b.dma("y", o_t, o_ivh[r, 1, :, lc:lc + CH], olos, olos.ap[:, 0:CH])
            b.phase = "p2"
            if hloc == 0:
                b.phase = "p0b"
                jb = banks[SGB + (qb + 1) % 2]
                quant_job(2 * qb, jb)
                quant_job(2 * qb + 1, jb)
                wo_job(qb)
                if qb == 3:
                    ev_ag_g = b.collective(
                        "AllGather", ALU.bypass, tern_in.ap[0:GPART].opt(),
                        tern_outg.ap.opt(), reads=(tern_in,), writes=(tern_outg,))
                if qb == 7:
                    ev_ag_u = b.collective(
                        "AllGather", ALU.bypass, tern_in.ap[GPART:2 * GPART].opt(),
                        tern_outu.ap.opt(), reads=(tern_in,), writes=(tern_outu,))
                b.phase = "p2"

        if hloc == 0:
            ev_o0 = b.collective("AllToAll", ALU.bypass, o_ain0.ap.opt(),
                                 o_aout0.ap.opt(), reads=(o_ain0,), writes=(o_aout0,))
            ev_ag_d = b.collective(
                "AllGather", ALU.bypass, tern_in.ap[2 * GPART:].opt(),
                tern_outd.ap.opt(), reads=(tern_in,), writes=(tern_outd,))

    ev_o1 = b.collective("AllToAll", ALU.bypass, o_ain1.ap.opt(), o_aout1.ap.opt(),
                         reads=(o_ain1,), writes=(o_aout1,))

    b.phase = "p3"
    b.phase_barrier(p2_tts)
    p2.close()
    b.phase_barrier(p0_tts)
    p0.close()
    p3 = ExitStack()
    p3_tts = []

    def sb3(name, shape, dtype):
        t = TT(p3.enter_context(nc.sbuf_tensor(name, shape, dtype)).ap())
        p3_tts.append(t)
        return t

    obuf_h = [sb3(f"obuf_h{i}", [128, 2 * NC * TQ], BF16) for i in range(2)]
    wo_g = [sb3(f"wo_g{i}", [128, 2 * NH * 512], BF16) for i in range(2)]
    xg = [sb3(f"xg{i}", [128, NXT * 512], F32) for i in range(2)]
    x1g = [sb3(f"x1g{i}", [128, NXT * 512], F32) for i in range(2)]

    o_ov0 = o_aout0.ap.rearrange("r (hl d q) -> r hl d q", hl=2, d=DH)
    o_ov1 = o_aout1.ap.rearrange("r (hl d q) -> r hl d q", hl=2, d=DH)
    obq = [obuf_h[i].ap.rearrange("p (b q) -> p b q", b=2 * NC) for i in range(2)]

    def load_obuf(half, ov, ot):
        for srcr in range(NC):
            for hl in range(2):
                qn = ("y", "s", "g")[(2 * srcr + hl) % 3]
                b.dma(qn, obuf_h[half], obq[half][:, hl * NC + srcr, :], ot, ov[srcr, hl])

    load_obuf(0, o_ov0, o_aout0)
    wo_gv = [wo_g[i].ap.rearrange("p (b c) -> p b c", b=2 * NH) for i in range(2)]

    def load_wog(g, slot):
        for hl in range(2):
            for hd in range(NDC):
                b.dma("y" if hd % 2 == 0 else "s", wo_g[slot],
                      wo_gv[slot][:, hl * NH + hd, :], wo_split_d,
                      wo_sv[hl, hd][:, 512 * g:512 * (g + 1)])
        for ti in range(NXT):
            b.dma("g", xg[slot], xg[slot].ap[:, 512 * ti:512 * (ti + 1)],
                  x_tm_d, x_tm_d.ap[ti][:, 512 * g:512 * (g + 1)])

    load_wog(0, 0)
    load_wog(1, 1)
    load_obuf(1, o_ov1, o_aout1)
    combos = ((0, 0), (0, 1), (1, 0))
    for g in range(4):
        slot = g % 2
        if g + 1 < 4 and g >= 1:
            load_wog(g + 1, 1 - slot)
        for ti in range(NXT):
            pb = banks[ti % 2]
            k = 0
            for half in range(2):
                for srcr in range(NC):
                    h = 2 * srcr + half
                    for (ohl, whl) in combos:
                        oap = obq[half][:, ohl * NC + srcr, 128 * ti:128 * (ti + 1)]
                        wap = wo_gv[slot][:, whl * NH + h, :]
                        b.emit("t", lambda E, oap=oap, wap=wap, pb=pb, st=(k == 0),
                               sp=(k == 3 * NH - 1): E.matmul(
                                   pb.ap[:, 0:512], oap, wap, start=st, stop=sp),
                               reads=(obuf_h[half], wo_g[slot]), writes=(pb,),
                               inc=(k == 3 * NH - 1))
                        k += 1
            b.emit("v", lambda E, pb=pb, ti=ti, slot=slot: E.tensor_add(
                x1g[slot].ap[:, 512 * ti:512 * (ti + 1)], pb.ap[:, 0:512],
                xg[slot].ap[:, 512 * ti:512 * (ti + 1)]),
                reads=(pb, xg[slot]), writes=(x1g[slot],))
            b.dma("y", x1_d, x1_d.ap[ti][:, 512 * g:512 * (g + 1)],
                  x1g[slot], x1g[slot].ap[:, 512 * ti:512 * (ti + 1)])

    b.phase = "p4"
    b.phase_barrier(p3_tts)
    p3.close()
    p4 = ExitStack()
    p4_tts = []

    def sb4(name, shape, dtype):
        t = TT(p4.enter_context(nc.sbuf_tensor(name, shape, dtype)).ap())
        p4_tts.append(t)
        return t

    T2 = 256
    NT2 = TQ // T2
    C1 = 1.0 / float(np.sqrt(float(D)))
    C2 = 1.0 / float(np.sqrt(float(FI)))

    x1t4 = [[sb4(f"x1t{p}_{t}", [128, D], F32) for t in range(2)] for p in range(2)]
    st8_b = [sb4(f"st8_{p}", [128, 10], F32) for p in range(2)]
    z1 = sb4("z1", [128, NDC * T2], F32)
    zt = sb4("zt", [128, NDC * T2], F32)
    xq1_b = [sb4(f"xq1_{p}", [128, NDC * T2], BF16) for p in range(2)]
    terng = [sb4(f"terng{i}", [128, NDC * 256], FP8) for i in range(2)]
    ternu = [sb4(f"ternu{i}", [128, NDC * 256], FP8) for i in range(2)]
    z2 = sb4("z2", [128, NIC * T2], F32)
    xnf = sb4("xnf", [128, 2 * NDC * T2], BF16)
    hst = [sb4(f"hst{i}", [128, 512], F32) for i in range(2)]
    hhi = [sb4(f"hhi{i}", [128, 512], BF16) for i in range(2)]
    hlo = [sb4(f"hlo{i}", [128, 512], BF16) for i in range(2)]
    u32 = [sb4(f"u32{i}", [128, 512], F32) for i in range(2)]
    qss1 = sb4("qss1", [128, 256], F32)
    qss2 = sb4("qss2", [128, 256], F32)
    dqs_b = [sb4(f"dqs_{p}", [128, 512], F32) for p in range(2)]
    qrow = sb4("qrow", [1, 1024], F32)
    wdb = [sb4(f"wdb{i}", [128, D], FP8) for i in range(2)]
    rtmp1 = sb4("rtmp1", [128, 8], F32)
    rtmpB = sb4("rtmpB", [128, 4], F32)
    st8b2 = sb4("st8b2", [128, 4], F32)

    class _View:
        __slots__ = ("ap",)

        def __init__(self, ap):
            self.ap = ap
    z2bf = z2.ap.bitcast(BF16)
    xnfm_hi = _View(xnf.ap[:, 0:NDC * T2])
    xnfm_lo = _View(xnf.ap[:, NDC * T2:2 * NDC * T2])
    ztbf = zt.ap.bitcast(BF16)
    zq1 = _View(zt.ap[:, 3584:4096])
    xn2lo = _View(ztbf[:, 4096:6144])
    xn2hi = _View(ztbf[:, 6144:8192])

    def butterfly_level(z, nchunk, hsz):
        zv = z.ap.rearrange("p (bq i q) -> p bq i q", bq=nchunk // (2 * hsz), i=2 * hsz)
        a = zv[:, :, 0:hsz, :]
        bb = zv[:, :, hsz:2 * hsz, :]
        b.emit("v", lambda E, a=a, bb=bb: E.tensor_add(a, a, bb), reads=(z,), writes=(z,))
        b.emit("v", lambda E, a=a, bb=bb: E.scalar_tensor_tensor(
            bb, bb, -2.0, a, ALU.mult, ALU.add), reads=(z,), writes=(z,))

    def h1_stage_fns(j):
        p = j % 2
        st8 = st8_b[p]
        xq1 = xq1_b[p]
        dqs = dqs_b[p]
        xn232 = _View(zt.ap[:, 0:D])
        fns = []

        def ln2_stats(stt):
            ti = 2 * j + stt
            xt = x1t4[p][stt]
            b.phase = "p4ln"
            b.dma("y", xt, xt.ap, x1_d, x1_d.ap[ti])
            b.emit("v", lambda E, xt=xt: E.reduce_sum(st8.ap[:, 0:1], xt.ap, AX),
                   reads=(xt,), writes=(st8,))
            b.emit("s", lambda E, xt=xt: E.activation(xn232.ap, xt.ap, AF.Square),
                   reads=(xt,), writes=(zt,))
            b.emit("v", lambda E: E.reduce_sum(st8.ap[:, 1:2], xn232.ap, AX),
                   reads=(zt,), writes=(st8,))
            b.emit("v", lambda E: E.tensor_scalar(st8.ap[:, 2:4], st8.ap[:, 0:2], 1.0 / D,
                                                  None, ALU.mult), reads=(st8,), writes=(st8,))
            b.emit("v", lambda E: E.tensor_scalar(st8.ap[:, 4:5], st8.ap[:, 2:3],
                                                  st8.ap[:, 2:3], None, ALU.mult),
                   reads=(st8,), writes=(st8,))
            b.emit("v", lambda E: E.tensor_sub(st8.ap[:, 5:6], st8.ap[:, 3:4], st8.ap[:, 4:5]),
                   reads=(st8,), writes=(st8,))
            b.emit("v", lambda E: E.tensor_scalar(st8.ap[:, 5:6], st8.ap[:, 5:6], EPS_LN,
                                                  None, ALU.add), reads=(st8,), writes=(st8,))
            b.emit("s", lambda E: E.activation(st8.ap[:, 5:6], st8.ap[:, 5:6], AF.Sqrt),
                   reads=(st8,), writes=(st8,))
            b.emit("v", lambda E: E.reciprocal(st8.ap[:, 6:7], st8.ap[:, 5:6]),
                   reads=(st8,), writes=(st8,))
            b.emit("v", lambda E: E.tensor_scalar(st8.ap[:, 7:8], st8.ap[:, 2:3],
                                                  st8.ap[:, 6:7], -1.0, ALU.mult, ALU.mult),
                   reads=(st8,), writes=(st8,))
            b.emit("v", lambda E, xt=xt: E.tensor_scalar(
                xn2hi.ap, xt.ap, st8.ap[:, 6:7], st8.ap[:, 7:8], ALU.mult, ALU.add),
                reads=(xt, st8), writes=(zt,))
            b.emit("v", lambda E, xt=xt: E.tensor_scalar(
                xn232.ap, xt.ap, st8.ap[:, 6:7], st8.ap[:, 7:8], ALU.mult, ALU.add),
                reads=(xt, st8), writes=(zt,))
            b.emit("v", lambda E: E.tensor_sub(xn2lo.ap, xn232.ap, xn2hi.ap),
                   reads=(zt,), writes=(zt,))

        def ln2_tr(stt, half, grp):
            srcT, dstT = ((xn2hi, xnfm_hi), (xn2lo, xnfm_lo))[half]
            b.phase = "p4ln"
            bk = banks[grp % 2]
            bkb = bk.ap.bitcast(BF16)
            for pp in range(8):
                cc = 8 * grp + pp
                b.emit("t", lambda E, bkb=bkb, pp=pp, srcT=srcT, cc=cc: E.transpose(
                    bkb[:, 128 * pp:128 * (pp + 1)], srcT.ap[:, 128 * cc:128 * (cc + 1)],
                    C["identb"].ap), reads=(zt, C["identb"]), writes=(bk,))

            def evac2(E, bkb=bkb, dstT=dstT, grp=grp, stt=stt):
                dst = dstT.ap.rearrange("p (c q) -> p c q", c=NDC)[
                    :, 8 * grp:8 * grp + 8, 128 * stt:128 * (stt + 1)]
                return E.tensor_copy(dst, bkb.rearrange("p (c q) -> p c q", c=8))
            b.emit("v", evac2, reads=(bk,), writes=(xnf,))

        def fw1(c4):
            b.phase = "p4fw1"
            for cc in range(4 * c4, 4 * c4 + 4):
                bk = banks[cc % 2]
                for pi, srcT in enumerate((xnfm_hi, xnfm_lo)):
                    b.emit("t", lambda E, srcT=srcT, cc=cc, bk=bk, st=(pi == 0),
                           sp=(pi == 1): E.matmul(
                        bk.ap[:, 0:T2], C["h128"].ap, srcT.ap[:, T2 * cc:T2 * (cc + 1)],
                        start=st, stop=sp), reads=(xnf, C["h128"]), writes=(bk,), inc=(pi == 1))
                b.emit("s", lambda E, cc=cc, bk=bk: E.activation(
                    z1.ap[:, T2 * cc:T2 * (cc + 1)], bk.ap[:, 0:T2], AF.Copy),
                    reads=(bk,), writes=(z1,))

        def bf1(lev):
            b.phase = "p4fw1"
            butterfly_level(z1, NDC, NDC // (2 ** (lev + 1)))

        def q1max():
            b.phase = "p4q1"
            b.emit("s", lambda E: E.activation(zt.ap, z1.ap, AF.Abs), reads=(z1,), writes=(zt,))
            cv = zt.ap.rearrange("p (c q) -> p c q", c=NDC)
            b.emit("v", lambda E, cv=cv: E.tensor_tensor(
                cv[:, 0:8, :], cv[:, 0:8, :], cv[:, 8:16, :], ALU.max),
                reads=(zt,), writes=(zt,))
            for nn in (4, 2, 1):
                b.emit("v", lambda E, nn=nn, cv=cv: E.tensor_tensor(
                    cv[:, 0:nn, :], cv[:, 0:nn, :], cv[:, nn:2 * nn, :], ALU.max),
                    reads=(zt,), writes=(zt,))
            for pp in range(2):
                b.emit("t", lambda E, pp=pp: E.transpose(
                    banks[0].ap[:, 128 * pp:128 * (pp + 1)], zt.ap[:, 128 * pp:128 * (pp + 1)],
                    C["identf"].ap), reads=(zt, C["identf"]), writes=(banks[0],))
            b.emit("v", lambda E: E.tensor_copy(zq1.ap[:, 256:512], banks[0].ap[:, 0:256]),
                   reads=(banks[0],), writes=(zt,))
            b.emit("v", lambda E: E.reduce_max(st8.ap[:, 8:9], zq1.ap[:, 256:384], AX),
                   reads=(zt,), writes=(st8,))
            b.emit("v", lambda E: E.reduce_max(st8.ap[:, 9:10], zq1.ap[:, 384:512], AX),
                   reads=(zt,), writes=(st8,))

        def q1scales():
            b.phase = "p4q1"
            b.emit("v", lambda E: E.tensor_scalar(st8.ap[:, 8:10], st8.ap[:, 8:10], C1, EPS_Q,
                                                  ALU.mult, ALU.max), reads=(st8,), writes=(st8,))
            b.emit("v", lambda E: E.reciprocal(rtmp1.ap[:, 0:2], st8.ap[:, 8:10]),
                   reads=(st8,), writes=(rtmp1,))
            b.emit("v", lambda E: E.tensor_scalar(rtmp1.ap[:, 0:2], rtmp1.ap[:, 0:2], 127.0 * C1,
                                                  None, ALU.mult), reads=(rtmp1,), writes=(rtmp1,))
            b.emit("v", lambda E: E.tensor_scalar(rtmp1.ap[:, 2:3], st8.ap[:, 8:9],
                                                  shalfB.ap[:, 0:1], 2.0 / 127.0, ALU.mult, ALU.mult),
                   reads=(st8, shalfB), writes=(rtmp1,))
            b.emit("v", lambda E: E.tensor_scalar(rtmp1.ap[:, 3:4], st8.ap[:, 9:10],
                                                  shalfB.ap[:, 0:1], 2.0 / 127.0, ALU.mult, ALU.mult),
                   reads=(st8, shalfB), writes=(rtmp1,))
            b.emit("v", lambda E: E.tensor_scalar(rtmp1.ap[:, 4:5], st8.ap[:, 8:9],
                                                  shalfB.ap[:, 1:2], 2.0 / 127.0, ALU.mult, ALU.mult),
                   reads=(st8, shalfB), writes=(rtmp1,))
            b.emit("v", lambda E: E.tensor_scalar(rtmp1.ap[:, 5:6], st8.ap[:, 9:10],
                                                  shalfB.ap[:, 1:2], 2.0 / 127.0, ALU.mult, ALU.mult),
                   reads=(st8, shalfB), writes=(rtmp1,))

        def q1rows():
            b.phase = "p4q1"
            for pp in range(4):
                b.emit("t", lambda E, pp=pp: E.transpose(
                    banks[0].ap[0:1, 128 * pp:128 * (pp + 1)], rtmp1.ap[:, pp:pp + 1],
                    C["identf"].ap), reads=(rtmp1, C["identf"]), writes=(banks[0],))
            b.emit("v", lambda E: E.tensor_copy(qrow.ap[0:1, 0:512], banks[0].ap[0:1, 0:512]),
                   reads=(banks[0],), writes=(qrow,))
            for pp in range(2):
                b.emit("t", lambda E, pp=pp: E.transpose(
                    banks[0].ap[0:1, 128 * pp:128 * (pp + 1)], rtmp1.ap[:, 4 + pp:5 + pp],
                    C["identf"].ap), reads=(rtmp1, C["identf"]), writes=(banks[0],))
            b.emit("v", lambda E: E.tensor_copy(qrow.ap[0:1, 512:768], banks[0].ap[0:1, 0:256]),
                   reads=(banks[0],), writes=(qrow,))
            b.emit("t", lambda E: E.matmul(banks[1].ap[:, 0:256], C["onesfr"].ap,
                                           qrow.ap[0:1, 0:256], start=True, stop=True),
                   reads=(C["onesfr"], qrow), writes=(banks[1],))
            b.emit("v", lambda E: E.tensor_copy(qss1.ap[:, 0:256], banks[1].ap[:, 0:256]),
                   reads=(banks[1],), writes=(qss1,))
            b.emit("t", lambda E: E.matmul(banks[1].ap[:, 0:512], C["onesfr"].ap,
                                           qrow.ap[0:1, 256:768], start=True, stop=True),
                   reads=(C["onesfr"], qrow), writes=(banks[1],))
            b.emit("v", lambda E, dqs=dqs: E.tensor_copy(dqs.ap[:, 0:512], banks[1].ap[:, 0:512]),
                   reads=(banks[1],), writes=(dqs,))

        def q1big(hf):
            b.phase = "p4q1"
            HC = NDC // 2
            sl = slice(hf * HC * T2, (hf + 1) * HC * T2)
            b.emit("v", lambda E, sl=sl: E.tensor_tensor(
                zt.ap[:, sl].rearrange("p (c q) -> p c q", c=HC),
                z1.ap[:, sl].rearrange("p (c q) -> p c q", c=HC),
                qss1.ap[:, 0:256].unsqueeze(1).broadcast_to([128, HC, 256]), ALU.mult),
                reads=(z1, qss1), writes=(zt,))
            b.emit("v", lambda E, sl=sl: E.tensor_scalar(zt.ap[:, sl], zt.ap[:, sl], C32, C32,
                                                         ALU.add, ALU.subtract),
                   reads=(zt,), writes=(zt,))
            b.emit("v", lambda E, sl=sl: E.tensor_scalar(z1.ap[:, sl], zt.ap[:, sl], C52, C52,
                                                         ALU.add, ALU.subtract),
                   reads=(zt,), writes=(z1,))
            b.emit("v", lambda E, sl=sl, xq1=xq1: E.scalar_tensor_tensor(
                xq1.ap[:, sl], zt.ap[:, sl], blend.ap[:, 1:2], z1.ap[:, sl],
                ALU.mult, ALU.add), reads=(zt, z1, blend), writes=(xq1,))

        for stt in range(2):
            fns.append(lambda stt=stt: ln2_stats(stt))
            for half in range(2):
                for grp in range(2):
                    fns.append(lambda stt=stt, half=half, grp=grp: ln2_tr(stt, half, grp))
        for c4 in range(4):
            fns.append(lambda c4=c4: fw1(c4))
        for lev in range(4):
            fns.append(lambda lev=lev: bf1(lev))
        fns.append(q1max)
        fns.append(q1scales)
        fns.append(q1rows)
        fns.append(lambda: q1big(0))
        fns.append(lambda: q1big(1))
        return fns

    def h2_emit(j, drain):
        p = j % 2
        xq1 = xq1_b[p]
        dqs = dqs_b[p]
        zqa = _View(xq1.ap.bitcast(F32)[:, 0:1024])
        zqb = _View(xq1.ap.bitcast(F32)[:, 1024:2048])
        b.phase = "p4gu"

        def tern_load(jj, hs):
            tg, tu = terng[hs % 2], ternu[hs % 2]
            r = (2 * hs) // (WGS // 128)
            col0 = (2 * hs) % (WGS // 128) * 128
            for m, dstT in ((0, tg), (1, tu)):
                tsrc = (tern_outg, tern_outu)[m]
                srcflat = tsrc.ap[r * GPART:(r + 1) * GPART]
                srcv = srcflat.rearrange("(c p o) -> c p o", c=NDC, p=128)[
                    :, :, col0:col0 + 256].transpose([1, 0, 2])
                b.dma(("s", "g")[m], dstT, dstT.ap.rearrange("p (c f) -> p c f", c=NDC),
                      tsrc, srcv)

        def stage_a(hs):
            par = hs % 2
            hhip, hlop = hhi[par], hlo[par]
            hbk = banks[4] if par == 0 else banks[7]
            for pi, srcT in enumerate((hhip, hlop)):
                b.emit("t", lambda E, srcT=srcT, st=(pi == 0), sp=(pi == 1), hbk=hbk:
                       E.matmul(hbk.ap[:, 0:512], C["h128"].ap, srcT.ap, start=st,
                                stop=sp), reads=(srcT, C["h128"]), writes=(hbk,),
                       inc=(pi == 1))
            b.emit("s", lambda E, hs=hs, hbk=hbk: E.activation(
                z2.ap[:, 512 * hs:512 * (hs + 1)], hbk.ap[:, 0:512], AF.Copy),
                reads=(hbk,), writes=(z2,))

        if j == 0:
            tern_load(j, 0)
            tern_load(j, 1)
        for hs in range(NIC // 2):
            tg, tu = terng[hs % 2], ternu[hs % 2]
            par = hs % 2
            gb, ub = (banks[2], banks[3]) if par == 0 else (banks[5], banks[6])
            hstp, u32p, hhip, hlop = hst[par], u32[par], hhi[par], hlo[par]
            for m, (wt, pb) in enumerate(((tg, gb), (tu, ub))):
                for dtl in range(2):
                    for cc in range(NDC):
                        b.emit("t", lambda E, wt=wt, pb=pb, cc=cc, dtl=dtl,
                               st=(cc == 0),
                               sp=(cc == NDC - 1): E.matmul(
                                   pb.ap[:, 256 * dtl:256 * (dtl + 1)],
                                   wt.ap[:, 256 * cc + 128 * dtl:256 * cc + 128 * (dtl + 1)],
                                   xq1.ap[:, T2 * cc:T2 * (cc + 1)], start=st, stop=sp),
                               reads=(wt, xq1), writes=(pb,),
                               inc=(dtl == 1 and cc == NDC - 1))
            if hs + 2 < NIC // 2:
                tern_load(j, hs + 2)
            if hs > 0:
                stage_a(hs - 1)
            dqv = dqs.ap[:, 0:256].unsqueeze(1).broadcast_to([128, 2, 256])
            dqv_u = dqs.ap[:, 256:512].unsqueeze(1).broadcast_to([128, 2, 256])
            b.emit("v", lambda E, gb=gb, hstp=hstp, dqv=dqv, dqs=dqs: E.tensor_tensor(
                hstp.ap.rearrange("p (c q) -> p c q", c=2),
                gb.ap[:, 0:512].rearrange("p (c q) -> p c q", c=2), dqv, ALU.mult),
                reads=(gb, dqs), writes=(hstp,))
            b.emit("s", lambda E, hstp=hstp: E.activation(hstp.ap, hstp.ap, AF.Silu),
                   reads=(hstp,), writes=(hstp,))
            b.emit("v", lambda E, ub=ub, u32p=u32p, dqv_u=dqv_u, dqs=dqs: E.tensor_tensor(
                u32p.ap.rearrange("p (c q) -> p c q", c=2),
                ub.ap[:, 0:512].rearrange("p (c q) -> p c q", c=2), dqv_u, ALU.mult),
                reads=(ub, dqs), writes=(u32p,))
            b.emit("v", lambda E, hstp=hstp, u32p=u32p: E.tensor_mul(
                hstp.ap, hstp.ap, u32p.ap), reads=(hstp, u32p), writes=(hstp,))
            b.emit("v", lambda E, hstp=hstp, hhip=hhip: E.tensor_copy(hhip.ap, hstp.ap),
                   reads=(hstp,), writes=(hhip,))
            b.emit("v", lambda E, hstp=hstp, hhip=hhip, hlop=hlop: E.tensor_sub(
                hlop.ap, hstp.ap, hhip.ap), reads=(hstp, hhip), writes=(hlop,))
            drain(1)
        stage_a(NIC // 2 - 1)
        if j + 1 < NT2:
            tern_load(j + 1, 0)
            tern_load(j + 1, 1)
        drain(-1)
        b.phase = "p4fw2"
        for lev in range(6):
            butterfly_level(z2, NIC, NIC // (2 ** (lev + 1)))
        b.phase = "p4q2"
        for seg in range(16):
            src = z2.ap[:, 1024 * seg:1024 * (seg + 1)]
            if seg == 0:
                b.emit("s", lambda E, src=src: E.activation(zqa.ap, src, AF.Abs),
                       reads=(z2,), writes=(xq1,))
            else:
                b.emit("s", lambda E, src=src: E.activation(zqb.ap, src, AF.Abs),
                       reads=(z2,), writes=(xq1,))
                b.emit("v", lambda E: E.tensor_tensor(zqa.ap, zqa.ap, zqb.ap, ALU.max),
                       reads=(xq1,), writes=(xq1,))
        b.emit("v", lambda E: E.tensor_tensor(zqa.ap[:, 0:512], zqa.ap[:, 0:512],
                                              zqa.ap[:, 512:1024], ALU.max),
               reads=(xq1,), writes=(xq1,))
        b.emit("v", lambda E: E.tensor_tensor(zqa.ap[:, 0:256], zqa.ap[:, 0:256],
                                              zqa.ap[:, 256:512], ALU.max),
               reads=(xq1,), writes=(xq1,))
        for pp in range(2):
            b.emit("t", lambda E, pp=pp: E.transpose(
                banks[2].ap[:, 128 * pp:128 * (pp + 1)], zqa.ap[:, 128 * pp:128 * (pp + 1)],
                C["identf"].ap), reads=(xq1, C["identf"]), writes=(banks[2],))
        b.emit("v", lambda E: E.tensor_copy(zqb.ap[:, 0:256], banks[2].ap[:, 0:256]),
               reads=(banks[2],), writes=(xq1,))
        b.emit("v", lambda E: E.reduce_max(st8b2.ap[:, 0:1], zqb.ap[:, 0:128], AX),
               reads=(xq1,), writes=(st8b2,))
        b.emit("v", lambda E: E.reduce_max(st8b2.ap[:, 1:2], zqb.ap[:, 128:256], AX),
               reads=(xq1,), writes=(st8b2,))
        b.emit("v", lambda E: E.tensor_scalar(st8b2.ap[:, 0:2], st8b2.ap[:, 0:2], C2, EPS_Q,
                                              ALU.mult, ALU.max), reads=(st8b2,), writes=(st8b2,))
        b.emit("v", lambda E: E.reciprocal(rtmpB.ap[:, 0:2], st8b2.ap[:, 0:2]),
               reads=(st8b2,), writes=(rtmpB,))
        b.emit("v", lambda E: E.tensor_scalar(rtmpB.ap[:, 0:2], rtmpB.ap[:, 0:2], 127.0 * C2,
                                              None, ALU.mult), reads=(rtmpB,), writes=(rtmpB,))
        b.emit("v", lambda E: E.tensor_scalar(st8b2.ap[:, 0:2], st8b2.ap[:, 0:2],
                                              shalfB.ap[:, 2:3], 2.0 / 127.0, ALU.mult, ALU.mult),
               reads=(st8b2, shalfB), writes=(st8b2,))
        for pp in range(2):
            b.emit("t", lambda E, pp=pp: E.transpose(
                banks[2].ap[0:1, 128 * pp:128 * (pp + 1)], rtmpB.ap[:, pp:pp + 1],
                C["identf"].ap), reads=(rtmpB, C["identf"]), writes=(banks[2],))
        b.emit("v", lambda E: E.tensor_copy(qrow.ap[0:1, 768:1024], banks[2].ap[0:1, 0:256]),
               reads=(banks[2],), writes=(qrow,))
        b.emit("t", lambda E: E.matmul(banks[3].ap[:, 0:256], C["onesfr"].ap,
                                       qrow.ap[0:1, 768:1024], start=True, stop=True),
               reads=(C["onesfr"], qrow), writes=(banks[3],))
        b.emit("v", lambda E: E.tensor_copy(qss2.ap[:, 0:256], banks[3].ap[:, 0:256]),
               reads=(banks[3],), writes=(qss2,))
        b.phase = "p4dn"
        for seg in range(16):
            sl = slice(1024 * seg, 1024 * (seg + 1))
            b.emit("v", lambda E, sl=sl: E.tensor_tensor(
                zqa.ap.rearrange("p (c q) -> p c q", c=4),
                z2.ap[:, sl].rearrange("p (c q) -> p c q", c=4),
                qss2.ap[:, 0:256].unsqueeze(1).broadcast_to([128, 4, 256]), ALU.mult),
                reads=(z2, qss2), writes=(xq1,))
            b.emit("v", lambda E: E.tensor_scalar(zqa.ap, zqa.ap, C32, C32,
                                                  ALU.add, ALU.subtract),
                   reads=(xq1,), writes=(xq1,))
            b.emit("v", lambda E: E.tensor_scalar(zqb.ap, zqa.ap, C52, C52,
                                                  ALU.add, ALU.subtract),
                   reads=(xq1,), writes=(xq1,))
            b.emit("v", lambda E, seg=seg: E.scalar_tensor_tensor(
                z2bf[:, 1024 * seg:1024 * (seg + 1)], zqa.ap, blend.ap[:, 1:2],
                zqb.ap, ALU.mult, ALU.add),
                reads=(xq1, blend), writes=(z2,))
        for cc in range(NIC):
            wb_ = wdb[cc % 2]
            rr = cc // (WDS // 128)
            blk = cc % (WDS // 128)
            srcw = tern_outd.ap[rr * DPART + blk * 128 * D:
                                rr * DPART + (blk + 1) * 128 * D]
            b.dma(("s", "g", "y")[cc % 3], wb_, wb_.ap, tern_outd,
                  srcw.rearrange("(p o) -> p o", p=128))
            kk = 0
            for th in range(2):
                for g in range(4):
                    pb = banks[th * 4 + g]
                    xap = z2bf[:, T2 * cc + 128 * th:T2 * cc + 128 * (th + 1)]
                    b.emit("t", lambda E, pb=pb, g=g, wb_=wb_, xap=xap,
                           st=(cc == 0), sp=(cc == NIC - 1): E.matmul(
                               pb.ap[:, 0:512], xap,
                               wb_.ap[:, 512 * g:512 * (g + 1)], start=st, stop=sp),
                           reads=(z2, wb_), writes=(pb,), inc=(kk == 7))
                    kk += 1
        for th in range(2):
            for g in range(4):
                pb = banks[th * 4 + g]
                xt = x1t4[p][th]
                b.emit("v", lambda E, pb=pb, th=th, g=g, xt=xt: E.scalar_tensor_tensor(
                    xt.ap[:, 512 * g:512 * (g + 1)], pb.ap[:, 0:512],
                    st8b2.ap[:, th:th + 1], xt.ap[:, 512 * g:512 * (g + 1)],
                    ALU.mult, ALU.add), reads=(pb, st8b2, xt), writes=(xt,))
            ti = 2 * j + th
            b.dma("y", out_d, out_d.ap[128 * ti:128 * (ti + 1), :],
                  x1t4[p][th], x1t4[p][th].ap)

    pend = h1_stage_fns(0)
    while pend:
        pend.pop(0)()
    for j in range(NT2):
        pend = h1_stage_fns(j + 1) if j + 1 < NT2 else []

        def drain(k, pend=pend):
            if k < 0:
                while pend:
                    pend.pop(0)()
            else:
                for _ in range(k):
                    if pend:
                        pend.pop(0)()
        h2_emit(j, drain)


    tap_outs = {}
    if "kv_in" in taps:
        tap_outs["t_kv"] = dram("t_kv", [NC, KVP], BF16, "ExternalOutput")
        b.dma("y", tap_outs["t_kv"], tap_outs["t_kv"].ap[:, :], kv_in, kv_in.ap[:, :])
    if "x_tm" in taps:
        tap_outs["t_x"] = dram("t_x", [NXT, 128, D], F32, "ExternalOutput")
        b.dma("y", tap_outs["t_x"], tap_outs["t_x"].ap[:], x_tm_d, x_tm_d.ap[:])
    if "svals" in taps:
        tap_outs["t_sv"] = dram("t_sv", [1, 3], F32, "ExternalOutput")
        b.emit("y", lambda E: E.dma_start(out=tap_outs["t_sv"].ap, in_=svals.ap[0:1, 4:7]),
               reads=(svals,), writes=(tap_outs["t_sv"],), dma=True)
    if "q" in taps:
        tap_outs["t_q"] = dram("t_q", [NC, QOP], BF16, "ExternalOutput")
        b.dma("y", tap_outs["t_q"], tap_outs["t_q"].ap[:, :], q_aout, q_aout.ap[:, :])
    if "oa" in taps:
        tap_outs["t_oa"] = dram("t_oa", [2, NC, OHP], BF16, "ExternalOutput")
        b.dma("y", tap_outs["t_oa"], tap_outs["t_oa"].ap[0], o_aout0, o_aout0.ap[:, :])
        b.dma("y", tap_outs["t_oa"], tap_outs["t_oa"].ap[1], o_aout1, o_aout1.ap[:, :])
    if "x1" in taps:
        tap_outs["t_x1"] = dram("t_x1", [NXT, 128, D], F32, "ExternalOutput")
        b.dma("y", tap_outs["t_x1"], tap_outs["t_x1"].ap[:], x1_d, x1_d.ap[:])
    if "tern" in taps:
        tap_outs["t_tern"] = dram("t_tern", [TERN_STRIDE], FP8, "ExternalOutput")
        b.dma("y", tap_outs["t_tern"], tap_outs["t_tern"].ap[:], tern_in, tern_in.ap[:])

    for q in ("DY", "DS", "DG"):
        for i in range(NQSEM):
            b.wait("y", (f"{q}{i}", b.cnt[f"{q}{i}"]))
    for e in ("G", "V", "S", "T"):
        b.wait("y", (e, b.cnt[e]))

    b.finalize()
    return nc


def kernel(L0_attn_x, L0_attn_Wq, L0_attn_Wk, L0_attn_Wv, L0_attn_Wo,
           L0_attn_gq, L0_attn_gk, L0_ffn_w_gate, L0_ffn_w_up, L0_ffn_w_down,
           _taps=(), _return_raw=False):
    x = np.ascontiguousarray(np.asarray(L0_attn_x, dtype=np.float32).reshape(S, D))
    nc = build_program(taps=_taps)
    in_maps = []
    for c in range(NC):
        lo = c * CH
        hi = (15 - c) * CH
        x_own = np.ascontiguousarray(np.concatenate([x[lo:lo + CH], x[hi:hi + CH]], axis=0))
        in_maps.append({
            "x_own": x_own,
            "Wq": np.ascontiguousarray(L0_attn_Wq, dtype=np.float32),
            "Wk": np.ascontiguousarray(L0_attn_Wk, dtype=np.float32),
            "Wv": np.ascontiguousarray(L0_attn_Wv, dtype=np.float32),
            "Wo": np.ascontiguousarray(L0_attn_Wo, dtype=np.float32),
            "gq": np.ascontiguousarray(np.asarray(L0_attn_gq, dtype=np.float32).reshape(1, DH)),
            "gk": np.ascontiguousarray(np.asarray(L0_attn_gk, dtype=np.float32).reshape(1, DH)),
            "wg_sl": np.ascontiguousarray(np.asarray(L0_ffn_w_gate, dtype=np.float32)[c * WGS:(c + 1) * WGS, :]),
            "wu_sl": np.ascontiguousarray(np.asarray(L0_ffn_w_up, dtype=np.float32)[c * WGS:(c + 1) * WGS, :]),
            "wd_sl": np.ascontiguousarray(np.asarray(L0_ffn_w_down, dtype=np.float32)[:, c * WDS:(c + 1) * WDS]),
        })
    res = run_bass_kernel_spmd(nc, in_maps, core_ids=list(range(NC)))
    if _return_raw:
        return res
    full = np.zeros((S, D), dtype=np.float32)
    for c in range(NC):
        o = res.results[c]["out"]
        full[c * CH:(c + 1) * CH] = o[:CH]
        full[(15 - c) * CH:(16 - c) * CH] = o[CH:]
    return full.reshape(1, S, D)


# version 26
# speedup vs baseline: 1.1986x; 1.0810x over previous
from contextlib import ExitStack

import numpy as np
import ml_dtypes

import concourse.bass as bass
import concourse.mybir as mybir
from concourse.bass_utils import run_bass_kernel_spmd
from concourse.aot_env import allow_ici_all_to_all

F32 = mybir.dt.float32
FP8 = mybir.dt.float8e4
BF16 = mybir.dt.bfloat16
ALU = mybir.AluOpType
AF = mybir.ActivationFunctionType
AX = mybir.AxisListType.X

C32 = float(np.float32(1.5 * 2.0 ** 23))
C52 = float(np.float32(1.5 * 2.0 ** 52))

S = 8192
D = 2048
NH = 16
NKV = 4
DH = 128
DKV = NKV * DH
FI = 8192
NC = 8
EPS_LN = 1e-5
EPS_RMS = 1e-6
EPS_Q = 1e-5

CH = S // 16
TQ = 2 * CH
NXT = TQ // 128
NKT = S // 128
NDC = D // 128
NIC = FI // 128
WGS = FI // NC
WDS = FI // NC

STOP_AFTER = None
ENGS = ("y", "t", "v", "s", "g")
DMAQ = {"y": "DY", "s": "DS", "g": "DG"}
ESEM = {"t": "T", "v": "V", "s": "S", "g": "G", "y": "Y"}


class TT:
    __slots__ = ("ap", "w", "r")

    def __init__(self, ap):
        self.ap = ap
        self.w = {}
        self.r = {}

    def __getitem__(self, k):
        return self.ap[k]


NQSEM = 8


class Bld:
    def __init__(self, nc):
        self.nc = nc
        self.phase = "init"
        self.pend = {e: [] for e in ENGS}
        self.ops = {e: [] for e in ENGS}
        self.semh = {}
        self.cnt = {}
        for nm in ["T", "V", "S", "G", "Y"] + [f"{q}{i}" for q in ("DY", "DS", "DG")
                                              for i in range(NQSEM)]:
            self.semh[nm] = nc.alloc_semaphore(name="sem" + nm)
            self.cnt[nm] = 0
        self.qidx = {"DY": 0, "DS": 0, "DG": 0}
        self.ncc = 0
        self.hwm = {e: {} for e in ENGS}

    def wait(self, eng, ev):
        if ev is None:
            return
        sem, val = ev
        if self.hwm[eng].get(sem, 0) >= val:
            return
        self.hwm[eng][sem] = val
        h = self.semh[sem]
        self.ops[eng].append(lambda E, h=h, val=val: E.wait_ge(h, val))

    def waits(self, eng, evs):
        best = {}
        for ev in evs:
            if ev is None:
                continue
            sem, val = ev
            if best.get(sem, -1) < val:
                best[sem] = val
        for sem, val in best.items():
            self.wait(eng, (sem, val))

    def raw(self, eng, fn, inc):
        step = 16 if inc.startswith("D") else 1
        self.cnt[inc] += step
        val = self.cnt[inc]
        h = self.semh[inc]
        ph = self.phase

        def f(E, fn=fn, h=h, step=step, ph=ph, nc=self.nc):
            nc.push_named_scope(ph)
            try:
                fn(E).then_inc(h, step)
            finally:
                nc.pop_named_scope(ph)
        self.ops[eng].append(f)
        return (inc, val)

    def emit(self, eng, fn, reads=(), writes=(), dma=False, sem=None, inc=True):
        evs = []
        for t in reads:
            if t is not None:
                evs.extend(t.w.items())
        for t in writes:
            if t is not None:
                evs.extend(t.w.items())
                evs.extend(t.r.items())
        self.waits(eng, evs)
        if not inc:
            ph = self.phase

            def f(E, fn=fn, ph=ph, nc=self.nc):
                nc.push_named_scope(ph)
                try:
                    fn(E)
                finally:
                    nc.pop_named_scope(ph)
            self.ops[eng].append(f)
            for t in reads:
                if t is not None:
                    self.pend[eng].append((t, False))
            for t in writes:
                if t is not None:
                    self.pend[eng].append((t, True))
            return None
        if sem is None:
            if dma:
                q = DMAQ[eng]
                sem = f"{q}{self.qidx[q]}"
                self.qidx[q] = (self.qidx[q] + 1) % NQSEM
                self.wait(eng, (sem, self.cnt[sem]))
            else:
                sem = ESEM[eng]
        ev = self.raw(eng, fn, sem)
        s, v = ev
        for t in reads:
            if t is not None and t.r.get(s, 0) < v:
                t.r[s] = v
        for t in writes:
            if t is not None:
                t.w = {s: v}
                t.r = {}
        if self.pend[eng] and not sem.startswith("D") and sem in ESEM.values():
            for t, is_w in self.pend[eng]:
                if is_w:
                    t.w[s] = v
                else:
                    if t.r.get(s, 0) < v:
                        t.r[s] = v
            self.pend[eng] = []
        return ev

    def phase_barrier(self, tts):
        evs = []
        for t in tts:
            evs.extend(t.w.items())
            evs.extend(t.r.items())
        for e in ENGS:
            self.waits(e, evs)

    def collective(self, kind, op, ins_ap, outs_ap, reads=(), writes=()):
        nm = f"CC{self.ncc}"
        self.ncc += 1
        self.semh[nm] = self.nc.alloc_semaphore(name="sem" + nm)
        self.cnt[nm] = 0
        return self.emit("g", lambda E: E.collective_compute(
            kind, op, replica_groups=[list(range(NC))], ins=[ins_ap], outs=[outs_ap]),
            reads=reads, writes=writes, sem=nm)

    def dma(self, eng, dst, dst_ap, src, src_ap):
        return self.emit(eng, lambda E, o=dst_ap, i=src_ap: E.dma_start(out=o, in_=i),
                         reads=(src,), writes=(dst,), dma=True)

    def finalize(self):
        with allow_ici_all_to_all(), self.nc.Block() as block:
            for ename, attr in (("y", "sync"), ("t", "tensor"), ("v", "vector"),
                                ("s", "scalar"), ("g", "gpsimd")):
                def body(E, ops=self.ops[ename]):
                    for f in ops:
                        f(E)
                getattr(block, attr)(body)


def build_program(taps=()):
    nc = bass.Bass("TRN2", target_bir_lowering=False)
    b = Bld(nc)
    ctx = ExitStack()

    def sb(name, shape, dtype):
        return TT(ctx.enter_context(nc.sbuf_tensor(name, shape, dtype)).ap())

    def dram(name, shape, dtype, kind="Internal"):
        return TT(nc.dram_tensor(name, shape, dtype, kind=kind).ap())

    x_in = dram("x_own", [TQ, D], F32, "ExternalInput")
    wq_in = dram("Wq", [D, D], F32, "ExternalInput")
    wk_in = dram("Wk", [D, DKV], F32, "ExternalInput")
    wv_in = dram("Wv", [D, DKV], F32, "ExternalInput")
    wo_in = dram("Wo", [D, D], F32, "ExternalInput")
    gq_in = dram("gq", [1, DH], F32, "ExternalInput")
    gk_in = dram("gk", [1, DH], F32, "ExternalInput")
    wg_in = dram("wg_sl", [WGS, D], F32, "ExternalInput")
    wu_in = dram("wu_sl", [WGS, D], F32, "ExternalInput")
    wd_in = dram("wd_sl", [D, WDS], F32, "ExternalInput")
    out_d = dram("out", [TQ, D], F32, "ExternalOutput")

    KVP = 4 * DH * TQ
    kv_in = dram("kv_in", [NC, KVP], BF16)
    kv_out = dram("kv_out", [NC, KVP], BF16)
    QOP = 4 * DH * TQ
    q_ain = dram("q_ain", [NC, QOP], BF16)
    q_aout = dram("q_aout", [NC, QOP], BF16)
    OHP = 2 * DH * TQ
    o_ain0 = dram("o_ain0", [NC, OHP], BF16)
    o_aout0 = dram("o_aout0", [NC, OHP], BF16)
    o_ain1 = dram("o_ain1", [NC, OHP], BF16)
    o_aout1 = dram("o_aout1", [NC, OHP], BF16)
    wo_split_d = dram("wo_split_d", [2, NDC, 128, D], BF16)

    GPART = D * WGS
    DPART = WDS * D
    tern_in = dram("tern_in", [2 * GPART + DPART], FP8)
    tern_outg = dram("tern_outg", [NC * GPART], FP8)
    tern_outu = dram("tern_outu", [NC * GPART], FP8)
    tern_outd = dram("tern_outd", [NC * DPART], FP8)
    TERN_STRIDE = 2 * GPART + DPART

    ar_in = dram("ar_in", [1, 3], F32)
    ar_out = dram("ar_out", [1, 3], F32)

    x_tm_d = dram("x_tm_d", [NXT, 128, D], F32)
    x1_d = dram("x1_d", [NXT, 128, D], F32)

    Hm = np.ones((1, 1), dtype=np.float64)
    while Hm.shape[0] < 128:
        Hm = np.block([[Hm, Hm], [Hm, -Hm]])
    consts = {
        "h128": (np.asarray(Hm, dtype=ml_dtypes.bfloat16), BF16, [128, 128]),
        "identb": (np.eye(128, dtype=ml_dtypes.bfloat16), BF16, [128, 128]),
        "identf": (np.eye(128, dtype=np.float32), F32, [128, 128]),
        "onesb": (np.ones((128, 1), dtype=ml_dtypes.bfloat16), BF16, [128, 1]),
        "onesfr": (np.ones((1, 128), dtype=np.float32), F32, [1, 128]),
        "onesfc": (np.ones((128, 1), dtype=np.float32), F32, [128, 1]),
        "tri": (np.tril(np.ones((128, 128), dtype=np.float32)).T.copy(), F32, [128, 128]),
    }
    C = {}
    for nm, (arr, dt, shape) in consts.items():
        h = nc.inline_tensor(np.ascontiguousarray(arr), name=nm + "_c")
        t = sb(nm, shape, dt)
        b.dma("y", t, t.ap[:, :], TT(h.ap()), h.ap())
        C[nm] = t

    gq_r = sb("gq_r", [1, 128], F32)
    gk_r = sb("gk_r", [1, 128], F32)
    gq_c = sb("gq_c", [128, 1], F32)
    gk_c = sb("gk_c", [128, 1], F32)
    b.dma("y", gq_r, gq_r.ap[:, :], gq_in, gq_in.ap)
    b.dma("y", gk_r, gk_r.ap[:, :], gk_in, gk_in.ap)
    b.dma("y", gq_c, gq_c.ap[:, :], gq_in, gq_in.ap.rearrange("a d -> d a"))
    b.dma("y", gk_c, gk_c.ap[:, :], gk_in, gk_in.ap.rearrange("a d -> d a"))

    banks = [TT(ctx.enter_context(nc.psum_tensor(f"bank{i}", [128, 512], F32)).ap())
             for i in range(8)]

    blend = sb("blend", [128, 2], F32)
    mtmp = sb("mtmp", [1, 8], F32)
    ebias = sb("ebias", [128, 1], F32)
    gabs = sb("gabs", [1, 256], F32)
    svals = sb("svals", [1, 8], F32)
    shalfB = sb("shalfB", [128, 8], F32)
    b.emit("v", lambda E: E.memset(blend.ap[:, 0:1], 1.0), writes=(blend,))
    b.emit("v", lambda E: E.tensor_scalar(blend.ap[:, 0:1], blend.ap[:, 0:1], C52, C52,
                                          ALU.add, ALU.subtract), reads=(blend,), writes=(blend,))
    b.emit("v", lambda E: E.tensor_scalar(blend.ap[:, 1:2], blend.ap[:, 0:1], -1.0, 1.0,
                                          ALU.mult, ALU.add), reads=(blend,), writes=(blend,))

    def emit_rne(eng, out_t, out_ap, in_t, in_ap, scale1, t1, t2):
        sc_reads = (in_t,)
        if isinstance(scale1, tuple):
            sc_t, sc_ap = scale1
            sc_reads = (in_t, sc_t)
        else:
            sc_ap = scale1
        b.emit(eng, lambda E: E.tensor_scalar(t1.ap, in_ap, sc_ap, C32, ALU.mult, ALU.add),
               reads=sc_reads, writes=(t1,))
        b.emit(eng, lambda E: E.tensor_scalar(t2.ap, t1.ap, C32, C52, ALU.subtract, ALU.add),
               reads=(t1,), writes=(t2,))
        b.emit(eng, lambda E: E.tensor_scalar(t2.ap, t2.ap, C52, None, ALU.subtract),
               reads=(t2,), writes=(t2,))
        b.emit(eng, lambda E: E.tensor_scalar(t1.ap, t1.ap, C32, None, ALU.subtract),
               reads=(t1,), writes=(t1,))
        return b.emit(eng, lambda E: E.scalar_tensor_tensor(out_ap, t1.ap, blend.ap[:, 1:2],
                                                            t2.ap, ALU.mult, ALU.add),
                      reads=(t1, t2, blend), writes=(out_t,))

    b.phase = "p0a"

    p0 = ExitStack()
    p0_tts = []

    def sb0(name, shape, dtype):
        t = TT(p0.enter_context(nc.sbuf_tensor(name, shape, dtype)).ap())
        p0_tts.append(t)
        return t

    wstage = [sb0(f"wstage{i}", [128, D], F32) for i in range(2)]
    q1 = sb0("q1", [128, D], F32)
    q2 = sb0("q2", [128, D], F32)
    q3 = sb0("q3", [128, D], BF16)
    q4 = sb0("q4", [128, 1024], FP8)
    wacc = sb0("wacc", [128, 4], F32)
    wred = sb0("wred", [128, 2], F32)

    wtiles = []
    for j in range(WGS // 128):
        wtiles.append((0, wg_in, wg_in.ap[128 * j:128 * (j + 1), :], D))
    for j in range(WGS // 128):
        wtiles.append((1, wu_in, wu_in.ap[128 * j:128 * (j + 1), :], D))
    for j in range(D // 128):
        wtiles.append((2, wd_in, wd_in.ap[128 * j:128 * (j + 1), :], WDS))

    b.phase = "p1ln"
    p1 = ExitStack()
    p1_tts = []

    def sb1(name, shape, dtype):
        t = TT(p1.enter_context(nc.sbuf_tensor(name, shape, dtype)).ap())
        p1_tts.append(t)
        return t

    ln_hi = sb1("ln_hi", [128, NDC * TQ], BF16)
    ln_lo = sb1("ln_lo", [128, NDC * TQ], BF16)
    p1a = ExitStack()
    p1a_tts = []

    def sb1a(name, shape, dtype):
        t = TT(p1a.enter_context(nc.sbuf_tensor(name, shape, dtype)).ap())
        p1a_tts.append(t)
        return t

    xstage = [sb1a(f"xstage{i}", [128, D], F32) for i in range(2)]
    xsq_b = [sb1a("xsq0", [128, D], F32)] * 2
    xn32_b = [sb1a("xn320", [128, D], F32)] * 2
    xnhi_b = [sb1a(f"xnhi{i}", [128, D], BF16) for i in range(2)]
    xnlo_b = [sb1a(f"xnlo{i}", [128, D], BF16) for i in range(2)]
    lnst_b = [sb1a(f"lnst{i}", [128, 8], F32) for i in range(2)]

    b.emit("v", lambda E: E.memset(wacc.ap[:, 0:3], 0.0), writes=(wacc,))

    def wsum_job(i):
        mat, src_t, src_ap, cols = wtiles[i]
        stw = wstage[i % 2]
        b.dma("s", stw, stw.ap[:, 0:cols], src_t, src_ap)
        b.emit("v", lambda E, stw=stw, cols=cols: E.reduce_sum(
            wred.ap[:, 0:1], stw.ap[:, 0:cols], AX, apply_absolute_value=True),
            reads=(stw,), writes=(wred,))
        b.emit("v", lambda E, mat=mat: E.tensor_add(
            wacc.ap[:, mat:mat + 1], wacc.ap[:, mat:mat + 1], wred.ap[:, 0:1]),
            reads=(wacc, wred), writes=(wacc,))

    NWJ = len(wtiles)
    wj = [0]
    for i in range(NXT):
        st = xstage[i % 2]
        xsq = xsq_b[i % 2]
        xn32 = xn32_b[i % 2]
        xnhi_tm = xnhi_b[i % 2]
        xnlo_tm = xnlo_b[i % 2]
        lnst = lnst_b[i % 2]
        b.dma("y", st, st.ap[:, :], x_in, x_in.ap[128 * i:128 * (i + 1), :])
        b.emit("v", lambda E, st=st, lnst=lnst: E.reduce_sum(lnst.ap[:, 0:1], st.ap, AX),
               reads=(st,), writes=(lnst,))
        b.emit("s", lambda E, st=st, xsq=xsq: E.activation(xsq.ap, st.ap, AF.Square),
               reads=(st,), writes=(xsq,))
        b.emit("v", lambda E, lnst=lnst, xsq=xsq: E.reduce_sum(lnst.ap[:, 1:2], xsq.ap, AX),
               reads=(xsq,), writes=(lnst,))
        b.emit("v", lambda E, lnst=lnst: E.tensor_scalar(lnst.ap[:, 2:4], lnst.ap[:, 0:2], 1.0 / D,
                                              None, ALU.mult), reads=(lnst,), writes=(lnst,))
        b.emit("v", lambda E, lnst=lnst: E.tensor_scalar(lnst.ap[:, 4:5], lnst.ap[:, 2:3],
                                              lnst.ap[:, 2:3], None, ALU.mult),
               reads=(lnst,), writes=(lnst,))
        b.emit("v", lambda E, lnst=lnst: E.tensor_sub(lnst.ap[:, 5:6], lnst.ap[:, 3:4], lnst.ap[:, 4:5]),
               reads=(lnst,), writes=(lnst,))
        b.emit("v", lambda E, lnst=lnst: E.tensor_scalar(lnst.ap[:, 5:6], lnst.ap[:, 5:6], EPS_LN,
                                              None, ALU.add), reads=(lnst,), writes=(lnst,))
        b.emit("s", lambda E, lnst=lnst: E.activation(lnst.ap[:, 5:6], lnst.ap[:, 5:6], AF.Sqrt),
               reads=(lnst,), writes=(lnst,))
        b.emit("v", lambda E, lnst=lnst: E.reciprocal(lnst.ap[:, 6:7], lnst.ap[:, 5:6]),
               reads=(lnst,), writes=(lnst,))
        b.emit("v", lambda E, lnst=lnst: E.tensor_scalar(lnst.ap[:, 7:8], lnst.ap[:, 2:3],
                                              lnst.ap[:, 6:7], -1.0, ALU.mult, ALU.mult),
               reads=(lnst,), writes=(lnst,))
        b.emit("v", lambda E, st=st, lnst=lnst, xnhi_tm=xnhi_tm: E.tensor_scalar(
            xnhi_tm.ap, st.ap, lnst.ap[:, 6:7], lnst.ap[:, 7:8], ALU.mult, ALU.add),
            reads=(st, lnst), writes=(xnhi_tm,))
        b.emit("v", lambda E, st=st, lnst=lnst, xn32=xn32: E.tensor_scalar(
            xn32.ap, st.ap, lnst.ap[:, 6:7], lnst.ap[:, 7:8], ALU.mult, ALU.add),
            reads=(st, lnst), writes=(xn32,))
        b.emit("v", lambda E, xn32=xn32, xnhi_tm=xnhi_tm, xnlo_tm=xnlo_tm: E.tensor_sub(
            xnlo_tm.ap, xn32.ap, xnhi_tm.ap),
               reads=(xn32, xnhi_tm), writes=(xnlo_tm,))
        b.dma("g", x_tm_d, x_tm_d.ap[i], st, st.ap)
        for half, (srcT, dstT) in enumerate(((xnhi_tm, ln_hi), (xnlo_tm, ln_lo))):
            for grp in range(2):
                bk = banks[2 + (half * 2 + grp) % 2]
                bkb = bk.ap.bitcast(BF16)
                for p in range(8):
                    cch = 8 * grp + p
                    b.emit("t", lambda E, bkb=bkb, p=p, srcT=srcT, cch=cch: E.transpose(
                        bkb[:, 128 * p:128 * (p + 1)], srcT.ap[:, 128 * cch:128 * (cch + 1)],
                        C["identb"].ap), reads=(srcT, C["identb"]), writes=(bk,))
                def evac(E, bkb=bkb, dstT=dstT, grp=grp, i=i):
                    dst = dstT.ap.rearrange("p (c q) -> p c q", c=NDC)[
                        :, 8 * grp:8 * grp + 8, 128 * i:128 * (i + 1)]
                    return E.tensor_copy(dst, bkb.rearrange("p (c q) -> p c q", c=8))
                b.emit("v", evac, reads=(bk,), writes=(dstT,))
        b.phase = "p0a"
        for _ in range(2):
            if wj[0] < NWJ:
                wsum_job(wj[0])
                wj[0] += 1
        b.phase = "p1ln"

    b.phase_barrier(p1a_tts)
    p1a.close()

    b.phase = "p1m"

    b.emit("s", lambda E: E.activation(gabs.ap[0:1, 0:128], gq_r.ap, AF.Abs),
           reads=(gq_r,), writes=(gabs,))
    b.emit("s", lambda E: E.activation(gabs.ap[0:1, 128:256], gk_r.ap, AF.Abs),
           reads=(gk_r,), writes=(gabs,))
    b.emit("v", lambda E: E.reduce_max(mtmp.ap[0:1, 0:1], gabs.ap[0:1, 0:128], AX),
           reads=(gabs,), writes=(mtmp,))
    b.emit("v", lambda E: E.reduce_max(mtmp.ap[0:1, 1:2], gabs.ap[0:1, 128:256], AX),
           reads=(gabs,), writes=(mtmp,))
    b.emit("v", lambda E: E.tensor_scalar(mtmp.ap[0:1, 2:3], mtmp.ap[0:1, 0:1],
                                          mtmp.ap[0:1, 1:2], -float(np.sqrt(128.0)),
                                          ALU.mult, ALU.mult), reads=(mtmp,), writes=(mtmp,))
    b.emit("t", lambda E: E.matmul(banks[4].ap[:, 0:1], C["onesfr"].ap, mtmp.ap[0:1, 2:3],
                                   start=True, stop=True), reads=(C["onesfr"], mtmp), writes=(banks[4],))
    b.emit("v", lambda E: E.tensor_copy(ebias.ap, banks[4].ap[:, 0:1]),
           reads=(banks[4],), writes=(ebias,))

    def split_weight(src_t, src_ap_fn, ntiles, cols, hi_t, lo_t, stage, w32, q="y"):
        w32b = w32 if isinstance(w32, list) else [w32, w32]
        for j in range(ntiles):
            w32 = w32b[j % 2]
            b.dma(q, w32, w32.ap[:, 0:cols], src_t, src_ap_fn(j))
            b.emit("s", lambda E, j=j, w32=w32: E.activation(
                hi_t.ap[:, cols * j:cols * (j + 1)], w32.ap[:, 0:cols], AF.Copy),
                   reads=(w32,), writes=(hi_t,))
            b.emit("v", lambda E, j=j, w32=w32: E.tensor_sub(
                lo_t.ap[:, cols * j:cols * (j + 1)], w32.ap[:, 0:cols],
                hi_t.ap[:, cols * j:cols * (j + 1)]),
                   reads=(w32, hi_t), writes=(lo_t,))

    b.phase = "p1kv"
    kvsq_b = [sb1(f"kvsq{i}", [128, 512], F32) for i in range(2)]
    kvn32_b = [sb1(f"kvn32{i}", [128, 512], F32) for i in range(2)]
    kvhi_b = [sb1(f"kvhi{i}", [128, 512], BF16) for i in range(2)]
    kvlo_b = [sb1(f"kvlo{i}", [128, 512], BF16) for i in range(2)]
    rmsrow_b = [sb1(f"rmsrow{i}", [1, 512], F32) for i in range(2)]

    p1b = ExitStack()
    p1b_tts = []

    def sb1b(name, shape, dtype):
        t = TT(p1b.enter_context(nc.sbuf_tensor(name, shape, dtype)).ap())
        p1b_tts.append(t)
        return t

    wkv32 = [sb1b(f"wkv32_{i}", [128, DKV], F32) for i in range(2)]
    wk_hi = sb1b("wk_hi", [128, NDC * DKV], BF16)
    wk_lo = sb1b("wk_lo", [128, NDC * DKV], BF16)
    wv_hi = sb1b("wv_hi", [128, NDC * DKV], BF16)
    wv_lo = sb1b("wv_lo", [128, NDC * DKV], BF16)
    split_weight(wk_in, lambda j: wk_in.ap[128 * j:128 * (j + 1), :], NDC, DKV, wk_hi, wk_lo, None, wkv32)
    split_weight(wv_in, lambda j: wv_in.ap[128 * j:128 * (j + 1), :], NDC, DKV, wv_hi, wv_lo, None, wkv32)

    def proj_head(w_hi, w_lo, hd, qh, pb):
        combos = ((w_hi, ln_hi), (w_hi, ln_lo), (w_lo, ln_hi))
        n_mm = 3 * NDC
        k = 0
        for (wt, lt) in combos:
            for cchunk in range(NDC):
                wap = wt.ap[:, DKV * cchunk + DH * hd:DKV * cchunk + DH * (hd + 1)]
                lap = lt.ap.rearrange("p (c q) -> p c q", c=NDC)[
                    :, cchunk, 512 * qh:512 * (qh + 1)]
                b.emit("t", lambda E, wap=wap, lap=lap, st=(k == 0), sp=(k == n_mm - 1),
                       pb=pb: E.matmul(pb.ap[:, 0:512], wap, lap, start=st, stop=sp),
                       reads=(wt, lt), writes=(pb,), inc=(k == n_mm - 1))
                k += 1

    def rms_split_store(pb, g_col, eps, dst_t, dst_pairs, do_rms, sbk2=None, par=0):
        kvsq, kvn32, kvhi, kvlo, rmsrow = (kvsq_b[par], kvn32_b[par], kvhi_b[par],
                                           kvlo_b[par], rmsrow_b[par])
        if sbk2 is None:
            sbk2 = banks[6]
        if do_rms:
            b.emit("s", lambda E: E.activation(kvsq.ap, pb.ap[:, 0:512], AF.Square),
                   reads=(pb,), writes=(kvsq,))
            b.emit("t", lambda E, sbk2=sbk2: E.matmul(sbk2.ap[0:1, 0:512], C["onesfc"].ap,
                                           kvsq.ap, start=True, stop=True),
                   reads=(C["onesfc"], kvsq), writes=(sbk2,))
            b.emit("v", lambda E, sbk2=sbk2: E.tensor_scalar(rmsrow.ap, sbk2.ap[0:1, 0:512],
                                                  1.0 / DH, eps, ALU.mult, ALU.add),
                   reads=(sbk2,), writes=(rmsrow,))
            b.emit("s", lambda E: E.activation(rmsrow.ap, rmsrow.ap, AF.Sqrt),
                   reads=(rmsrow,), writes=(rmsrow,))
            b.emit("v", lambda E: E.reciprocal(rmsrow.ap, rmsrow.ap),
                   reads=(rmsrow,), writes=(rmsrow,))
            b.emit("t", lambda E, sbk2=sbk2: E.matmul(sbk2.ap[:, 0:512], C["onesfr"].ap,
                                           rmsrow.ap, start=True, stop=True),
                   reads=(C["onesfr"], rmsrow), writes=(sbk2,))
            b.emit("v", lambda E, sbk2=sbk2: E.tensor_copy(kvsq.ap, sbk2.ap[:, 0:512]),
                   reads=(sbk2,), writes=(kvsq,))
            b.emit("v", lambda E: E.scalar_tensor_tensor(kvn32.ap, pb.ap[:, 0:512],
                                                         g_col.ap, kvsq.ap,
                                                         ALU.mult, ALU.mult),
                   reads=(pb, g_col, kvsq), writes=(kvn32,))
        else:
            b.emit("v", lambda E: E.tensor_copy(kvn32.ap, pb.ap[:, 0:512]),
                   reads=(pb,), writes=(kvn32,))
        b.emit("v", lambda E: E.tensor_copy(kvhi.ap, kvn32.ap), reads=(kvn32,), writes=(kvhi,))
        b.emit("v", lambda E: E.tensor_sub(kvlo.ap, kvn32.ap, kvhi.ap),
               reads=(kvn32, kvhi), writes=(kvlo,))
        for dst_hi_ap, dst_lo_ap in dst_pairs:
            b.dma("g", dst_t, dst_hi_ap, kvhi, kvhi.ap)
            b.dma("g", dst_t, dst_lo_ap, kvlo, kvlo.ap)

    kv_inv = kv_in.ap.rearrange("r (s hl d q) -> r s hl d q", s=2, hl=2, d=DH)
    pcall = [0]
    for hd in range(NKV):
        for qh in range(TQ // 512):
            pb = banks[5 + 2 * (pcall[0] % 2)]
            sbk2 = banks[6 - 2 * (pcall[0] % 2)]
            pcall[0] += 1
            proj_head(wk_hi, wk_lo, hd, qh, pb)
            dsts = []
            for dr in (2 * hd, 2 * hd + 1):
                dsts.append((kv_inv[dr, 0, 0, :, 512 * qh:512 * (qh + 1)],
                             kv_inv[dr, 0, 1, :, 512 * qh:512 * (qh + 1)]))
            rms_split_store(pb, gk_c, EPS_RMS, kv_in, dsts, True, sbk2,
                            par=(pcall[0] % 2))
            if wj[0] < NWJ:
                b.phase = "p0a"
                wsum_job(wj[0])
                wj[0] += 1
                b.phase = "p1kv"

    v32sb_b = kvn32_b
    v32p_b = kvsq_b
    vhip_b = kvhi_b
    vlop_b = kvlo_b
    for hd in range(NKV):
        for qh in range(TQ // 512):
            pb = banks[5 + 2 * (pcall[0] % 2)]
            sbk2 = banks[6 - 2 * (pcall[0] % 2)]
            pcall[0] += 1
            par = pcall[0] % 2
            v32sb, v32p, vhip, vlop = (v32sb_b[par], v32p_b[par], vhip_b[par],
                                       vlop_b[par])
            proj_head(wv_hi, wv_lo, hd, qh, pb)
            b.emit("v", lambda E, pb=pb, v32sb=v32sb: E.tensor_copy(v32sb.ap, pb.ap[:, 0:512]),
                   reads=(pb,), writes=(v32sb,))
            for p in range(4):
                b.emit("t", lambda E, p=p, sbk2=sbk2, v32sb=v32sb: E.transpose(
                    sbk2.ap[:, 128 * p:128 * (p + 1)],
                    v32sb.ap[:, 128 * p:128 * (p + 1)], C["identf"].ap),
                    reads=(v32sb, C["identf"]), writes=(sbk2,), inc=(p == 3))
            b.emit("v", lambda E, sbk2=sbk2, v32p=v32p: E.tensor_copy(v32p.ap, sbk2.ap),
                   reads=(sbk2,), writes=(v32p,))
            b.emit("v", lambda E, v32p=v32p, vhip=vhip: E.tensor_copy(vhip.ap, v32p.ap),
                   reads=(v32p,), writes=(vhip,))
            b.emit("v", lambda E, v32p=v32p, vhip=vhip, vlop=vlop: E.tensor_sub(
                vlop.ap, v32p.ap, vhip.ap), reads=(v32p, vhip), writes=(vlop,))
            for p in range(4):
                tok0 = 512 * qh + 128 * p
                for dr in (2 * hd, 2 * hd + 1):
                    vv = kv_inv[dr, 1].rearrange("hl d q -> hl (d q)").rearrange(
                        "hl (q d) -> hl q d", d=DH)
                    b.dma("g", kv_in, vv[0, tok0:tok0 + 128, :],
                          vhip, vhip.ap[:, 128 * p:128 * (p + 1)])
                    b.dma("g", kv_in, vv[1, tok0:tok0 + 128, :],
                          vlop, vlop.ap[:, 128 * p:128 * (p + 1)])
            if wj[0] < NWJ:
                b.phase = "p0a"
                wsum_job(wj[0])
                wj[0] += 1
                b.phase = "p1kv"

    ev_kv = b.collective("AllToAll", ALU.bypass, kv_in.ap.opt(), kv_out.ap.opt(),
                         reads=(kv_in,), writes=(kv_out,))

    b.phase = "p0a"
    while wj[0] < NWJ:
        wsum_job(wj[0])
        wj[0] += 1
    b.emit("t", lambda E: E.matmul(banks[0].ap[0:1, 0:3], C["onesfc"].ap, wacc.ap[:, 0:3],
                                   start=True, stop=True),
           reads=(C["onesfc"], wacc), writes=(banks[0],))
    b.emit("v", lambda E: E.tensor_copy(svals.ap[0:1, 0:3], banks[0].ap[0:1, 0:3]),
           reads=(banks[0],), writes=(svals,))
    b.dma("g", ar_in, ar_in.ap[0:1, 0:3], svals, svals.ap[0:1, 0:3])
    ev_ar = b.collective("AllReduce", ALU.add, ar_in.ap.opt(), ar_out.ap.opt(),
                         reads=(ar_in,), writes=(ar_out,))

    b.phase_barrier(p1b_tts)
    p1b.close()

    b.phase = "p1q"
    HD2 = D // 2
    wq_hi = sb1("wq_hi", [128, NDC * HD2], BF16)
    wq_lo = sb1("wq_lo", [128, NDC * HD2], BF16)
    wq32 = [sb1(f"wq32_{i}", [128, HD2], F32) for i in range(2)]
    for wh in range(2):
        split_weight(wq_in, lambda j: wq_in.ap[128 * j:128 * (j + 1), HD2 * wh:HD2 * (wh + 1)],
                     NDC, HD2, wq_hi, wq_lo, None, wq32, q=("y" if wh == 0 else "s"))
        for hd8 in range(NH // 2):
            hd = 8 * wh + hd8
            for qh in range(TQ // 512):
                pb = banks[5 + 2 * (pcall[0] % 2)]
                sbk2 = banks[6 - 2 * (pcall[0] % 2)]
                pcall[0] += 1
                combos = ((wq_hi, ln_hi), (wq_hi, ln_lo), (wq_lo, ln_hi))
                n_mm = 3 * NDC
                k = 0
                for (wt, lt) in combos:
                    for cchunk in range(NDC):
                        wap = wt.ap[:, HD2 * cchunk + DH * hd8:HD2 * cchunk + DH * (hd8 + 1)]
                        lap = lt.ap.rearrange("p (c q) -> p c q", c=NDC)[
                            :, cchunk, 512 * qh:512 * (qh + 1)]
                        b.emit("t", lambda E, wap=wap, lap=lap, st=(k == 0), sp=(k == n_mm - 1),
                               pb=pb: E.matmul(pb.ap[:, 0:512], wap, lap, start=st, stop=sp),
                               reads=(wt, lt), writes=(pb,), inc=(k == n_mm - 1))
                        k += 1
                qv = q_ain.ap.rearrange("r (hl g d q) -> r hl g d q", hl=2, g=2, d=DH)
                dst_hi = qv[hd // 2, 0, hd % 2, :, 512 * qh:512 * (qh + 1)]
                dst_lo = qv[hd // 2, 1, hd % 2, :, 512 * qh:512 * (qh + 1)]
                rms_split_store(pb, gq_c, EPS_RMS, q_ain, [(dst_hi, dst_lo)], True,
                                sbk2, par=(pcall[0] % 2))


    ev_q = b.collective("AllToAll", ALU.bypass, q_ain.ap.opt(), q_aout.ap.opt(),
                        reads=(q_ain,), writes=(q_aout,))

    b.phase = "p0a"
    b.dma("g", svals, svals.ap[0:1, 4:7], ar_out, ar_out.ap[0:1, 0:3])
    b.emit("v", lambda E: E.tensor_scalar(svals.ap[0:1, 4:7], svals.ap[0:1, 4:7],
                                          1.0 / (FI * D), EPS_Q, ALU.mult, ALU.max),
           reads=(svals,), writes=(svals,))
    b.emit("v", lambda E: E.tensor_scalar(svals.ap[0:1, 0:3], svals.ap[0:1, 4:7],
                                          0.5, None, ALU.mult), reads=(svals,), writes=(svals,))
    b.emit("t", lambda E: E.matmul(banks[0].ap[:, 0:3], C["onesfr"].ap, svals.ap[0:1, 0:3],
                                   start=True, stop=True),
           reads=(C["onesfr"], svals), writes=(banks[0],))
    b.emit("v", lambda E: E.tensor_copy(shalfB.ap[:, 0:3], banks[0].ap[:, 0:3]),
           reads=(banks[0],), writes=(shalfB,))
    b.emit("v", lambda E: E.tensor_scalar(shalfB.ap[:, 4:7], shalfB.ap[:, 0:3], -1.0,
                                          None, ALU.mult), reads=(shalfB,), writes=(shalfB,))

    b.phase = "p0b"
    tern_g = tern_in.ap[0:2 * GPART].rearrange("(m c p o) -> m c p o", m=2, c=NDC, p=128)
    tern_dv = tern_in.ap[2 * GPART:].rearrange("(bq p o) -> bq p o", bq=WDS // 128, p=128)

    def quant_job(i, jb):
        mat, src_t, src_ap, cols = wtiles[i]
        st = wstage[i % 2]
        b.dma("g", st, st.ap[:, 0:cols], src_t, src_ap)
        b.emit("s", lambda E, st=st, cols=cols: E.activation(q1.ap[:, 0:cols],
                                                             st.ap[:, 0:cols], AF.Abs),
               reads=(st,), writes=(q1,))
        b.emit("s", lambda E, st=st, cols=cols: E.sign(q2.ap[:, 0:cols], st.ap[:, 0:cols]),
               reads=(st,), writes=(q2,))
        b.emit("v", lambda E, cols=cols, mat=mat: E.tensor_single_scalar(
            q1.ap[:, 0:cols], q1.ap[:, 0:cols], shalfB.ap[:, mat:mat + 1], ALU.is_gt),
            reads=(q1, shalfB), writes=(q1,))
        b.emit("v", lambda E, cols=cols: E.tensor_mul(q3.ap[:, 0:cols], q1.ap[:, 0:cols],
                                                      q2.ap[:, 0:cols]),
            reads=(q1, q2), writes=(q3,))
        j = i % (WGS // 128) if mat < 2 else i - 2 * (WGS // 128)
        for grp in range(cols // 1024):
            bk = jb
            bkb = bk.ap.bitcast(BF16)
            for p in range(8):
                cc = 8 * grp + p
                b.emit("t", lambda E, bkb=bkb, p=p, cc=cc, bk=bk: E.transpose(
                    bkb[:, 128 * p:128 * (p + 1)], q3.ap[:, 128 * cc:128 * (cc + 1)],
                    C["identb"].ap), reads=(q3, C["identb"]), writes=(bk,), inc=(p == 7))
            b.emit("v", lambda E, bkb=bkb, bk=bk: E.tensor_copy(q4.ap, bkb[:, 0:1024]),
                   reads=(bk,), writes=(q4,))
            if mat < 2:
                dst = tern_g[mat, 8 * grp:8 * grp + 8, :, 128 * j:128 * (j + 1)]
            else:
                dst = tern_dv[8 * grp:8 * grp + 8, :, 128 * j:128 * (j + 1)]
            b.dma("g", tern_in, dst.transpose([1, 0, 2]),
                  q4, q4.ap.rearrange("p (c f) -> p c f", c=8))

    b.phase = "p2"
    b.phase_barrier(p1_tts)
    p1.close()
    p2 = ExitStack()
    p2_tts = []

    def sb2(name, shape, dtype):
        t = TT(p2.enter_context(nc.sbuf_tensor(name, shape, dtype)).ap())
        p2_tts.append(t)
        return t

    kfull = sb2("kfull", [128, 2 * S], BF16)
    vfull = sb2("vfull", [128, 2 * S], BF16)
    qfull = sb2("qfull", [128, 2 * S], BF16)
    e32b = [sb2(f"e32b{i}", [128, 512], F32) for i in range(4)]
    ehib = [sb2(f"ehib{i}", [128, 512], BF16) for i in range(4)]
    elob = [sb2(f"elob{i}", [128, 512], BF16) for i in range(4)]
    o32s = sb2("o32s", [128, 512], F32)
    ohis = sb2("ohis", [128, 512], BF16)
    olos = sb2("olos", [128, 512], BF16)
    recrow = sb2("recrow", [1, 512], F32)
    sclc = sb2("sclc", [128, 1], F32)
    b.emit("v", lambda E: E.memset(sclc.ap, 1.0 / float(np.sqrt(128.0))), writes=(sclc,))
    wos32 = [sb2(f"wos32_{i}", [128, D], F32) for i in range(2)]
    woh = [sb2(f"woh{i}", [128, D], BF16) for i in range(2)]
    wol = [sb2(f"wol{i}", [128, D], BF16) for i in range(2)]
    wo_sv = wo_split_d.ap.rearrange("hl j p d -> hl j p d")

    def wo_job(j):
        w32 = wos32[j % 2]
        wh, wl = woh[j % 2], wol[j % 2]
        b.dma("s", w32, w32.ap, wo_in, wo_in.ap[128 * j:128 * (j + 1), :])
        b.emit("s", lambda E, w32=w32, wh=wh: E.activation(wh.ap, w32.ap, AF.Copy),
               reads=(w32,), writes=(wh,))
        b.emit("v", lambda E, w32=w32, wh=wh, wl=wl: E.tensor_sub(wl.ap, w32.ap, wh.ap),
               reads=(w32, wh), writes=(wl,))
        b.dma("s", wo_split_d, wo_sv[0, j], wh, wh.ap)
        b.dma("s", wo_split_d, wo_sv[1, j], wl, wl.ap)

    def chunk_rank(ch):
        return ch if ch < NC else 15 - ch

    def chunk_local(ch):
        return 0 if ch < NC else CH

    kv_ov = kv_out.ap.rearrange("r (s hl d q) -> r s hl d q", s=2, hl=2, d=DH)
    q_ov = q_aout.ap.rearrange("r (hl g d q) -> r hl g d q", hl=2, g=2, d=DH)
    o_iv0 = o_ain0.ap.rearrange("r (hl d q) -> r hl d q", hl=2, d=DH)
    o_iv1 = o_ain1.ap.rearrange("r (hl d q) -> r hl d q", hl=2, d=DH)

    for ch in range(16):
        r, lc = chunk_rank(ch), chunk_local(ch)
        for hl in range(2):
            b.dma("y", kfull, kfull.ap[:, hl * S + ch * CH:hl * S + (ch + 1) * CH],
                  kv_out, kv_ov[r, 0, hl, :, lc:lc + CH])
            vsrc = kv_ov[r, 1, hl].rearrange("d q -> (d q)").rearrange(
                "(q d) -> q d", d=DH)[lc:lc + CH, :]
            vsrc = vsrc.rearrange("(sg p) d -> p sg d", p=128)
            vdst = vfull.ap.rearrange("p (hl f) -> p hl f", hl=2)[
                :, hl, ch * CH:(ch + 1) * CH].rearrange("p (sg d) -> p sg d", d=128)
            b.dma("y", vfull, vdst, kv_out, vsrc)

    NQB = S // CH
    KTC = CH // 128
    SB0, SGB, OB0 = 0, 4, 6
    for hloc in range(2):
        for ch in range(16):
            r, lc = chunk_rank(ch), chunk_local(ch)
            for hl in range(2):
                b.dma("y", qfull, qfull.ap[:, hl * S + ch * CH:hl * S + (ch + 1) * CH],
                      q_aout, q_ov[r, hl, hloc, :, lc:lc + CH])
        for qb in range(NQB):
            G0 = qb * CH
            dstart = G0 // 128
            nkt = dstart + KTC
            ob = banks[OB0 + qb % 2]
            sgb = banks[SGB + qb % 2]

            def s_mms(kt):
                sbk = banks[SB0 + kt % 4]
                qcol0 = max(0, 128 * (kt - dstart))
                N = CH - qcol0
                combos = ((0, 0), (0, 1), (1, 0))
                for mi, (khl, qhl) in enumerate(combos):
                    kap = kfull.ap[:, khl * S + 128 * kt:khl * S + 128 * (kt + 1)]
                    qap = qfull.ap[:, qhl * S + G0 + qcol0:qhl * S + G0 + CH]
                    b.emit("t", lambda E, kap=kap, qap=qap, sbk=sbk, qcol0=qcol0,
                           st=(mi == 0), sp=(mi == 2), N=N: E.matmul(
                               sbk.ap[:, qcol0:qcol0 + N], kap, qap, start=st, stop=sp),
                           reads=(kfull, qfull), writes=(sbk,), inc=(mi == 2))

            def e_path(kt):
                sbk = banks[SB0 + kt % 4]
                qcol0 = max(0, 128 * (kt - dstart))
                N = CH - qcol0
                e32 = e32b[kt % 4]
                b.emit("s", lambda E, sbk=sbk, e32=e32, qcol0=qcol0, N=N: E.activation(
                    e32.ap[:, 0:N], sbk.ap[:, qcol0:qcol0 + N], AF.Exp,
                    bias=ebias.ap[:, 0:1], scale=sclc.ap[:, 0:1]),
                    reads=(sbk, ebias, sclc), writes=(e32,))
                if kt >= dstart:
                    b.emit("v", lambda E, e32=e32: E.tensor_mul(
                        e32.ap[:, 0:128], e32.ap[:, 0:128], C["tri"].ap),
                        reads=(e32, C["tri"]), writes=(e32,))
                ehi, elo = ehib[kt % 4], elob[kt % 4]
                b.emit("s", lambda E, e32=e32, ehi=ehi, N=N: E.activation(
                    ehi.ap[:, 0:N], e32.ap[:, 0:N], AF.Copy), reads=(e32,), writes=(ehi,))
                b.emit("v", lambda E, e32=e32, ehi=ehi, elo=elo, N=N: E.tensor_sub(
                    elo.ap[:, 0:N], e32.ap[:, 0:N], ehi.ap[:, 0:N]),
                    reads=(e32, ehi), writes=(elo,))

            def sig_pv(kt):
                qcol0 = max(0, 128 * (kt - dstart))
                N = CH - qcol0
                ehi, elo = ehib[kt % 4], elob[kt % 4]
                last = (kt == nkt - 1)
                for pi, et in enumerate((ehi, elo)):
                    b.emit("t", lambda E, et=et, sgb=sgb, qcol0=qcol0, N=N,
                           st=(kt == 0 and pi == 0), sp=(last and pi == 1): E.matmul(
                               sgb.ap[0:1, qcol0:qcol0 + N], C["onesb"].ap, et.ap[:, 0:N],
                               start=st, stop=sp),
                           reads=(et, C["onesb"]), writes=(sgb,), inc=(pi == 1))
                pvcombos = ((0, ehi), (0, elo), (1, ehi))
                for mi, (vhl, et) in enumerate(pvcombos):
                    vap = vfull.ap[:, vhl * S + 128 * kt:vhl * S + 128 * (kt + 1)]
                    b.emit("t", lambda E, vap=vap, et=et, ob=ob, qcol0=qcol0, N=N,
                           st=(kt == 0 and mi == 0), sp=(last and mi == 2): E.matmul(
                               ob.ap[:, qcol0:qcol0 + N], vap, et.ap[:, 0:N],
                               start=st, stop=sp),
                           reads=(vfull, et), writes=(ob,), inc=(mi == 2))

            DEPTH = 3
            for kt in range(nkt):
                s_mms(kt)
                e_path(kt)
                if kt >= DEPTH:
                    sig_pv(kt - DEPTH)
            for kt in range(max(0, nkt - DEPTH), nkt):
                sig_pv(kt)
            b.phase = "p2ep"
            b.emit("v", lambda E, sgb=sgb: E.reciprocal(recrow.ap[0:1, 0:CH],
                                                        sgb.ap[0:1, 0:CH]),
                   reads=(sgb,), writes=(recrow,))
            rb = banks[SB0 + (nkt + 1) % 4]
            b.emit("t", lambda E, rb=rb: E.matmul(rb.ap[:, 0:CH], C["onesfr"].ap,
                                                  recrow.ap[0:1, 0:CH], start=True, stop=True),
                   reads=(C["onesfr"], recrow), writes=(rb,))
            b.emit("v", lambda E, rb=rb: E.tensor_copy(e32b[0].ap[:, 0:CH], rb.ap[:, 0:CH]),
                   reads=(rb,), writes=(e32b[0],))
            b.emit("v", lambda E, ob=ob: E.tensor_mul(o32s.ap[:, 0:CH],
                                                      ob.ap[:, 0:CH], e32b[0].ap[:, 0:CH]),
                   reads=(ob, e32b[0]), writes=(o32s,))
            b.emit("v", lambda E: E.tensor_copy(ohis.ap[:, 0:CH], o32s.ap[:, 0:CH]),
                   reads=(o32s,), writes=(ohis,))
            b.emit("v", lambda E: E.tensor_sub(olos.ap[:, 0:CH], o32s.ap[:, 0:CH],
                                               ohis.ap[:, 0:CH]), reads=(o32s, ohis), writes=(olos,))
            r, lc = chunk_rank(qb), chunk_local(qb)
            o_t, o_ivh = ((o_ain0, o_iv0), (o_ain1, o_iv1))[hloc]
            b.dma("y", o_t, o_ivh[r, 0, :, lc:lc + CH], ohis, ohis.ap[:, 0:CH])
            b.dma("y", o_t, o_ivh[r, 1, :, lc:lc + CH], olos, olos.ap[:, 0:CH])
            b.phase = "p2"
            if hloc == 0:
                b.phase = "p0b"
                jb = banks[SGB + (qb + 1) % 2]
                quant_job(2 * qb, jb)
                quant_job(2 * qb + 1, jb)
                wo_job(qb)
                if qb == 3:
                    ev_ag_g = b.collective(
                        "AllGather", ALU.bypass, tern_in.ap[0:GPART].opt(),
                        tern_outg.ap.opt(), reads=(tern_in,), writes=(tern_outg,))
                if qb == 7:
                    ev_ag_u = b.collective(
                        "AllGather", ALU.bypass, tern_in.ap[GPART:2 * GPART].opt(),
                        tern_outu.ap.opt(), reads=(tern_in,), writes=(tern_outu,))
                b.phase = "p2"

        if hloc == 0:
            ev_o0 = b.collective("AllToAll", ALU.bypass, o_ain0.ap.opt(),
                                 o_aout0.ap.opt(), reads=(o_ain0,), writes=(o_aout0,))

    ev_o1 = b.collective("AllToAll", ALU.bypass, o_ain1.ap.opt(), o_aout1.ap.opt(),
                         reads=(o_ain1,), writes=(o_aout1,))
    ev_ag_d = b.collective(
        "AllGather", ALU.bypass, tern_in.ap[2 * GPART:].opt(),
        tern_outd.ap.opt(), reads=(tern_in,), writes=(tern_outd,))

    b.phase = "p3"
    b.phase_barrier(p2_tts)
    p2.close()
    b.phase_barrier(p0_tts)
    p0.close()
    p3 = ExitStack()
    p3_tts = []

    def sb3(name, shape, dtype):
        t = TT(p3.enter_context(nc.sbuf_tensor(name, shape, dtype)).ap())
        p3_tts.append(t)
        return t

    obuf_h = [sb3(f"obuf_h{i}", [128, 2 * NC * TQ], BF16) for i in range(2)]
    wo_g = [sb3(f"wo_g{i}", [128, 2 * NH * 512], BF16) for i in range(2)]
    xg = [sb3(f"xg{i}", [128, NXT * 512], F32) for i in range(2)]
    x1g = [sb3(f"x1g{i}", [128, NXT * 512], F32) for i in range(2)]

    o_ov0 = o_aout0.ap.rearrange("r (hl d q) -> r hl d q", hl=2, d=DH)
    o_ov1 = o_aout1.ap.rearrange("r (hl d q) -> r hl d q", hl=2, d=DH)
    obq = [obuf_h[i].ap.rearrange("p (b q) -> p b q", b=2 * NC) for i in range(2)]

    def load_obuf(half, ov, ot):
        for srcr in range(NC):
            for hl in range(2):
                qn = ("y", "s", "g")[(2 * srcr + hl) % 3]
                b.dma(qn, obuf_h[half], obq[half][:, hl * NC + srcr, :], ot, ov[srcr, hl])

    load_obuf(0, o_ov0, o_aout0)
    wo_gv = [wo_g[i].ap.rearrange("p (b c) -> p b c", b=2 * NH) for i in range(2)]

    def load_wog(g, slot):
        for hl in range(2):
            for hd in range(NDC):
                b.dma("y" if hd % 2 == 0 else "s", wo_g[slot],
                      wo_gv[slot][:, hl * NH + hd, :], wo_split_d,
                      wo_sv[hl, hd][:, 512 * g:512 * (g + 1)])
        for ti in range(NXT):
            b.dma("g", xg[slot], xg[slot].ap[:, 512 * ti:512 * (ti + 1)],
                  x_tm_d, x_tm_d.ap[ti][:, 512 * g:512 * (g + 1)])

    load_wog(0, 0)
    load_wog(1, 1)
    load_obuf(1, o_ov1, o_aout1)
    combos = ((0, 0), (0, 1), (1, 0))
    for g in range(4):
        slot = g % 2
        if g + 1 < 4 and g >= 1:
            load_wog(g + 1, 1 - slot)
        for ti in range(NXT):
            pb = banks[ti % 2]
            k = 0
            for half in range(2):
                for srcr in range(NC):
                    h = 2 * srcr + half
                    for (ohl, whl) in combos:
                        oap = obq[half][:, ohl * NC + srcr, 128 * ti:128 * (ti + 1)]
                        wap = wo_gv[slot][:, whl * NH + h, :]
                        b.emit("t", lambda E, oap=oap, wap=wap, pb=pb, st=(k == 0),
                               sp=(k == 3 * NH - 1): E.matmul(
                                   pb.ap[:, 0:512], oap, wap, start=st, stop=sp),
                               reads=(obuf_h[half], wo_g[slot]), writes=(pb,),
                               inc=(k == 3 * NH - 1))
                        k += 1
            b.emit("v", lambda E, pb=pb, ti=ti, slot=slot: E.tensor_add(
                x1g[slot].ap[:, 512 * ti:512 * (ti + 1)], pb.ap[:, 0:512],
                xg[slot].ap[:, 512 * ti:512 * (ti + 1)]),
                reads=(pb, xg[slot]), writes=(x1g[slot],))
            b.dma("y", x1_d, x1_d.ap[ti][:, 512 * g:512 * (g + 1)],
                  x1g[slot], x1g[slot].ap[:, 512 * ti:512 * (ti + 1)])

    b.phase = "p4"
    b.phase_barrier(p3_tts)
    p3.close()
    p4 = ExitStack()
    p4_tts = []

    def sb4(name, shape, dtype):
        t = TT(p4.enter_context(nc.sbuf_tensor(name, shape, dtype)).ap())
        p4_tts.append(t)
        return t

    T2 = 256
    NT2 = TQ // T2
    C1 = 1.0 / float(np.sqrt(float(D)))
    C2 = 1.0 / float(np.sqrt(float(FI)))

    x1t4 = [[sb4(f"x1t{p}_{t}", [128, D], F32) for t in range(2)] for p in range(2)]
    st8_b = [sb4(f"st8_{p}", [128, 10], F32) for p in range(2)]
    z1 = sb4("z1", [128, NDC * T2], F32)
    zt = sb4("zt", [128, NDC * T2], F32)
    xq1_b = [sb4(f"xq1_{p}", [128, NDC * T2], BF16) for p in range(2)]
    terng = [sb4(f"terng{i}", [128, NDC * 256], FP8) for i in range(2)]
    ternu = [sb4(f"ternu{i}", [128, NDC * 256], FP8) for i in range(2)]
    z2 = sb4("z2", [128, NIC * T2], F32)
    xnf = sb4("xnf", [128, 2 * NDC * T2], BF16)
    hst = [sb4(f"hst{i}", [128, 512], F32) for i in range(2)]
    hhi = [sb4(f"hhi{i}", [128, 512], BF16) for i in range(2)]
    hlo = [sb4(f"hlo{i}", [128, 512], BF16) for i in range(2)]
    u32 = [sb4(f"u32{i}", [128, 512], F32) for i in range(2)]
    qss1 = sb4("qss1", [128, 256], F32)
    qss2 = sb4("qss2", [128, 256], F32)
    dqs_b = [sb4(f"dqs_{p}", [128, 512], F32) for p in range(2)]
    qrow = sb4("qrow", [1, 1024], F32)
    wdb = [sb4(f"wdb{i}", [128, D], FP8) for i in range(2)]
    rtmp1 = sb4("rtmp1", [128, 8], F32)
    rtmpB = sb4("rtmpB", [128, 4], F32)
    st8b2 = sb4("st8b2", [128, 4], F32)

    class _View:
        __slots__ = ("ap",)

        def __init__(self, ap):
            self.ap = ap
    z2bf = z2.ap.bitcast(BF16)
    xnfm_hi = _View(xnf.ap[:, 0:NDC * T2])
    xnfm_lo = _View(xnf.ap[:, NDC * T2:2 * NDC * T2])
    ztbf = zt.ap.bitcast(BF16)
    zq1 = _View(zt.ap[:, 3584:4096])
    xn2lo = _View(ztbf[:, 4096:6144])
    xn2hi = _View(ztbf[:, 6144:8192])

    def butterfly_level(z, nchunk, hsz):
        zv = z.ap.rearrange("p (bq i q) -> p bq i q", bq=nchunk // (2 * hsz), i=2 * hsz)
        a = zv[:, :, 0:hsz, :]
        bb = zv[:, :, hsz:2 * hsz, :]
        b.emit("v", lambda E, a=a, bb=bb: E.tensor_add(a, a, bb), reads=(z,), writes=(z,))
        b.emit("v", lambda E, a=a, bb=bb: E.scalar_tensor_tensor(
            bb, bb, -2.0, a, ALU.mult, ALU.add), reads=(z,), writes=(z,))

    def h1_stage_fns(j):
        p = j % 2
        st8 = st8_b[p]
        xq1 = xq1_b[p]
        dqs = dqs_b[p]
        xn232 = _View(zt.ap[:, 0:D])
        fns = []

        def ln2_stats(stt):
            ti = 2 * j + stt
            xt = x1t4[p][stt]
            b.phase = "p4ln"
            b.dma("y", xt, xt.ap, x1_d, x1_d.ap[ti])
            b.emit("v", lambda E, xt=xt: E.reduce_sum(st8.ap[:, 0:1], xt.ap, AX),
                   reads=(xt,), writes=(st8,))
            b.emit("s", lambda E, xt=xt: E.activation(xn232.ap, xt.ap, AF.Square),
                   reads=(xt,), writes=(zt,))
            b.emit("v", lambda E: E.reduce_sum(st8.ap[:, 1:2], xn232.ap, AX),
                   reads=(zt,), writes=(st8,))
            b.emit("v", lambda E: E.tensor_scalar(st8.ap[:, 2:4], st8.ap[:, 0:2], 1.0 / D,
                                                  None, ALU.mult), reads=(st8,), writes=(st8,))
            b.emit("v", lambda E: E.tensor_scalar(st8.ap[:, 4:5], st8.ap[:, 2:3],
                                                  st8.ap[:, 2:3], None, ALU.mult),
                   reads=(st8,), writes=(st8,))
            b.emit("v", lambda E: E.tensor_sub(st8.ap[:, 5:6], st8.ap[:, 3:4], st8.ap[:, 4:5]),
                   reads=(st8,), writes=(st8,))
            b.emit("v", lambda E: E.tensor_scalar(st8.ap[:, 5:6], st8.ap[:, 5:6], EPS_LN,
                                                  None, ALU.add), reads=(st8,), writes=(st8,))
            b.emit("s", lambda E: E.activation(st8.ap[:, 5:6], st8.ap[:, 5:6], AF.Sqrt),
                   reads=(st8,), writes=(st8,))
            b.emit("v", lambda E: E.reciprocal(st8.ap[:, 6:7], st8.ap[:, 5:6]),
                   reads=(st8,), writes=(st8,))
            b.emit("v", lambda E: E.tensor_scalar(st8.ap[:, 7:8], st8.ap[:, 2:3],
                                                  st8.ap[:, 6:7], -1.0, ALU.mult, ALU.mult),
                   reads=(st8,), writes=(st8,))
            b.emit("v", lambda E, xt=xt: E.tensor_scalar(
                xn2hi.ap, xt.ap, st8.ap[:, 6:7], st8.ap[:, 7:8], ALU.mult, ALU.add),
                reads=(xt, st8), writes=(zt,))
            b.emit("v", lambda E, xt=xt: E.tensor_scalar(
                xn232.ap, xt.ap, st8.ap[:, 6:7], st8.ap[:, 7:8], ALU.mult, ALU.add),
                reads=(xt, st8), writes=(zt,))
            b.emit("v", lambda E: E.tensor_sub(xn2lo.ap, xn232.ap, xn2hi.ap),
                   reads=(zt,), writes=(zt,))

        def ln2_tr(stt, half, grp):
            srcT, dstT = ((xn2hi, xnfm_hi), (xn2lo, xnfm_lo))[half]
            b.phase = "p4ln"
            bk = banks[grp % 2]
            bkb = bk.ap.bitcast(BF16)
            for pp in range(8):
                cc = 8 * grp + pp
                b.emit("t", lambda E, bkb=bkb, pp=pp, srcT=srcT, cc=cc: E.transpose(
                    bkb[:, 128 * pp:128 * (pp + 1)], srcT.ap[:, 128 * cc:128 * (cc + 1)],
                    C["identb"].ap), reads=(zt, C["identb"]), writes=(bk,))

            def evac2(E, bkb=bkb, dstT=dstT, grp=grp, stt=stt):
                dst = dstT.ap.rearrange("p (c q) -> p c q", c=NDC)[
                    :, 8 * grp:8 * grp + 8, 128 * stt:128 * (stt + 1)]
                return E.tensor_copy(dst, bkb.rearrange("p (c q) -> p c q", c=8))
            b.emit("v", evac2, reads=(bk,), writes=(xnf,))

        def fw1(c4):
            b.phase = "p4fw1"
            for cc in range(4 * c4, 4 * c4 + 4):
                bk = banks[cc % 2]
                for pi, srcT in enumerate((xnfm_hi, xnfm_lo)):
                    b.emit("t", lambda E, srcT=srcT, cc=cc, bk=bk, st=(pi == 0),
                           sp=(pi == 1): E.matmul(
                        bk.ap[:, 0:T2], C["h128"].ap, srcT.ap[:, T2 * cc:T2 * (cc + 1)],
                        start=st, stop=sp), reads=(xnf, C["h128"]), writes=(bk,), inc=(pi == 1))
                b.emit("s", lambda E, cc=cc, bk=bk: E.activation(
                    z1.ap[:, T2 * cc:T2 * (cc + 1)], bk.ap[:, 0:T2], AF.Copy),
                    reads=(bk,), writes=(z1,))

        def bf1(lev):
            b.phase = "p4fw1"
            butterfly_level(z1, NDC, NDC // (2 ** (lev + 1)))

        def q1max():
            b.phase = "p4q1"
            b.emit("s", lambda E: E.activation(zt.ap, z1.ap, AF.Abs), reads=(z1,), writes=(zt,))
            cv = zt.ap.rearrange("p (c q) -> p c q", c=NDC)
            b.emit("v", lambda E, cv=cv: E.tensor_tensor(
                cv[:, 0:8, :], cv[:, 0:8, :], cv[:, 8:16, :], ALU.max),
                reads=(zt,), writes=(zt,))
            for nn in (4, 2, 1):
                b.emit("v", lambda E, nn=nn, cv=cv: E.tensor_tensor(
                    cv[:, 0:nn, :], cv[:, 0:nn, :], cv[:, nn:2 * nn, :], ALU.max),
                    reads=(zt,), writes=(zt,))
            for pp in range(2):
                b.emit("t", lambda E, pp=pp: E.transpose(
                    banks[0].ap[:, 128 * pp:128 * (pp + 1)], zt.ap[:, 128 * pp:128 * (pp + 1)],
                    C["identf"].ap), reads=(zt, C["identf"]), writes=(banks[0],))
            b.emit("v", lambda E: E.tensor_copy(zq1.ap[:, 256:512], banks[0].ap[:, 0:256]),
                   reads=(banks[0],), writes=(zt,))
            b.emit("v", lambda E: E.reduce_max(st8.ap[:, 8:9], zq1.ap[:, 256:384], AX),
                   reads=(zt,), writes=(st8,))
            b.emit("v", lambda E: E.reduce_max(st8.ap[:, 9:10], zq1.ap[:, 384:512], AX),
                   reads=(zt,), writes=(st8,))

        def q1scales():
            b.phase = "p4q1"
            b.emit("v", lambda E: E.tensor_scalar(st8.ap[:, 8:10], st8.ap[:, 8:10], C1, EPS_Q,
                                                  ALU.mult, ALU.max), reads=(st8,), writes=(st8,))
            b.emit("v", lambda E: E.reciprocal(rtmp1.ap[:, 0:2], st8.ap[:, 8:10]),
                   reads=(st8,), writes=(rtmp1,))
            b.emit("v", lambda E: E.tensor_scalar(rtmp1.ap[:, 0:2], rtmp1.ap[:, 0:2], 127.0 * C1,
                                                  None, ALU.mult), reads=(rtmp1,), writes=(rtmp1,))
            b.emit("v", lambda E: E.tensor_scalar(rtmp1.ap[:, 2:3], st8.ap[:, 8:9],
                                                  shalfB.ap[:, 0:1], 2.0 / 127.0, ALU.mult, ALU.mult),
                   reads=(st8, shalfB), writes=(rtmp1,))
            b.emit("v", lambda E: E.tensor_scalar(rtmp1.ap[:, 3:4], st8.ap[:, 9:10],
                                                  shalfB.ap[:, 0:1], 2.0 / 127.0, ALU.mult, ALU.mult),
                   reads=(st8, shalfB), writes=(rtmp1,))
            b.emit("v", lambda E: E.tensor_scalar(rtmp1.ap[:, 4:5], st8.ap[:, 8:9],
                                                  shalfB.ap[:, 1:2], 2.0 / 127.0, ALU.mult, ALU.mult),
                   reads=(st8, shalfB), writes=(rtmp1,))
            b.emit("v", lambda E: E.tensor_scalar(rtmp1.ap[:, 5:6], st8.ap[:, 9:10],
                                                  shalfB.ap[:, 1:2], 2.0 / 127.0, ALU.mult, ALU.mult),
                   reads=(st8, shalfB), writes=(rtmp1,))

        def q1rows():
            b.phase = "p4q1"
            for pp in range(4):
                b.emit("t", lambda E, pp=pp: E.transpose(
                    banks[0].ap[0:1, 128 * pp:128 * (pp + 1)], rtmp1.ap[:, pp:pp + 1],
                    C["identf"].ap), reads=(rtmp1, C["identf"]), writes=(banks[0],))
            b.emit("v", lambda E: E.tensor_copy(qrow.ap[0:1, 0:512], banks[0].ap[0:1, 0:512]),
                   reads=(banks[0],), writes=(qrow,))
            for pp in range(2):
                b.emit("t", lambda E, pp=pp: E.transpose(
                    banks[0].ap[0:1, 128 * pp:128 * (pp + 1)], rtmp1.ap[:, 4 + pp:5 + pp],
                    C["identf"].ap), reads=(rtmp1, C["identf"]), writes=(banks[0],))
            b.emit("v", lambda E: E.tensor_copy(qrow.ap[0:1, 512:768], banks[0].ap[0:1, 0:256]),
                   reads=(banks[0],), writes=(qrow,))
            b.emit("t", lambda E: E.matmul(banks[1].ap[:, 0:256], C["onesfr"].ap,
                                           qrow.ap[0:1, 0:256], start=True, stop=True),
                   reads=(C["onesfr"], qrow), writes=(banks[1],))
            b.emit("v", lambda E: E.tensor_copy(qss1.ap[:, 0:256], banks[1].ap[:, 0:256]),
                   reads=(banks[1],), writes=(qss1,))
            b.emit("t", lambda E: E.matmul(banks[1].ap[:, 0:512], C["onesfr"].ap,
                                           qrow.ap[0:1, 256:768], start=True, stop=True),
                   reads=(C["onesfr"], qrow), writes=(banks[1],))
            b.emit("v", lambda E, dqs=dqs: E.tensor_copy(dqs.ap[:, 0:512], banks[1].ap[:, 0:512]),
                   reads=(banks[1],), writes=(dqs,))

        def q1big(hf):
            b.phase = "p4q1"
            HC = NDC // 2
            sl = slice(hf * HC * T2, (hf + 1) * HC * T2)
            b.emit("v", lambda E, sl=sl: E.tensor_tensor(
                zt.ap[:, sl].rearrange("p (c q) -> p c q", c=HC),
                z1.ap[:, sl].rearrange("p (c q) -> p c q", c=HC),
                qss1.ap[:, 0:256].unsqueeze(1).broadcast_to([128, HC, 256]), ALU.mult),
                reads=(z1, qss1), writes=(zt,))
            b.emit("v", lambda E, sl=sl: E.tensor_scalar(zt.ap[:, sl], zt.ap[:, sl], C32, C32,
                                                         ALU.add, ALU.subtract),
                   reads=(zt,), writes=(zt,))
            b.emit("v", lambda E, sl=sl: E.tensor_scalar(z1.ap[:, sl], zt.ap[:, sl], C52, C52,
                                                         ALU.add, ALU.subtract),
                   reads=(zt,), writes=(z1,))
            b.emit("v", lambda E, sl=sl, xq1=xq1: E.scalar_tensor_tensor(
                xq1.ap[:, sl], zt.ap[:, sl], blend.ap[:, 1:2], z1.ap[:, sl],
                ALU.mult, ALU.add), reads=(zt, z1, blend), writes=(xq1,))

        for stt in range(2):
            fns.append(lambda stt=stt: ln2_stats(stt))
            for half in range(2):
                for grp in range(2):
                    fns.append(lambda stt=stt, half=half, grp=grp: ln2_tr(stt, half, grp))
        for c4 in range(4):
            fns.append(lambda c4=c4: fw1(c4))
        for lev in range(4):
            fns.append(lambda lev=lev: bf1(lev))
        fns.append(q1max)
        fns.append(q1scales)
        fns.append(q1rows)
        fns.append(lambda: q1big(0))
        fns.append(lambda: q1big(1))
        return fns

    def h2_emit(j, drain):
        p = j % 2
        xq1 = xq1_b[p]
        dqs = dqs_b[p]
        zqa = _View(xq1.ap.bitcast(F32)[:, 0:1024])
        zqb = _View(xq1.ap.bitcast(F32)[:, 1024:2048])
        b.phase = "p4gu"

        def tern_load(jj, hs):
            tg, tu = terng[hs % 2], ternu[hs % 2]
            r = (2 * hs) // (WGS // 128)
            col0 = (2 * hs) % (WGS // 128) * 128
            for m, dstT in ((0, tg), (1, tu)):
                tsrc = (tern_outg, tern_outu)[m]
                srcflat = tsrc.ap[r * GPART:(r + 1) * GPART]
                srcv = srcflat.rearrange("(c p o) -> c p o", c=NDC, p=128)[
                    :, :, col0:col0 + 256].transpose([1, 0, 2])
                b.dma(("s", "g")[m], dstT, dstT.ap.rearrange("p (c f) -> p c f", c=NDC),
                      tsrc, srcv)

        def stage_a(hs):
            par = hs % 2
            hhip, hlop = hhi[par], hlo[par]
            hbk = banks[4] if par == 0 else banks[7]
            for pi, srcT in enumerate((hhip, hlop)):
                b.emit("t", lambda E, srcT=srcT, st=(pi == 0), sp=(pi == 1), hbk=hbk:
                       E.matmul(hbk.ap[:, 0:512], C["h128"].ap, srcT.ap, start=st,
                                stop=sp), reads=(srcT, C["h128"]), writes=(hbk,),
                       inc=(pi == 1))
            b.emit("s", lambda E, hs=hs, hbk=hbk: E.activation(
                z2.ap[:, 512 * hs:512 * (hs + 1)], hbk.ap[:, 0:512], AF.Copy),
                reads=(hbk,), writes=(z2,))

        if j == 0:
            tern_load(j, 0)
            tern_load(j, 1)
        for hs in range(NIC // 2):
            tg, tu = terng[hs % 2], ternu[hs % 2]
            par = hs % 2
            gb, ub = (banks[2], banks[3]) if par == 0 else (banks[5], banks[6])
            hstp, u32p, hhip, hlop = hst[par], u32[par], hhi[par], hlo[par]
            for m, (wt, pb) in enumerate(((tg, gb), (tu, ub))):
                for dtl in range(2):
                    for cc in range(NDC):
                        b.emit("t", lambda E, wt=wt, pb=pb, cc=cc, dtl=dtl,
                               st=(cc == 0),
                               sp=(cc == NDC - 1): E.matmul(
                                   pb.ap[:, 256 * dtl:256 * (dtl + 1)],
                                   wt.ap[:, 256 * cc + 128 * dtl:256 * cc + 128 * (dtl + 1)],
                                   xq1.ap[:, T2 * cc:T2 * (cc + 1)], start=st, stop=sp),
                               reads=(wt, xq1), writes=(pb,),
                               inc=(dtl == 1 and cc == NDC - 1))
            if hs + 2 < NIC // 2:
                tern_load(j, hs + 2)
            if hs > 0:
                stage_a(hs - 1)
            dqv = dqs.ap[:, 0:256].unsqueeze(1).broadcast_to([128, 2, 256])
            dqv_u = dqs.ap[:, 256:512].unsqueeze(1).broadcast_to([128, 2, 256])
            b.emit("v", lambda E, gb=gb, hstp=hstp, dqv=dqv, dqs=dqs: E.tensor_tensor(
                hstp.ap.rearrange("p (c q) -> p c q", c=2),
                gb.ap[:, 0:512].rearrange("p (c q) -> p c q", c=2), dqv, ALU.mult),
                reads=(gb, dqs), writes=(hstp,))
            b.emit("s", lambda E, hstp=hstp: E.activation(hstp.ap, hstp.ap, AF.Silu),
                   reads=(hstp,), writes=(hstp,))
            b.emit("v", lambda E, ub=ub, u32p=u32p, dqv_u=dqv_u, dqs=dqs: E.tensor_tensor(
                u32p.ap.rearrange("p (c q) -> p c q", c=2),
                ub.ap[:, 0:512].rearrange("p (c q) -> p c q", c=2), dqv_u, ALU.mult),
                reads=(ub, dqs), writes=(u32p,))
            b.emit("v", lambda E, hstp=hstp, u32p=u32p: E.tensor_mul(
                hstp.ap, hstp.ap, u32p.ap), reads=(hstp, u32p), writes=(hstp,))
            b.emit("v", lambda E, hstp=hstp, hhip=hhip: E.tensor_copy(hhip.ap, hstp.ap),
                   reads=(hstp,), writes=(hhip,))
            b.emit("v", lambda E, hstp=hstp, hhip=hhip, hlop=hlop: E.tensor_sub(
                hlop.ap, hstp.ap, hhip.ap), reads=(hstp, hhip), writes=(hlop,))
            drain(1)
        stage_a(NIC // 2 - 1)
        if j + 1 < NT2:
            tern_load(j + 1, 0)
            tern_load(j + 1, 1)
        drain(-1)
        b.phase = "p4fw2"
        for lev in range(6):
            butterfly_level(z2, NIC, NIC // (2 ** (lev + 1)))
        b.phase = "p4q2"
        for seg in range(16):
            src = z2.ap[:, 1024 * seg:1024 * (seg + 1)]
            if seg == 0:
                b.emit("s", lambda E, src=src: E.activation(zqa.ap, src, AF.Abs),
                       reads=(z2,), writes=(xq1,))
            else:
                b.emit("s", lambda E, src=src: E.activation(zqb.ap, src, AF.Abs),
                       reads=(z2,), writes=(xq1,))
                b.emit("v", lambda E: E.tensor_tensor(zqa.ap, zqa.ap, zqb.ap, ALU.max),
                       reads=(xq1,), writes=(xq1,))
        b.emit("v", lambda E: E.tensor_tensor(zqa.ap[:, 0:512], zqa.ap[:, 0:512],
                                              zqa.ap[:, 512:1024], ALU.max),
               reads=(xq1,), writes=(xq1,))
        b.emit("v", lambda E: E.tensor_tensor(zqa.ap[:, 0:256], zqa.ap[:, 0:256],
                                              zqa.ap[:, 256:512], ALU.max),
               reads=(xq1,), writes=(xq1,))
        for pp in range(2):
            b.emit("t", lambda E, pp=pp: E.transpose(
                banks[2].ap[:, 128 * pp:128 * (pp + 1)], zqa.ap[:, 128 * pp:128 * (pp + 1)],
                C["identf"].ap), reads=(xq1, C["identf"]), writes=(banks[2],))
        b.emit("v", lambda E: E.tensor_copy(zqb.ap[:, 0:256], banks[2].ap[:, 0:256]),
               reads=(banks[2],), writes=(xq1,))
        b.emit("v", lambda E: E.reduce_max(st8b2.ap[:, 0:1], zqb.ap[:, 0:128], AX),
               reads=(xq1,), writes=(st8b2,))
        b.emit("v", lambda E: E.reduce_max(st8b2.ap[:, 1:2], zqb.ap[:, 128:256], AX),
               reads=(xq1,), writes=(st8b2,))
        b.emit("v", lambda E: E.tensor_scalar(st8b2.ap[:, 0:2], st8b2.ap[:, 0:2], C2, EPS_Q,
                                              ALU.mult, ALU.max), reads=(st8b2,), writes=(st8b2,))
        b.emit("v", lambda E: E.reciprocal(rtmpB.ap[:, 0:2], st8b2.ap[:, 0:2]),
               reads=(st8b2,), writes=(rtmpB,))
        b.emit("v", lambda E: E.tensor_scalar(rtmpB.ap[:, 0:2], rtmpB.ap[:, 0:2], 127.0 * C2,
                                              None, ALU.mult), reads=(rtmpB,), writes=(rtmpB,))
        b.emit("v", lambda E: E.tensor_scalar(st8b2.ap[:, 0:2], st8b2.ap[:, 0:2],
                                              shalfB.ap[:, 2:3], 2.0 / 127.0, ALU.mult, ALU.mult),
               reads=(st8b2, shalfB), writes=(st8b2,))
        for pp in range(2):
            b.emit("t", lambda E, pp=pp: E.transpose(
                banks[2].ap[0:1, 128 * pp:128 * (pp + 1)], rtmpB.ap[:, pp:pp + 1],
                C["identf"].ap), reads=(rtmpB, C["identf"]), writes=(banks[2],))
        b.emit("v", lambda E: E.tensor_copy(qrow.ap[0:1, 768:1024], banks[2].ap[0:1, 0:256]),
               reads=(banks[2],), writes=(qrow,))
        b.emit("t", lambda E: E.matmul(banks[3].ap[:, 0:256], C["onesfr"].ap,
                                       qrow.ap[0:1, 768:1024], start=True, stop=True),
               reads=(C["onesfr"], qrow), writes=(banks[3],))
        b.emit("v", lambda E: E.tensor_copy(qss2.ap[:, 0:256], banks[3].ap[:, 0:256]),
               reads=(banks[3],), writes=(qss2,))
        b.phase = "p4dn"

        def down_cc(cc):
            wb_ = wdb[cc % 2]
            rr = cc // (WDS // 128)
            blk = cc % (WDS // 128)
            srcw = tern_outd.ap[rr * DPART + blk * 128 * D:
                                rr * DPART + (blk + 1) * 128 * D]
            b.dma(("s", "g", "y")[cc % 3], wb_, wb_.ap, tern_outd,
                  srcw.rearrange("(p o) -> p o", p=128))
            kk = 0
            for th in range(2):
                for g in range(4):
                    pb = banks[th * 4 + g]
                    xap = z2bf[:, T2 * cc + 128 * th:T2 * cc + 128 * (th + 1)]
                    b.emit("t", lambda E, pb=pb, g=g, wb_=wb_, xap=xap,
                           st=(cc == 0), sp=(cc == NIC - 1): E.matmul(
                               pb.ap[:, 0:512], xap,
                               wb_.ap[:, 512 * g:512 * (g + 1)], start=st, stop=sp),
                           reads=(z2, wb_), writes=(pb,), inc=(kk == 7))
                    kk += 1

        for seg in range(16):
            sl = slice(1024 * seg, 1024 * (seg + 1))
            b.emit("v", lambda E, sl=sl: E.tensor_tensor(
                zqa.ap.rearrange("p (c q) -> p c q", c=4),
                z2.ap[:, sl].rearrange("p (c q) -> p c q", c=4),
                qss2.ap[:, 0:256].unsqueeze(1).broadcast_to([128, 4, 256]), ALU.mult),
                reads=(z2, qss2), writes=(xq1,))
            b.emit("v", lambda E: E.tensor_scalar(zqa.ap, zqa.ap, C32, C32,
                                                  ALU.add, ALU.subtract),
                   reads=(xq1,), writes=(xq1,))
            b.emit("v", lambda E: E.tensor_scalar(zqb.ap, zqa.ap, C52, C52,
                                                  ALU.add, ALU.subtract),
                   reads=(xq1,), writes=(xq1,))
            b.emit("v", lambda E, seg=seg: E.scalar_tensor_tensor(
                z2bf[:, 1024 * seg:1024 * (seg + 1)], zqa.ap, blend.ap[:, 1:2],
                zqb.ap, ALU.mult, ALU.add),
                reads=(xq1, blend), writes=(z2,))
            for cc in range(4 * seg, 4 * seg + 4):
                down_cc(cc)
        for th in range(2):
            for g in range(4):
                pb = banks[th * 4 + g]
                xt = x1t4[p][th]
                b.emit("v", lambda E, pb=pb, th=th, g=g, xt=xt: E.scalar_tensor_tensor(
                    xt.ap[:, 512 * g:512 * (g + 1)], pb.ap[:, 0:512],
                    st8b2.ap[:, th:th + 1], xt.ap[:, 512 * g:512 * (g + 1)],
                    ALU.mult, ALU.add), reads=(pb, st8b2, xt), writes=(xt,))
            ti = 2 * j + th
            b.dma("y", out_d, out_d.ap[128 * ti:128 * (ti + 1), :],
                  x1t4[p][th], x1t4[p][th].ap)

    pend = h1_stage_fns(0)
    while pend:
        pend.pop(0)()
    for j in range(NT2):
        pend = h1_stage_fns(j + 1) if j + 1 < NT2 else []

        def drain(k, pend=pend):
            if k < 0:
                while pend:
                    pend.pop(0)()
            else:
                for _ in range(k):
                    if pend:
                        pend.pop(0)()
        h2_emit(j, drain)


    tap_outs = {}
    if "kv_in" in taps:
        tap_outs["t_kv"] = dram("t_kv", [NC, KVP], BF16, "ExternalOutput")
        b.dma("y", tap_outs["t_kv"], tap_outs["t_kv"].ap[:, :], kv_in, kv_in.ap[:, :])
    if "x_tm" in taps:
        tap_outs["t_x"] = dram("t_x", [NXT, 128, D], F32, "ExternalOutput")
        b.dma("y", tap_outs["t_x"], tap_outs["t_x"].ap[:], x_tm_d, x_tm_d.ap[:])
    if "svals" in taps:
        tap_outs["t_sv"] = dram("t_sv", [1, 3], F32, "ExternalOutput")
        b.emit("y", lambda E: E.dma_start(out=tap_outs["t_sv"].ap, in_=svals.ap[0:1, 4:7]),
               reads=(svals,), writes=(tap_outs["t_sv"],), dma=True)
    if "q" in taps:
        tap_outs["t_q"] = dram("t_q", [NC, QOP], BF16, "ExternalOutput")
        b.dma("y", tap_outs["t_q"], tap_outs["t_q"].ap[:, :], q_aout, q_aout.ap[:, :])
    if "oa" in taps:
        tap_outs["t_oa"] = dram("t_oa", [2, NC, OHP], BF16, "ExternalOutput")
        b.dma("y", tap_outs["t_oa"], tap_outs["t_oa"].ap[0], o_aout0, o_aout0.ap[:, :])
        b.dma("y", tap_outs["t_oa"], tap_outs["t_oa"].ap[1], o_aout1, o_aout1.ap[:, :])
    if "x1" in taps:
        tap_outs["t_x1"] = dram("t_x1", [NXT, 128, D], F32, "ExternalOutput")
        b.dma("y", tap_outs["t_x1"], tap_outs["t_x1"].ap[:], x1_d, x1_d.ap[:])
    if "tern" in taps:
        tap_outs["t_tern"] = dram("t_tern", [TERN_STRIDE], FP8, "ExternalOutput")
        b.dma("y", tap_outs["t_tern"], tap_outs["t_tern"].ap[:], tern_in, tern_in.ap[:])

    for q in ("DY", "DS", "DG"):
        for i in range(NQSEM):
            b.wait("y", (f"{q}{i}", b.cnt[f"{q}{i}"]))
    for e in ("G", "V", "S", "T"):
        b.wait("y", (e, b.cnt[e]))

    b.finalize()
    return nc


def kernel(L0_attn_x, L0_attn_Wq, L0_attn_Wk, L0_attn_Wv, L0_attn_Wo,
           L0_attn_gq, L0_attn_gk, L0_ffn_w_gate, L0_ffn_w_up, L0_ffn_w_down,
           _taps=(), _return_raw=False):
    x = np.ascontiguousarray(np.asarray(L0_attn_x, dtype=np.float32).reshape(S, D))
    nc = build_program(taps=_taps)
    in_maps = []
    for c in range(NC):
        lo = c * CH
        hi = (15 - c) * CH
        x_own = np.ascontiguousarray(np.concatenate([x[lo:lo + CH], x[hi:hi + CH]], axis=0))
        in_maps.append({
            "x_own": x_own,
            "Wq": np.ascontiguousarray(L0_attn_Wq, dtype=np.float32),
            "Wk": np.ascontiguousarray(L0_attn_Wk, dtype=np.float32),
            "Wv": np.ascontiguousarray(L0_attn_Wv, dtype=np.float32),
            "Wo": np.ascontiguousarray(L0_attn_Wo, dtype=np.float32),
            "gq": np.ascontiguousarray(np.asarray(L0_attn_gq, dtype=np.float32).reshape(1, DH)),
            "gk": np.ascontiguousarray(np.asarray(L0_attn_gk, dtype=np.float32).reshape(1, DH)),
            "wg_sl": np.ascontiguousarray(np.asarray(L0_ffn_w_gate, dtype=np.float32)[c * WGS:(c + 1) * WGS, :]),
            "wu_sl": np.ascontiguousarray(np.asarray(L0_ffn_w_up, dtype=np.float32)[c * WGS:(c + 1) * WGS, :]),
            "wd_sl": np.ascontiguousarray(np.asarray(L0_ffn_w_down, dtype=np.float32)[:, c * WDS:(c + 1) * WDS]),
        })
    res = run_bass_kernel_spmd(nc, in_maps, core_ids=list(range(NC)))
    if _return_raw:
        return res
    full = np.zeros((S, D), dtype=np.float32)
    for c in range(NC):
        o = res.results[c]["out"]
        full[c * CH:(c + 1) * CH] = o[:CH]
        full[(15 - c) * CH:(16 - c) * CH] = o[CH:]
    return full.reshape(1, S, D)


# version 27
# speedup vs baseline: 1.2070x; 1.0069x over previous
from contextlib import ExitStack

import numpy as np
import ml_dtypes

import concourse.bass as bass
import concourse.mybir as mybir
from concourse.bass_utils import run_bass_kernel_spmd
from concourse.aot_env import allow_ici_all_to_all

F32 = mybir.dt.float32
FP8 = mybir.dt.float8e4
BF16 = mybir.dt.bfloat16
ALU = mybir.AluOpType
AF = mybir.ActivationFunctionType
AX = mybir.AxisListType.X

C32 = float(np.float32(1.5 * 2.0 ** 23))
C52 = float(np.float32(1.5 * 2.0 ** 52))

S = 8192
D = 2048
NH = 16
NKV = 4
DH = 128
DKV = NKV * DH
FI = 8192
NC = 8
EPS_LN = 1e-5
EPS_RMS = 1e-6
EPS_Q = 1e-5

CH = S // 16
TQ = 2 * CH
NXT = TQ // 128
NKT = S // 128
NDC = D // 128
NIC = FI // 128
WGS = FI // NC
WDS = FI // NC

STOP_AFTER = None
ENGS = ("y", "t", "v", "s", "g")
DMAQ = {"y": "DY", "s": "DS", "g": "DG"}
ESEM = {"t": "T", "v": "V", "s": "S", "g": "G", "y": "Y"}


class TT:
    __slots__ = ("ap", "w", "r")

    def __init__(self, ap):
        self.ap = ap
        self.w = {}
        self.r = {}

    def __getitem__(self, k):
        return self.ap[k]


NQSEM = 8


class Bld:
    def __init__(self, nc):
        self.nc = nc
        self.phase = "init"
        self.pend = {e: [] for e in ENGS}
        self.ops = {e: [] for e in ENGS}
        self.semh = {}
        self.cnt = {}
        for nm in ["T", "V", "S", "G", "Y"] + [f"{q}{i}" for q in ("DY", "DS", "DG")
                                              for i in range(NQSEM)]:
            self.semh[nm] = nc.alloc_semaphore(name="sem" + nm)
            self.cnt[nm] = 0
        self.qidx = {"DY": 0, "DS": 0, "DG": 0}
        self.ncc = 0
        self.hwm = {e: {} for e in ENGS}

    def wait(self, eng, ev):
        if ev is None:
            return
        sem, val = ev
        if self.hwm[eng].get(sem, 0) >= val:
            return
        self.hwm[eng][sem] = val
        h = self.semh[sem]
        self.ops[eng].append(lambda E, h=h, val=val: E.wait_ge(h, val))

    def waits(self, eng, evs):
        best = {}
        for ev in evs:
            if ev is None:
                continue
            sem, val = ev
            if best.get(sem, -1) < val:
                best[sem] = val
        for sem, val in best.items():
            self.wait(eng, (sem, val))

    def raw(self, eng, fn, inc):
        step = 16 if inc.startswith("D") else 1
        self.cnt[inc] += step
        val = self.cnt[inc]
        h = self.semh[inc]
        ph = self.phase

        def f(E, fn=fn, h=h, step=step, ph=ph, nc=self.nc):
            nc.push_named_scope(ph)
            try:
                fn(E).then_inc(h, step)
            finally:
                nc.pop_named_scope(ph)
        self.ops[eng].append(f)
        return (inc, val)

    def emit(self, eng, fn, reads=(), writes=(), dma=False, sem=None, inc=True):
        evs = []
        for t in reads:
            if t is not None:
                evs.extend(t.w.items())
        for t in writes:
            if t is not None:
                evs.extend(t.w.items())
                evs.extend(t.r.items())
        self.waits(eng, evs)
        if not inc:
            ph = self.phase

            def f(E, fn=fn, ph=ph, nc=self.nc):
                nc.push_named_scope(ph)
                try:
                    fn(E)
                finally:
                    nc.pop_named_scope(ph)
            self.ops[eng].append(f)
            for t in reads:
                if t is not None:
                    self.pend[eng].append((t, False))
            for t in writes:
                if t is not None:
                    self.pend[eng].append((t, True))
            return None
        if sem is None:
            if dma:
                q = DMAQ[eng]
                sem = f"{q}{self.qidx[q]}"
                self.qidx[q] = (self.qidx[q] + 1) % NQSEM
                self.wait(eng, (sem, self.cnt[sem]))
            else:
                sem = ESEM[eng]
        ev = self.raw(eng, fn, sem)
        s, v = ev
        for t in reads:
            if t is not None and t.r.get(s, 0) < v:
                t.r[s] = v
        for t in writes:
            if t is not None:
                t.w = {s: v}
                t.r = {}
        if self.pend[eng] and not sem.startswith("D") and sem in ESEM.values():
            for t, is_w in self.pend[eng]:
                if is_w:
                    t.w[s] = v
                else:
                    if t.r.get(s, 0) < v:
                        t.r[s] = v
            self.pend[eng] = []
        return ev

    def phase_barrier(self, tts):
        evs = []
        for t in tts:
            evs.extend(t.w.items())
            evs.extend(t.r.items())
        for e in ENGS:
            self.waits(e, evs)

    def collective(self, kind, op, ins_ap, outs_ap, reads=(), writes=()):
        nm = f"CC{self.ncc}"
        self.ncc += 1
        self.semh[nm] = self.nc.alloc_semaphore(name="sem" + nm)
        self.cnt[nm] = 0
        return self.emit("g", lambda E: E.collective_compute(
            kind, op, replica_groups=[list(range(NC))], ins=[ins_ap], outs=[outs_ap]),
            reads=reads, writes=writes, sem=nm)

    def dma(self, eng, dst, dst_ap, src, src_ap):
        return self.emit(eng, lambda E, o=dst_ap, i=src_ap: E.dma_start(out=o, in_=i),
                         reads=(src,), writes=(dst,), dma=True)

    def finalize(self):
        with allow_ici_all_to_all(), self.nc.Block() as block:
            for ename, attr in (("y", "sync"), ("t", "tensor"), ("v", "vector"),
                                ("s", "scalar"), ("g", "gpsimd")):
                def body(E, ops=self.ops[ename]):
                    for f in ops:
                        f(E)
                getattr(block, attr)(body)


def build_program(taps=()):
    nc = bass.Bass("TRN2", target_bir_lowering=False)
    b = Bld(nc)
    ctx = ExitStack()

    def sb(name, shape, dtype):
        return TT(ctx.enter_context(nc.sbuf_tensor(name, shape, dtype)).ap())

    def dram(name, shape, dtype, kind="Internal"):
        return TT(nc.dram_tensor(name, shape, dtype, kind=kind).ap())

    x_in = dram("x_own", [TQ, D], F32, "ExternalInput")
    wq_in = dram("Wq", [D, D], F32, "ExternalInput")
    wk_in = dram("Wk", [D, DKV], F32, "ExternalInput")
    wv_in = dram("Wv", [D, DKV], F32, "ExternalInput")
    wo_in = dram("Wo", [D, D], F32, "ExternalInput")
    gq_in = dram("gq", [1, DH], F32, "ExternalInput")
    gk_in = dram("gk", [1, DH], F32, "ExternalInput")
    wg_in = dram("wg_sl", [WGS, D], F32, "ExternalInput")
    wu_in = dram("wu_sl", [WGS, D], F32, "ExternalInput")
    wd_in = dram("wd_sl", [D, WDS], F32, "ExternalInput")
    out_d = dram("out", [TQ, D], F32, "ExternalOutput")

    KVP = 4 * DH * TQ
    kv_in = dram("kv_in", [NC, KVP], BF16)
    kv_out = dram("kv_out", [NC, KVP], BF16)
    QOP = 4 * DH * TQ
    q_ain = dram("q_ain", [NC, QOP], BF16)
    q_aout = dram("q_aout", [NC, QOP], BF16)
    OHP = 2 * DH * TQ
    o_ain0 = dram("o_ain0", [NC, OHP], BF16)
    o_aout0 = dram("o_aout0", [NC, OHP], BF16)
    o_ain1 = dram("o_ain1", [NC, OHP], BF16)
    o_aout1 = dram("o_aout1", [NC, OHP], BF16)
    wo_split_d = dram("wo_split_d", [2, NDC, 128, D], BF16)

    GPART = D * WGS
    DPART = WDS * D
    tern_in = dram("tern_in", [2 * GPART + DPART], FP8)
    tern_outg = dram("tern_outg", [NC * GPART], FP8)
    tern_outu = dram("tern_outu", [NC * GPART], FP8)
    tern_outd = dram("tern_outd", [NC * DPART], FP8)
    TERN_STRIDE = 2 * GPART + DPART

    ar_in = dram("ar_in", [1, 3], F32)
    ar_out = dram("ar_out", [1, 3], F32)

    x_tm_d = dram("x_tm_d", [NXT, 128, D], F32)
    x1_d = dram("x1_d", [NXT, 128, D], F32)

    Hm = np.ones((1, 1), dtype=np.float64)
    while Hm.shape[0] < 128:
        Hm = np.block([[Hm, Hm], [Hm, -Hm]])
    consts = {
        "h128": (np.asarray(Hm, dtype=ml_dtypes.bfloat16), BF16, [128, 128]),
        "identb": (np.eye(128, dtype=ml_dtypes.bfloat16), BF16, [128, 128]),
        "identf": (np.eye(128, dtype=np.float32), F32, [128, 128]),
        "onesb": (np.ones((128, 1), dtype=ml_dtypes.bfloat16), BF16, [128, 1]),
        "onesfr": (np.ones((1, 128), dtype=np.float32), F32, [1, 128]),
        "onesfc": (np.ones((128, 1), dtype=np.float32), F32, [128, 1]),
        "tri": (np.tril(np.ones((128, 128), dtype=np.float32)).T.copy(), F32, [128, 128]),
    }
    C = {}
    for nm, (arr, dt, shape) in consts.items():
        h = nc.inline_tensor(np.ascontiguousarray(arr), name=nm + "_c")
        t = sb(nm, shape, dt)
        b.dma("y", t, t.ap[:, :], TT(h.ap()), h.ap())
        C[nm] = t

    gq_r = sb("gq_r", [1, 128], F32)
    gk_r = sb("gk_r", [1, 128], F32)
    gq_c = sb("gq_c", [128, 1], F32)
    gk_c = sb("gk_c", [128, 1], F32)
    b.dma("y", gq_r, gq_r.ap[:, :], gq_in, gq_in.ap)
    b.dma("y", gk_r, gk_r.ap[:, :], gk_in, gk_in.ap)
    b.dma("y", gq_c, gq_c.ap[:, :], gq_in, gq_in.ap.rearrange("a d -> d a"))
    b.dma("y", gk_c, gk_c.ap[:, :], gk_in, gk_in.ap.rearrange("a d -> d a"))

    banks = [TT(ctx.enter_context(nc.psum_tensor(f"bank{i}", [128, 512], F32)).ap())
             for i in range(8)]

    blend = sb("blend", [128, 2], F32)
    mtmp = sb("mtmp", [1, 8], F32)
    ebias = sb("ebias", [128, 1], F32)
    gabs = sb("gabs", [1, 256], F32)
    svals = sb("svals", [1, 8], F32)
    shalfB = sb("shalfB", [128, 8], F32)
    b.emit("v", lambda E: E.memset(blend.ap[:, 0:1], 1.0), writes=(blend,))
    b.emit("v", lambda E: E.tensor_scalar(blend.ap[:, 0:1], blend.ap[:, 0:1], C52, C52,
                                          ALU.add, ALU.subtract), reads=(blend,), writes=(blend,))
    b.emit("v", lambda E: E.tensor_scalar(blend.ap[:, 1:2], blend.ap[:, 0:1], -1.0, 1.0,
                                          ALU.mult, ALU.add), reads=(blend,), writes=(blend,))

    def emit_rne(eng, out_t, out_ap, in_t, in_ap, scale1, t1, t2):
        sc_reads = (in_t,)
        if isinstance(scale1, tuple):
            sc_t, sc_ap = scale1
            sc_reads = (in_t, sc_t)
        else:
            sc_ap = scale1
        b.emit(eng, lambda E: E.tensor_scalar(t1.ap, in_ap, sc_ap, C32, ALU.mult, ALU.add),
               reads=sc_reads, writes=(t1,))
        b.emit(eng, lambda E: E.tensor_scalar(t2.ap, t1.ap, C32, C52, ALU.subtract, ALU.add),
               reads=(t1,), writes=(t2,))
        b.emit(eng, lambda E: E.tensor_scalar(t2.ap, t2.ap, C52, None, ALU.subtract),
               reads=(t2,), writes=(t2,))
        b.emit(eng, lambda E: E.tensor_scalar(t1.ap, t1.ap, C32, None, ALU.subtract),
               reads=(t1,), writes=(t1,))
        return b.emit(eng, lambda E: E.scalar_tensor_tensor(out_ap, t1.ap, blend.ap[:, 1:2],
                                                            t2.ap, ALU.mult, ALU.add),
                      reads=(t1, t2, blend), writes=(out_t,))

    b.phase = "p0a"

    p0 = ExitStack()
    p0_tts = []

    def sb0(name, shape, dtype):
        t = TT(p0.enter_context(nc.sbuf_tensor(name, shape, dtype)).ap())
        p0_tts.append(t)
        return t

    wstage = [sb0(f"wstage{i}", [128, D], F32) for i in range(2)]
    q1 = sb0("q1", [128, D], F32)
    q2 = sb0("q2", [128, D], F32)
    q3 = sb0("q3", [128, D], BF16)
    q4 = sb0("q4", [128, 1024], FP8)
    wacc = sb0("wacc", [128, 4], F32)
    wred = sb0("wred", [128, 2], F32)

    wtiles = []
    for j in range(WGS // 128):
        wtiles.append((0, wg_in, wg_in.ap[128 * j:128 * (j + 1), :], D))
    for j in range(WGS // 128):
        wtiles.append((1, wu_in, wu_in.ap[128 * j:128 * (j + 1), :], D))
    for j in range(D // 128):
        wtiles.append((2, wd_in, wd_in.ap[128 * j:128 * (j + 1), :], WDS))

    b.phase = "p1ln"
    p1 = ExitStack()
    p1_tts = []

    def sb1(name, shape, dtype):
        t = TT(p1.enter_context(nc.sbuf_tensor(name, shape, dtype)).ap())
        p1_tts.append(t)
        return t

    ln_hi = sb1("ln_hi", [128, NDC * TQ], BF16)
    ln_lo = sb1("ln_lo", [128, NDC * TQ], BF16)
    p1a = ExitStack()
    p1a_tts = []

    def sb1a(name, shape, dtype):
        t = TT(p1a.enter_context(nc.sbuf_tensor(name, shape, dtype)).ap())
        p1a_tts.append(t)
        return t

    xstage = [sb1a(f"xstage{i}", [128, D], F32) for i in range(2)]
    xsq_b = [sb1a("xsq0", [128, D], F32)] * 2
    xn32_b = [sb1a("xn320", [128, D], F32)] * 2
    xnhi_b = [sb1a(f"xnhi{i}", [128, D], BF16) for i in range(2)]
    xnlo_b = [sb1a(f"xnlo{i}", [128, D], BF16) for i in range(2)]
    lnst_b = [sb1a(f"lnst{i}", [128, 8], F32) for i in range(2)]

    b.emit("v", lambda E: E.memset(wacc.ap[:, 0:3], 0.0), writes=(wacc,))

    def wsum_job(i):
        mat, src_t, src_ap, cols = wtiles[i]
        stw = wstage[i % 2]
        b.dma("s", stw, stw.ap[:, 0:cols], src_t, src_ap)
        b.emit("v", lambda E, stw=stw, cols=cols: E.reduce_sum(
            wred.ap[:, 0:1], stw.ap[:, 0:cols], AX, apply_absolute_value=True),
            reads=(stw,), writes=(wred,))
        b.emit("v", lambda E, mat=mat: E.tensor_add(
            wacc.ap[:, mat:mat + 1], wacc.ap[:, mat:mat + 1], wred.ap[:, 0:1]),
            reads=(wacc, wred), writes=(wacc,))

    NWJ = len(wtiles)
    wj = [0]
    for i in range(NXT):
        st = xstage[i % 2]
        xsq = xsq_b[i % 2]
        xn32 = xn32_b[i % 2]
        xnhi_tm = xnhi_b[i % 2]
        xnlo_tm = xnlo_b[i % 2]
        lnst = lnst_b[i % 2]
        b.dma("y", st, st.ap[:, :], x_in, x_in.ap[128 * i:128 * (i + 1), :])
        b.emit("v", lambda E, st=st, lnst=lnst: E.reduce_sum(lnst.ap[:, 0:1], st.ap, AX),
               reads=(st,), writes=(lnst,))
        b.emit("s", lambda E, st=st, xsq=xsq: E.activation(xsq.ap, st.ap, AF.Square),
               reads=(st,), writes=(xsq,))
        b.emit("v", lambda E, lnst=lnst, xsq=xsq: E.reduce_sum(lnst.ap[:, 1:2], xsq.ap, AX),
               reads=(xsq,), writes=(lnst,))
        b.emit("v", lambda E, lnst=lnst: E.tensor_scalar(lnst.ap[:, 2:4], lnst.ap[:, 0:2], 1.0 / D,
                                              None, ALU.mult), reads=(lnst,), writes=(lnst,))
        b.emit("v", lambda E, lnst=lnst: E.tensor_scalar(lnst.ap[:, 4:5], lnst.ap[:, 2:3],
                                              lnst.ap[:, 2:3], None, ALU.mult),
               reads=(lnst,), writes=(lnst,))
        b.emit("v", lambda E, lnst=lnst: E.tensor_sub(lnst.ap[:, 5:6], lnst.ap[:, 3:4], lnst.ap[:, 4:5]),
               reads=(lnst,), writes=(lnst,))
        b.emit("v", lambda E, lnst=lnst: E.tensor_scalar(lnst.ap[:, 5:6], lnst.ap[:, 5:6], EPS_LN,
                                              None, ALU.add), reads=(lnst,), writes=(lnst,))
        b.emit("s", lambda E, lnst=lnst: E.activation(lnst.ap[:, 5:6], lnst.ap[:, 5:6], AF.Sqrt),
               reads=(lnst,), writes=(lnst,))
        b.emit("v", lambda E, lnst=lnst: E.reciprocal(lnst.ap[:, 6:7], lnst.ap[:, 5:6]),
               reads=(lnst,), writes=(lnst,))
        b.emit("v", lambda E, lnst=lnst: E.tensor_scalar(lnst.ap[:, 7:8], lnst.ap[:, 2:3],
                                              lnst.ap[:, 6:7], -1.0, ALU.mult, ALU.mult),
               reads=(lnst,), writes=(lnst,))
        b.emit("v", lambda E, st=st, lnst=lnst, xnhi_tm=xnhi_tm: E.tensor_scalar(
            xnhi_tm.ap, st.ap, lnst.ap[:, 6:7], lnst.ap[:, 7:8], ALU.mult, ALU.add),
            reads=(st, lnst), writes=(xnhi_tm,))
        b.emit("v", lambda E, st=st, lnst=lnst, xn32=xn32: E.tensor_scalar(
            xn32.ap, st.ap, lnst.ap[:, 6:7], lnst.ap[:, 7:8], ALU.mult, ALU.add),
            reads=(st, lnst), writes=(xn32,))
        b.emit("v", lambda E, xn32=xn32, xnhi_tm=xnhi_tm, xnlo_tm=xnlo_tm: E.tensor_sub(
            xnlo_tm.ap, xn32.ap, xnhi_tm.ap),
               reads=(xn32, xnhi_tm), writes=(xnlo_tm,))
        b.dma("g", x_tm_d, x_tm_d.ap[i], st, st.ap)
        for half, (srcT, dstT) in enumerate(((xnhi_tm, ln_hi), (xnlo_tm, ln_lo))):
            for grp in range(2):
                bk = banks[2 + (half * 2 + grp) % 2]
                bkb = bk.ap.bitcast(BF16)
                for p in range(8):
                    cch = 8 * grp + p
                    b.emit("t", lambda E, bkb=bkb, p=p, srcT=srcT, cch=cch: E.transpose(
                        bkb[:, 128 * p:128 * (p + 1)], srcT.ap[:, 128 * cch:128 * (cch + 1)],
                        C["identb"].ap), reads=(srcT, C["identb"]), writes=(bk,))
                def evac(E, bkb=bkb, dstT=dstT, grp=grp, i=i):
                    dst = dstT.ap.rearrange("p (c q) -> p c q", c=NDC)[
                        :, 8 * grp:8 * grp + 8, 128 * i:128 * (i + 1)]
                    return E.tensor_copy(dst, bkb.rearrange("p (c q) -> p c q", c=8))
                b.emit("v", evac, reads=(bk,), writes=(dstT,))
        b.phase = "p0a"
        for _ in range(2):
            if wj[0] < NWJ:
                wsum_job(wj[0])
                wj[0] += 1
        b.phase = "p1ln"

    b.phase_barrier(p1a_tts)
    p1a.close()

    b.phase = "p1m"

    b.emit("s", lambda E: E.activation(gabs.ap[0:1, 0:128], gq_r.ap, AF.Abs),
           reads=(gq_r,), writes=(gabs,))
    b.emit("s", lambda E: E.activation(gabs.ap[0:1, 128:256], gk_r.ap, AF.Abs),
           reads=(gk_r,), writes=(gabs,))
    b.emit("v", lambda E: E.reduce_max(mtmp.ap[0:1, 0:1], gabs.ap[0:1, 0:128], AX),
           reads=(gabs,), writes=(mtmp,))
    b.emit("v", lambda E: E.reduce_max(mtmp.ap[0:1, 1:2], gabs.ap[0:1, 128:256], AX),
           reads=(gabs,), writes=(mtmp,))
    b.emit("v", lambda E: E.tensor_scalar(mtmp.ap[0:1, 2:3], mtmp.ap[0:1, 0:1],
                                          mtmp.ap[0:1, 1:2], -float(np.sqrt(128.0)),
                                          ALU.mult, ALU.mult), reads=(mtmp,), writes=(mtmp,))
    b.emit("t", lambda E: E.matmul(banks[4].ap[:, 0:1], C["onesfr"].ap, mtmp.ap[0:1, 2:3],
                                   start=True, stop=True), reads=(C["onesfr"], mtmp), writes=(banks[4],))
    b.emit("v", lambda E: E.tensor_copy(ebias.ap, banks[4].ap[:, 0:1]),
           reads=(banks[4],), writes=(ebias,))

    def split_weight(src_t, src_ap_fn, ntiles, cols, hi_t, lo_t, stage, w32, q="y"):
        w32b = w32 if isinstance(w32, list) else [w32, w32]
        for j in range(ntiles):
            w32 = w32b[j % 2]
            b.dma(q, w32, w32.ap[:, 0:cols], src_t, src_ap_fn(j))
            b.emit("s", lambda E, j=j, w32=w32: E.activation(
                hi_t.ap[:, cols * j:cols * (j + 1)], w32.ap[:, 0:cols], AF.Copy),
                   reads=(w32,), writes=(hi_t,))
            b.emit("v", lambda E, j=j, w32=w32: E.tensor_sub(
                lo_t.ap[:, cols * j:cols * (j + 1)], w32.ap[:, 0:cols],
                hi_t.ap[:, cols * j:cols * (j + 1)]),
                   reads=(w32, hi_t), writes=(lo_t,))

    b.phase = "p1kv"
    kvsq_b = [sb1(f"kvsq{i}", [128, 512], F32) for i in range(2)]
    kvn32_b = [sb1(f"kvn32{i}", [128, 512], F32) for i in range(2)]
    kvhi_b = [sb1(f"kvhi{i}", [128, 512], BF16) for i in range(2)]
    kvlo_b = [sb1(f"kvlo{i}", [128, 512], BF16) for i in range(2)]
    rmsrow_b = [sb1(f"rmsrow{i}", [1, 512], F32) for i in range(2)]

    p1b = ExitStack()
    p1b_tts = []

    def sb1b(name, shape, dtype):
        t = TT(p1b.enter_context(nc.sbuf_tensor(name, shape, dtype)).ap())
        p1b_tts.append(t)
        return t

    wkv32 = [sb1b(f"wkv32_{i}", [128, DKV], F32) for i in range(2)]
    wk_hi = sb1b("wk_hi", [128, NDC * DKV], BF16)
    wk_lo = sb1b("wk_lo", [128, NDC * DKV], BF16)
    wv_hi = sb1b("wv_hi", [128, NDC * DKV], BF16)
    wv_lo = sb1b("wv_lo", [128, NDC * DKV], BF16)
    split_weight(wk_in, lambda j: wk_in.ap[128 * j:128 * (j + 1), :], NDC, DKV, wk_hi, wk_lo, None, wkv32)
    split_weight(wv_in, lambda j: wv_in.ap[128 * j:128 * (j + 1), :], NDC, DKV, wv_hi, wv_lo, None, wkv32)

    def proj_head(w_hi, w_lo, hd, qh, pb):
        combos = ((w_hi, ln_hi), (w_hi, ln_lo), (w_lo, ln_hi))
        n_mm = 3 * NDC
        k = 0
        for (wt, lt) in combos:
            for cchunk in range(NDC):
                wap = wt.ap[:, DKV * cchunk + DH * hd:DKV * cchunk + DH * (hd + 1)]
                lap = lt.ap.rearrange("p (c q) -> p c q", c=NDC)[
                    :, cchunk, 512 * qh:512 * (qh + 1)]
                b.emit("t", lambda E, wap=wap, lap=lap, st=(k == 0), sp=(k == n_mm - 1),
                       pb=pb: E.matmul(pb.ap[:, 0:512], wap, lap, start=st, stop=sp),
                       reads=(wt, lt), writes=(pb,), inc=(k == n_mm - 1))
                k += 1

    def rms_split_store(pb, g_col, eps, dst_t, dst_pairs, do_rms, sbk2=None, par=0):
        kvsq, kvn32, kvhi, kvlo, rmsrow = (kvsq_b[par], kvn32_b[par], kvhi_b[par],
                                           kvlo_b[par], rmsrow_b[par])
        if sbk2 is None:
            sbk2 = banks[6]
        if do_rms:
            b.emit("s", lambda E: E.activation(kvsq.ap, pb.ap[:, 0:512], AF.Square),
                   reads=(pb,), writes=(kvsq,))
            b.emit("t", lambda E, sbk2=sbk2: E.matmul(sbk2.ap[0:1, 0:512], C["onesfc"].ap,
                                           kvsq.ap, start=True, stop=True),
                   reads=(C["onesfc"], kvsq), writes=(sbk2,))
            b.emit("v", lambda E, sbk2=sbk2: E.tensor_scalar(rmsrow.ap, sbk2.ap[0:1, 0:512],
                                                  1.0 / DH, eps, ALU.mult, ALU.add),
                   reads=(sbk2,), writes=(rmsrow,))
            b.emit("s", lambda E: E.activation(rmsrow.ap, rmsrow.ap, AF.Sqrt),
                   reads=(rmsrow,), writes=(rmsrow,))
            b.emit("v", lambda E: E.reciprocal(rmsrow.ap, rmsrow.ap),
                   reads=(rmsrow,), writes=(rmsrow,))
            b.emit("t", lambda E, sbk2=sbk2: E.matmul(sbk2.ap[:, 0:512], C["onesfr"].ap,
                                           rmsrow.ap, start=True, stop=True),
                   reads=(C["onesfr"], rmsrow), writes=(sbk2,))
            b.emit("v", lambda E, sbk2=sbk2: E.tensor_copy(kvsq.ap, sbk2.ap[:, 0:512]),
                   reads=(sbk2,), writes=(kvsq,))
            b.emit("v", lambda E: E.scalar_tensor_tensor(kvn32.ap, pb.ap[:, 0:512],
                                                         g_col.ap, kvsq.ap,
                                                         ALU.mult, ALU.mult),
                   reads=(pb, g_col, kvsq), writes=(kvn32,))
        else:
            b.emit("v", lambda E: E.tensor_copy(kvn32.ap, pb.ap[:, 0:512]),
                   reads=(pb,), writes=(kvn32,))
        b.emit("v", lambda E: E.tensor_copy(kvhi.ap, kvn32.ap), reads=(kvn32,), writes=(kvhi,))
        b.emit("v", lambda E: E.tensor_sub(kvlo.ap, kvn32.ap, kvhi.ap),
               reads=(kvn32, kvhi), writes=(kvlo,))
        for dst_hi_ap, dst_lo_ap in dst_pairs:
            b.dma("g", dst_t, dst_hi_ap, kvhi, kvhi.ap)
            b.dma("g", dst_t, dst_lo_ap, kvlo, kvlo.ap)

    kv_inv = kv_in.ap.rearrange("r (s hl d q) -> r s hl d q", s=2, hl=2, d=DH)
    pcall = [0]
    for hd in range(NKV):
        for qh in range(TQ // 512):
            pb = banks[5 + 2 * (pcall[0] % 2)]
            sbk2 = banks[6 - 2 * (pcall[0] % 2)]
            pcall[0] += 1
            proj_head(wk_hi, wk_lo, hd, qh, pb)
            dsts = []
            for dr in (2 * hd, 2 * hd + 1):
                dsts.append((kv_inv[dr, 0, 0, :, 512 * qh:512 * (qh + 1)],
                             kv_inv[dr, 0, 1, :, 512 * qh:512 * (qh + 1)]))
            rms_split_store(pb, gk_c, EPS_RMS, kv_in, dsts, True, sbk2,
                            par=(pcall[0] % 2))
            if wj[0] < NWJ:
                b.phase = "p0a"
                wsum_job(wj[0])
                wj[0] += 1
                b.phase = "p1kv"

    v32sb_b = kvn32_b
    v32p_b = kvsq_b
    vhip_b = kvhi_b
    vlop_b = kvlo_b
    for hd in range(NKV):
        for qh in range(TQ // 512):
            pb = banks[5 + 2 * (pcall[0] % 2)]
            sbk2 = banks[6 - 2 * (pcall[0] % 2)]
            pcall[0] += 1
            par = pcall[0] % 2
            v32sb, v32p, vhip, vlop = (v32sb_b[par], v32p_b[par], vhip_b[par],
                                       vlop_b[par])
            proj_head(wv_hi, wv_lo, hd, qh, pb)
            b.emit("v", lambda E, pb=pb, v32sb=v32sb: E.tensor_copy(v32sb.ap, pb.ap[:, 0:512]),
                   reads=(pb,), writes=(v32sb,))
            for p in range(4):
                b.emit("t", lambda E, p=p, sbk2=sbk2, v32sb=v32sb: E.transpose(
                    sbk2.ap[:, 128 * p:128 * (p + 1)],
                    v32sb.ap[:, 128 * p:128 * (p + 1)], C["identf"].ap),
                    reads=(v32sb, C["identf"]), writes=(sbk2,), inc=(p == 3))
            b.emit("v", lambda E, sbk2=sbk2, v32p=v32p: E.tensor_copy(v32p.ap, sbk2.ap),
                   reads=(sbk2,), writes=(v32p,))
            b.emit("v", lambda E, v32p=v32p, vhip=vhip: E.tensor_copy(vhip.ap, v32p.ap),
                   reads=(v32p,), writes=(vhip,))
            b.emit("v", lambda E, v32p=v32p, vhip=vhip, vlop=vlop: E.tensor_sub(
                vlop.ap, v32p.ap, vhip.ap), reads=(v32p, vhip), writes=(vlop,))
            for p in range(4):
                tok0 = 512 * qh + 128 * p
                for dr in (2 * hd, 2 * hd + 1):
                    vv = kv_inv[dr, 1].rearrange("hl d q -> hl (d q)").rearrange(
                        "hl (q d) -> hl q d", d=DH)
                    b.dma("g", kv_in, vv[0, tok0:tok0 + 128, :],
                          vhip, vhip.ap[:, 128 * p:128 * (p + 1)])
                    b.dma("g", kv_in, vv[1, tok0:tok0 + 128, :],
                          vlop, vlop.ap[:, 128 * p:128 * (p + 1)])
            if wj[0] < NWJ:
                b.phase = "p0a"
                wsum_job(wj[0])
                wj[0] += 1
                b.phase = "p1kv"

    ev_kv = b.collective("AllToAll", ALU.bypass, kv_in.ap.opt(), kv_out.ap.opt(),
                         reads=(kv_in,), writes=(kv_out,))

    b.phase = "p0a"
    while wj[0] < NWJ:
        wsum_job(wj[0])
        wj[0] += 1
    b.emit("t", lambda E: E.matmul(banks[0].ap[0:1, 0:3], C["onesfc"].ap, wacc.ap[:, 0:3],
                                   start=True, stop=True),
           reads=(C["onesfc"], wacc), writes=(banks[0],))
    b.emit("v", lambda E: E.tensor_copy(svals.ap[0:1, 0:3], banks[0].ap[0:1, 0:3]),
           reads=(banks[0],), writes=(svals,))
    b.dma("g", ar_in, ar_in.ap[0:1, 0:3], svals, svals.ap[0:1, 0:3])
    ev_ar = b.collective("AllReduce", ALU.add, ar_in.ap.opt(), ar_out.ap.opt(),
                         reads=(ar_in,), writes=(ar_out,))

    b.phase_barrier(p1b_tts)
    p1b.close()

    b.phase = "p1q"
    HD2 = D // 2
    wq_hi = sb1("wq_hi", [128, NDC * HD2], BF16)
    wq_lo = sb1("wq_lo", [128, NDC * HD2], BF16)
    wq32 = [sb1(f"wq32_{i}", [128, HD2], F32) for i in range(2)]
    for wh in range(2):
        split_weight(wq_in, lambda j: wq_in.ap[128 * j:128 * (j + 1), HD2 * wh:HD2 * (wh + 1)],
                     NDC, HD2, wq_hi, wq_lo, None, wq32, q=("y" if wh == 0 else "s"))
        for hd8 in range(NH // 2):
            hd = 8 * wh + hd8
            for qh in range(TQ // 512):
                pb = banks[5 + 2 * (pcall[0] % 2)]
                sbk2 = banks[6 - 2 * (pcall[0] % 2)]
                pcall[0] += 1
                combos = ((wq_hi, ln_hi), (wq_hi, ln_lo), (wq_lo, ln_hi))
                n_mm = 3 * NDC
                k = 0
                for (wt, lt) in combos:
                    for cchunk in range(NDC):
                        wap = wt.ap[:, HD2 * cchunk + DH * hd8:HD2 * cchunk + DH * (hd8 + 1)]
                        lap = lt.ap.rearrange("p (c q) -> p c q", c=NDC)[
                            :, cchunk, 512 * qh:512 * (qh + 1)]
                        b.emit("t", lambda E, wap=wap, lap=lap, st=(k == 0), sp=(k == n_mm - 1),
                               pb=pb: E.matmul(pb.ap[:, 0:512], wap, lap, start=st, stop=sp),
                               reads=(wt, lt), writes=(pb,), inc=(k == n_mm - 1))
                        k += 1
                qv = q_ain.ap.rearrange("r (hl g d q) -> r hl g d q", hl=2, g=2, d=DH)
                dst_hi = qv[hd // 2, 0, hd % 2, :, 512 * qh:512 * (qh + 1)]
                dst_lo = qv[hd // 2, 1, hd % 2, :, 512 * qh:512 * (qh + 1)]
                rms_split_store(pb, gq_c, EPS_RMS, q_ain, [(dst_hi, dst_lo)], True,
                                sbk2, par=(pcall[0] % 2))


    ev_q = b.collective("AllToAll", ALU.bypass, q_ain.ap.opt(), q_aout.ap.opt(),
                        reads=(q_ain,), writes=(q_aout,))

    b.phase = "p0a"
    b.dma("g", svals, svals.ap[0:1, 4:7], ar_out, ar_out.ap[0:1, 0:3])
    b.emit("v", lambda E: E.tensor_scalar(svals.ap[0:1, 4:7], svals.ap[0:1, 4:7],
                                          1.0 / (FI * D), EPS_Q, ALU.mult, ALU.max),
           reads=(svals,), writes=(svals,))
    b.emit("v", lambda E: E.tensor_scalar(svals.ap[0:1, 0:3], svals.ap[0:1, 4:7],
                                          0.5, None, ALU.mult), reads=(svals,), writes=(svals,))
    b.emit("t", lambda E: E.matmul(banks[0].ap[:, 0:3], C["onesfr"].ap, svals.ap[0:1, 0:3],
                                   start=True, stop=True),
           reads=(C["onesfr"], svals), writes=(banks[0],))
    b.emit("v", lambda E: E.tensor_copy(shalfB.ap[:, 0:3], banks[0].ap[:, 0:3]),
           reads=(banks[0],), writes=(shalfB,))
    b.emit("v", lambda E: E.tensor_scalar(shalfB.ap[:, 4:7], shalfB.ap[:, 0:3], -1.0,
                                          None, ALU.mult), reads=(shalfB,), writes=(shalfB,))

    b.phase = "p0b"
    tern_g = tern_in.ap[0:2 * GPART].rearrange("(m c p o) -> m c p o", m=2, c=NDC, p=128)
    tern_dv = tern_in.ap[2 * GPART:].rearrange("(bq p o) -> bq p o", bq=WDS // 128, p=128)

    def quant_job(i, jb):
        mat, src_t, src_ap, cols = wtiles[i]
        st = wstage[i % 2]
        b.dma("g", st, st.ap[:, 0:cols], src_t, src_ap)
        b.emit("s", lambda E, st=st, cols=cols: E.activation(q1.ap[:, 0:cols],
                                                             st.ap[:, 0:cols], AF.Abs),
               reads=(st,), writes=(q1,))
        b.emit("s", lambda E, st=st, cols=cols: E.sign(q2.ap[:, 0:cols], st.ap[:, 0:cols]),
               reads=(st,), writes=(q2,))
        b.emit("v", lambda E, cols=cols, mat=mat: E.tensor_single_scalar(
            q1.ap[:, 0:cols], q1.ap[:, 0:cols], shalfB.ap[:, mat:mat + 1], ALU.is_gt),
            reads=(q1, shalfB), writes=(q1,))
        b.emit("v", lambda E, cols=cols: E.tensor_mul(q3.ap[:, 0:cols], q1.ap[:, 0:cols],
                                                      q2.ap[:, 0:cols]),
            reads=(q1, q2), writes=(q3,))
        j = i % (WGS // 128) if mat < 2 else i - 2 * (WGS // 128)
        for grp in range(cols // 1024):
            bk = jb
            bkb = bk.ap.bitcast(BF16)
            for p in range(8):
                cc = 8 * grp + p
                b.emit("t", lambda E, bkb=bkb, p=p, cc=cc, bk=bk: E.transpose(
                    bkb[:, 128 * p:128 * (p + 1)], q3.ap[:, 128 * cc:128 * (cc + 1)],
                    C["identb"].ap), reads=(q3, C["identb"]), writes=(bk,), inc=(p == 7))
            b.emit("v", lambda E, bkb=bkb, bk=bk: E.tensor_copy(q4.ap, bkb[:, 0:1024]),
                   reads=(bk,), writes=(q4,))
            if mat < 2:
                dst = tern_g[mat, 8 * grp:8 * grp + 8, :, 128 * j:128 * (j + 1)]
            else:
                dst = tern_dv[8 * grp:8 * grp + 8, :, 128 * j:128 * (j + 1)]
            b.dma("g", tern_in, dst.transpose([1, 0, 2]),
                  q4, q4.ap.rearrange("p (c f) -> p c f", c=8))

    b.phase = "p2"
    b.phase_barrier(p1_tts)
    p1.close()
    p2 = ExitStack()
    p2_tts = []

    def sb2(name, shape, dtype):
        t = TT(p2.enter_context(nc.sbuf_tensor(name, shape, dtype)).ap())
        p2_tts.append(t)
        return t

    kfull = sb2("kfull", [128, 2 * S], BF16)
    vfull = sb2("vfull", [128, 2 * S], BF16)
    qfull = sb2("qfull", [128, 2 * S], BF16)
    e32b = [sb2(f"e32b{i}", [128, 512], F32) for i in range(4)]
    ehib = [sb2(f"ehib{i}", [128, 512], BF16) for i in range(4)]
    elob = [sb2(f"elob{i}", [128, 512], BF16) for i in range(4)]
    o32s = sb2("o32s", [128, 512], F32)
    ohis = sb2("ohis", [128, 512], BF16)
    olos = sb2("olos", [128, 512], BF16)
    recrow = sb2("recrow", [1, 512], F32)
    sclc = sb2("sclc", [128, 1], F32)
    b.emit("v", lambda E: E.memset(sclc.ap, 1.0 / float(np.sqrt(128.0))), writes=(sclc,))
    wos32 = [sb2(f"wos32_{i}", [128, D], F32) for i in range(2)]
    woh = [sb2(f"woh{i}", [128, D], BF16) for i in range(2)]
    wol = [sb2(f"wol{i}", [128, D], BF16) for i in range(2)]
    wo_sv = wo_split_d.ap.rearrange("hl j p d -> hl j p d")

    def wo_job(j):
        w32 = wos32[j % 2]
        wh, wl = woh[j % 2], wol[j % 2]
        b.dma("s", w32, w32.ap, wo_in, wo_in.ap[128 * j:128 * (j + 1), :])
        b.emit("s", lambda E, w32=w32, wh=wh: E.activation(wh.ap, w32.ap, AF.Copy),
               reads=(w32,), writes=(wh,))
        b.emit("v", lambda E, w32=w32, wh=wh, wl=wl: E.tensor_sub(wl.ap, w32.ap, wh.ap),
               reads=(w32, wh), writes=(wl,))
        b.dma("s", wo_split_d, wo_sv[0, j], wh, wh.ap)
        b.dma("s", wo_split_d, wo_sv[1, j], wl, wl.ap)

    def chunk_rank(ch):
        return ch if ch < NC else 15 - ch

    def chunk_local(ch):
        return 0 if ch < NC else CH

    kv_ov = kv_out.ap.rearrange("r (s hl d q) -> r s hl d q", s=2, hl=2, d=DH)
    q_ov = q_aout.ap.rearrange("r (hl g d q) -> r hl g d q", hl=2, g=2, d=DH)
    o_iv0 = o_ain0.ap.rearrange("r (hl d q) -> r hl d q", hl=2, d=DH)
    o_iv1 = o_ain1.ap.rearrange("r (hl d q) -> r hl d q", hl=2, d=DH)

    for ch in range(16):
        r, lc = chunk_rank(ch), chunk_local(ch)
        for hl in range(2):
            b.dma("y", kfull, kfull.ap[:, hl * S + ch * CH:hl * S + (ch + 1) * CH],
                  kv_out, kv_ov[r, 0, hl, :, lc:lc + CH])
            vsrc = kv_ov[r, 1, hl].rearrange("d q -> (d q)").rearrange(
                "(q d) -> q d", d=DH)[lc:lc + CH, :]
            vsrc = vsrc.rearrange("(sg p) d -> p sg d", p=128)
            vdst = vfull.ap.rearrange("p (hl f) -> p hl f", hl=2)[
                :, hl, ch * CH:(ch + 1) * CH].rearrange("p (sg d) -> p sg d", d=128)
            b.dma("y", vfull, vdst, kv_out, vsrc)

    NQB = S // CH
    KTC = CH // 128
    SB0, SGB, OB0 = 0, 4, 6
    for hloc in range(2):
        for ch in range(16):
            r, lc = chunk_rank(ch), chunk_local(ch)
            for hl in range(2):
                b.dma("y", qfull, qfull.ap[:, hl * S + ch * CH:hl * S + (ch + 1) * CH],
                      q_aout, q_ov[r, hl, hloc, :, lc:lc + CH])
        for qb in range(NQB):
            G0 = qb * CH
            dstart = G0 // 128
            nkt = dstart + KTC
            ob = banks[OB0 + qb % 2]
            sgb = banks[SGB + qb % 2]

            def s_mms(kt):
                sbk = banks[SB0 + kt % 4]
                qcol0 = max(0, 128 * (kt - dstart))
                N = CH - qcol0
                combos = ((0, 0), (0, 1), (1, 0))
                for mi, (khl, qhl) in enumerate(combos):
                    kap = kfull.ap[:, khl * S + 128 * kt:khl * S + 128 * (kt + 1)]
                    qap = qfull.ap[:, qhl * S + G0 + qcol0:qhl * S + G0 + CH]
                    b.emit("t", lambda E, kap=kap, qap=qap, sbk=sbk, qcol0=qcol0,
                           st=(mi == 0), sp=(mi == 2), N=N: E.matmul(
                               sbk.ap[:, qcol0:qcol0 + N], kap, qap, start=st, stop=sp),
                           reads=(kfull, qfull), writes=(sbk,), inc=(mi == 2))

            def e_path(kt):
                sbk = banks[SB0 + kt % 4]
                qcol0 = max(0, 128 * (kt - dstart))
                N = CH - qcol0
                e32 = e32b[kt % 4]
                b.emit("s", lambda E, sbk=sbk, e32=e32, qcol0=qcol0, N=N: E.activation(
                    e32.ap[:, 0:N], sbk.ap[:, qcol0:qcol0 + N], AF.Exp,
                    bias=ebias.ap[:, 0:1], scale=sclc.ap[:, 0:1]),
                    reads=(sbk, ebias, sclc), writes=(e32,))
                if kt >= dstart:
                    b.emit("v", lambda E, e32=e32: E.tensor_mul(
                        e32.ap[:, 0:128], e32.ap[:, 0:128], C["tri"].ap),
                        reads=(e32, C["tri"]), writes=(e32,))
                ehi, elo = ehib[kt % 4], elob[kt % 4]
                b.emit("s", lambda E, e32=e32, ehi=ehi, N=N: E.activation(
                    ehi.ap[:, 0:N], e32.ap[:, 0:N], AF.Copy), reads=(e32,), writes=(ehi,))
                b.emit("v", lambda E, e32=e32, ehi=ehi, elo=elo, N=N: E.tensor_sub(
                    elo.ap[:, 0:N], e32.ap[:, 0:N], ehi.ap[:, 0:N]),
                    reads=(e32, ehi), writes=(elo,))

            def sig_pv(kt):
                qcol0 = max(0, 128 * (kt - dstart))
                N = CH - qcol0
                ehi, elo = ehib[kt % 4], elob[kt % 4]
                last = (kt == nkt - 1)
                for pi, et in enumerate((ehi, elo)):
                    b.emit("t", lambda E, et=et, sgb=sgb, qcol0=qcol0, N=N,
                           st=(kt == 0 and pi == 0), sp=(last and pi == 1): E.matmul(
                               sgb.ap[0:1, qcol0:qcol0 + N], C["onesb"].ap, et.ap[:, 0:N],
                               start=st, stop=sp),
                           reads=(et, C["onesb"]), writes=(sgb,), inc=(pi == 1))
                pvcombos = ((0, ehi), (0, elo), (1, ehi))
                for mi, (vhl, et) in enumerate(pvcombos):
                    vap = vfull.ap[:, vhl * S + 128 * kt:vhl * S + 128 * (kt + 1)]
                    b.emit("t", lambda E, vap=vap, et=et, ob=ob, qcol0=qcol0, N=N,
                           st=(kt == 0 and mi == 0), sp=(last and mi == 2): E.matmul(
                               ob.ap[:, qcol0:qcol0 + N], vap, et.ap[:, 0:N],
                               start=st, stop=sp),
                           reads=(vfull, et), writes=(ob,), inc=(mi == 2))

            DEPTH = 3
            for kt in range(nkt):
                s_mms(kt)
                e_path(kt)
                if kt >= DEPTH:
                    sig_pv(kt - DEPTH)
            for kt in range(max(0, nkt - DEPTH), nkt):
                sig_pv(kt)
            b.phase = "p2ep"
            b.emit("v", lambda E, sgb=sgb: E.reciprocal(recrow.ap[0:1, 0:CH],
                                                        sgb.ap[0:1, 0:CH]),
                   reads=(sgb,), writes=(recrow,))
            rb = banks[SB0 + (nkt + 1) % 4]
            b.emit("t", lambda E, rb=rb: E.matmul(rb.ap[:, 0:CH], C["onesfr"].ap,
                                                  recrow.ap[0:1, 0:CH], start=True, stop=True),
                   reads=(C["onesfr"], recrow), writes=(rb,))
            b.emit("v", lambda E, rb=rb: E.tensor_copy(e32b[0].ap[:, 0:CH], rb.ap[:, 0:CH]),
                   reads=(rb,), writes=(e32b[0],))
            b.emit("v", lambda E, ob=ob: E.tensor_mul(o32s.ap[:, 0:CH],
                                                      ob.ap[:, 0:CH], e32b[0].ap[:, 0:CH]),
                   reads=(ob, e32b[0]), writes=(o32s,))
            b.emit("v", lambda E: E.tensor_copy(ohis.ap[:, 0:CH], o32s.ap[:, 0:CH]),
                   reads=(o32s,), writes=(ohis,))
            b.emit("v", lambda E: E.tensor_sub(olos.ap[:, 0:CH], o32s.ap[:, 0:CH],
                                               ohis.ap[:, 0:CH]), reads=(o32s, ohis), writes=(olos,))
            r, lc = chunk_rank(qb), chunk_local(qb)
            o_t, o_ivh = ((o_ain0, o_iv0), (o_ain1, o_iv1))[hloc]
            b.dma("y", o_t, o_ivh[r, 0, :, lc:lc + CH], ohis, ohis.ap[:, 0:CH])
            b.dma("y", o_t, o_ivh[r, 1, :, lc:lc + CH], olos, olos.ap[:, 0:CH])
            b.phase = "p2"
            if hloc == 0:
                b.phase = "p0b"
                jb = banks[SGB + (qb + 1) % 2]
                perm = list(range(16, 32)) + list(range(0, 16))
                quant_job(perm[2 * qb], jb)
                quant_job(perm[2 * qb + 1], jb)
                wo_job(qb)
                if qb == 7:
                    ev_ag_d = b.collective(
                        "AllGather", ALU.bypass, tern_in.ap[2 * GPART:].opt(),
                        tern_outd.ap.opt(), reads=(tern_in,), writes=(tern_outd,))
                if qb == 11:
                    ev_ag_g = b.collective(
                        "AllGather", ALU.bypass, tern_in.ap[0:GPART].opt(),
                        tern_outg.ap.opt(), reads=(tern_in,), writes=(tern_outg,))
                if qb == 15:
                    ev_ag_u = b.collective(
                        "AllGather", ALU.bypass, tern_in.ap[GPART:2 * GPART].opt(),
                        tern_outu.ap.opt(), reads=(tern_in,), writes=(tern_outu,))
                b.phase = "p2"

        if hloc == 0:
            ev_o0 = b.collective("AllToAll", ALU.bypass, o_ain0.ap.opt(),
                                 o_aout0.ap.opt(), reads=(o_ain0,), writes=(o_aout0,))

    ev_o1 = b.collective("AllToAll", ALU.bypass, o_ain1.ap.opt(), o_aout1.ap.opt(),
                         reads=(o_ain1,), writes=(o_aout1,))

    b.phase = "p3"
    b.phase_barrier(p2_tts)
    p2.close()
    b.phase_barrier(p0_tts)
    p0.close()
    p3 = ExitStack()
    p3_tts = []

    def sb3(name, shape, dtype):
        t = TT(p3.enter_context(nc.sbuf_tensor(name, shape, dtype)).ap())
        p3_tts.append(t)
        return t

    obuf_h = [sb3(f"obuf_h{i}", [128, 2 * NC * TQ], BF16) for i in range(2)]
    wo_g = [sb3(f"wo_g{i}", [128, 2 * NH * 512], BF16) for i in range(2)]
    xg = [sb3(f"xg{i}", [128, NXT * 512], F32) for i in range(2)]
    x1g = [sb3(f"x1g{i}", [128, NXT * 512], F32) for i in range(2)]

    o_ov0 = o_aout0.ap.rearrange("r (hl d q) -> r hl d q", hl=2, d=DH)
    o_ov1 = o_aout1.ap.rearrange("r (hl d q) -> r hl d q", hl=2, d=DH)
    obq = [obuf_h[i].ap.rearrange("p (b q) -> p b q", b=2 * NC) for i in range(2)]

    def load_obuf(half, ov, ot):
        for srcr in range(NC):
            for hl in range(2):
                qn = ("y", "s", "g")[(2 * srcr + hl) % 3]
                b.dma(qn, obuf_h[half], obq[half][:, hl * NC + srcr, :], ot, ov[srcr, hl])

    load_obuf(0, o_ov0, o_aout0)
    wo_gv = [wo_g[i].ap.rearrange("p (b c) -> p b c", b=2 * NH) for i in range(2)]

    def load_wog(g, slot):
        for hl in range(2):
            for hd in range(NDC):
                b.dma("y" if hd % 2 == 0 else "s", wo_g[slot],
                      wo_gv[slot][:, hl * NH + hd, :], wo_split_d,
                      wo_sv[hl, hd][:, 512 * g:512 * (g + 1)])
        for ti in range(NXT):
            b.dma("g", xg[slot], xg[slot].ap[:, 512 * ti:512 * (ti + 1)],
                  x_tm_d, x_tm_d.ap[ti][:, 512 * g:512 * (g + 1)])

    load_wog(0, 0)
    load_wog(1, 1)
    load_obuf(1, o_ov1, o_aout1)
    combos = ((0, 0), (0, 1), (1, 0))
    for g in range(4):
        slot = g % 2
        if g + 1 < 4 and g >= 1:
            load_wog(g + 1, 1 - slot)
        for ti in range(NXT):
            pb = banks[ti % 2]
            k = 0
            for half in range(2):
                for srcr in range(NC):
                    h = 2 * srcr + half
                    for (ohl, whl) in combos:
                        oap = obq[half][:, ohl * NC + srcr, 128 * ti:128 * (ti + 1)]
                        wap = wo_gv[slot][:, whl * NH + h, :]
                        b.emit("t", lambda E, oap=oap, wap=wap, pb=pb, st=(k == 0),
                               sp=(k == 3 * NH - 1): E.matmul(
                                   pb.ap[:, 0:512], oap, wap, start=st, stop=sp),
                               reads=(obuf_h[half], wo_g[slot]), writes=(pb,),
                               inc=(k == 3 * NH - 1))
                        k += 1
            b.emit("v", lambda E, pb=pb, ti=ti, slot=slot: E.tensor_add(
                x1g[slot].ap[:, 512 * ti:512 * (ti + 1)], pb.ap[:, 0:512],
                xg[slot].ap[:, 512 * ti:512 * (ti + 1)]),
                reads=(pb, xg[slot]), writes=(x1g[slot],))
            b.dma("y", x1_d, x1_d.ap[ti][:, 512 * g:512 * (g + 1)],
                  x1g[slot], x1g[slot].ap[:, 512 * ti:512 * (ti + 1)])

    b.phase = "p4"
    b.phase_barrier(p3_tts)
    p3.close()
    p4 = ExitStack()
    p4_tts = []

    def sb4(name, shape, dtype):
        t = TT(p4.enter_context(nc.sbuf_tensor(name, shape, dtype)).ap())
        p4_tts.append(t)
        return t

    T2 = 256
    NT2 = TQ // T2
    C1 = 1.0 / float(np.sqrt(float(D)))
    C2 = 1.0 / float(np.sqrt(float(FI)))

    x1t4 = [[sb4(f"x1t{p}_{t}", [128, D], F32) for t in range(2)] for p in range(2)]
    st8_b = [sb4(f"st8_{p}", [128, 10], F32) for p in range(2)]
    z1 = sb4("z1", [128, NDC * T2], F32)
    zt = sb4("zt", [128, NDC * T2], F32)
    xq1_b = [sb4(f"xq1_{p}", [128, NDC * T2], BF16) for p in range(2)]
    terng = [sb4(f"terng{i}", [128, NDC * 256], FP8) for i in range(2)]
    ternu = [sb4(f"ternu{i}", [128, NDC * 256], FP8) for i in range(2)]
    z2 = sb4("z2", [128, NIC * T2], F32)
    xnf = sb4("xnf", [128, 2 * NDC * T2], BF16)
    hst = [sb4(f"hst{i}", [128, 512], F32) for i in range(2)]
    hhi = [sb4(f"hhi{i}", [128, 512], BF16) for i in range(2)]
    hlo = [sb4(f"hlo{i}", [128, 512], BF16) for i in range(2)]
    u32 = [sb4(f"u32{i}", [128, 512], F32) for i in range(2)]
    qss1 = sb4("qss1", [128, 256], F32)
    qss2 = sb4("qss2", [128, 256], F32)
    dqs_b = [sb4(f"dqs_{p}", [128, 512], F32) for p in range(2)]
    qrow = sb4("qrow", [1, 1024], F32)
    wdb = [sb4(f"wdb{i}", [128, D], FP8) for i in range(2)]
    rtmp1 = sb4("rtmp1", [128, 8], F32)
    rtmpB = sb4("rtmpB", [128, 4], F32)
    st8b2 = sb4("st8b2", [128, 4], F32)

    class _View:
        __slots__ = ("ap",)

        def __init__(self, ap):
            self.ap = ap
    z2bf = z2.ap.bitcast(BF16)
    xnfm_hi = _View(xnf.ap[:, 0:NDC * T2])
    xnfm_lo = _View(xnf.ap[:, NDC * T2:2 * NDC * T2])
    ztbf = zt.ap.bitcast(BF16)
    zq1 = _View(zt.ap[:, 3584:4096])
    xn2lo = _View(ztbf[:, 4096:6144])
    xn2hi = _View(ztbf[:, 6144:8192])

    def butterfly_level(z, nchunk, hsz):
        zv = z.ap.rearrange("p (bq i q) -> p bq i q", bq=nchunk // (2 * hsz), i=2 * hsz)
        a = zv[:, :, 0:hsz, :]
        bb = zv[:, :, hsz:2 * hsz, :]
        b.emit("v", lambda E, a=a, bb=bb: E.tensor_add(a, a, bb), reads=(z,), writes=(z,))
        b.emit("v", lambda E, a=a, bb=bb: E.scalar_tensor_tensor(
            bb, bb, -2.0, a, ALU.mult, ALU.add), reads=(z,), writes=(z,))

    def h1_stage_fns(j):
        p = j % 2
        st8 = st8_b[p]
        xq1 = xq1_b[p]
        dqs = dqs_b[p]
        xn232 = _View(zt.ap[:, 0:D])
        fns = []

        def ln2_stats(stt):
            ti = 2 * j + stt
            xt = x1t4[p][stt]
            b.phase = "p4ln"
            b.dma("y", xt, xt.ap, x1_d, x1_d.ap[ti])
            b.emit("v", lambda E, xt=xt: E.reduce_sum(st8.ap[:, 0:1], xt.ap, AX),
                   reads=(xt,), writes=(st8,))
            b.emit("s", lambda E, xt=xt: E.activation(xn232.ap, xt.ap, AF.Square),
                   reads=(xt,), writes=(zt,))
            b.emit("v", lambda E: E.reduce_sum(st8.ap[:, 1:2], xn232.ap, AX),
                   reads=(zt,), writes=(st8,))
            b.emit("v", lambda E: E.tensor_scalar(st8.ap[:, 2:4], st8.ap[:, 0:2], 1.0 / D,
                                                  None, ALU.mult), reads=(st8,), writes=(st8,))
            b.emit("v", lambda E: E.tensor_scalar(st8.ap[:, 4:5], st8.ap[:, 2:3],
                                                  st8.ap[:, 2:3], None, ALU.mult),
                   reads=(st8,), writes=(st8,))
            b.emit("v", lambda E: E.tensor_sub(st8.ap[:, 5:6], st8.ap[:, 3:4], st8.ap[:, 4:5]),
                   reads=(st8,), writes=(st8,))
            b.emit("v", lambda E: E.tensor_scalar(st8.ap[:, 5:6], st8.ap[:, 5:6], EPS_LN,
                                                  None, ALU.add), reads=(st8,), writes=(st8,))
            b.emit("s", lambda E: E.activation(st8.ap[:, 5:6], st8.ap[:, 5:6], AF.Sqrt),
                   reads=(st8,), writes=(st8,))
            b.emit("v", lambda E: E.reciprocal(st8.ap[:, 6:7], st8.ap[:, 5:6]),
                   reads=(st8,), writes=(st8,))
            b.emit("v", lambda E: E.tensor_scalar(st8.ap[:, 7:8], st8.ap[:, 2:3],
                                                  st8.ap[:, 6:7], -1.0, ALU.mult, ALU.mult),
                   reads=(st8,), writes=(st8,))
            b.emit("v", lambda E, xt=xt: E.tensor_scalar(
                xn2hi.ap, xt.ap, st8.ap[:, 6:7], st8.ap[:, 7:8], ALU.mult, ALU.add),
                reads=(xt, st8), writes=(zt,))
            b.emit("v", lambda E, xt=xt: E.tensor_scalar(
                xn232.ap, xt.ap, st8.ap[:, 6:7], st8.ap[:, 7:8], ALU.mult, ALU.add),
                reads=(xt, st8), writes=(zt,))
            b.emit("v", lambda E: E.tensor_sub(xn2lo.ap, xn232.ap, xn2hi.ap),
                   reads=(zt,), writes=(zt,))

        def ln2_tr(stt, half, grp):
            srcT, dstT = ((xn2hi, xnfm_hi), (xn2lo, xnfm_lo))[half]
            b.phase = "p4ln"
            bk = banks[grp % 2]
            bkb = bk.ap.bitcast(BF16)
            for pp in range(8):
                cc = 8 * grp + pp
                b.emit("t", lambda E, bkb=bkb, pp=pp, srcT=srcT, cc=cc: E.transpose(
                    bkb[:, 128 * pp:128 * (pp + 1)], srcT.ap[:, 128 * cc:128 * (cc + 1)],
                    C["identb"].ap), reads=(zt, C["identb"]), writes=(bk,))

            def evac2(E, bkb=bkb, dstT=dstT, grp=grp, stt=stt):
                dst = dstT.ap.rearrange("p (c q) -> p c q", c=NDC)[
                    :, 8 * grp:8 * grp + 8, 128 * stt:128 * (stt + 1)]
                return E.tensor_copy(dst, bkb.rearrange("p (c q) -> p c q", c=8))
            b.emit("v", evac2, reads=(bk,), writes=(xnf,))

        def fw1(c4):
            b.phase = "p4fw1"
            for cc in range(4 * c4, 4 * c4 + 4):
                bk = banks[cc % 2]
                for pi, srcT in enumerate((xnfm_hi, xnfm_lo)):
                    b.emit("t", lambda E, srcT=srcT, cc=cc, bk=bk, st=(pi == 0),
                           sp=(pi == 1): E.matmul(
                        bk.ap[:, 0:T2], C["h128"].ap, srcT.ap[:, T2 * cc:T2 * (cc + 1)],
                        start=st, stop=sp), reads=(xnf, C["h128"]), writes=(bk,), inc=(pi == 1))
                b.emit("s", lambda E, cc=cc, bk=bk: E.activation(
                    z1.ap[:, T2 * cc:T2 * (cc + 1)], bk.ap[:, 0:T2], AF.Copy),
                    reads=(bk,), writes=(z1,))

        def bf1(lev):
            b.phase = "p4fw1"
            butterfly_level(z1, NDC, NDC // (2 ** (lev + 1)))

        def q1max():
            b.phase = "p4q1"
            b.emit("s", lambda E: E.activation(zt.ap, z1.ap, AF.Abs), reads=(z1,), writes=(zt,))
            cv = zt.ap.rearrange("p (c q) -> p c q", c=NDC)
            b.emit("v", lambda E, cv=cv: E.tensor_tensor(
                cv[:, 0:8, :], cv[:, 0:8, :], cv[:, 8:16, :], ALU.max),
                reads=(zt,), writes=(zt,))
            for nn in (4, 2, 1):
                b.emit("v", lambda E, nn=nn, cv=cv: E.tensor_tensor(
                    cv[:, 0:nn, :], cv[:, 0:nn, :], cv[:, nn:2 * nn, :], ALU.max),
                    reads=(zt,), writes=(zt,))
            for pp in range(2):
                b.emit("t", lambda E, pp=pp: E.transpose(
                    banks[0].ap[:, 128 * pp:128 * (pp + 1)], zt.ap[:, 128 * pp:128 * (pp + 1)],
                    C["identf"].ap), reads=(zt, C["identf"]), writes=(banks[0],))
            b.emit("v", lambda E: E.tensor_copy(zq1.ap[:, 256:512], banks[0].ap[:, 0:256]),
                   reads=(banks[0],), writes=(zt,))
            b.emit("v", lambda E: E.reduce_max(st8.ap[:, 8:9], zq1.ap[:, 256:384], AX),
                   reads=(zt,), writes=(st8,))
            b.emit("v", lambda E: E.reduce_max(st8.ap[:, 9:10], zq1.ap[:, 384:512], AX),
                   reads=(zt,), writes=(st8,))

        def q1scales():
            b.phase = "p4q1"
            b.emit("v", lambda E: E.tensor_scalar(st8.ap[:, 8:10], st8.ap[:, 8:10], C1, EPS_Q,
                                                  ALU.mult, ALU.max), reads=(st8,), writes=(st8,))
            b.emit("v", lambda E: E.reciprocal(rtmp1.ap[:, 0:2], st8.ap[:, 8:10]),
                   reads=(st8,), writes=(rtmp1,))
            b.emit("v", lambda E: E.tensor_scalar(rtmp1.ap[:, 0:2], rtmp1.ap[:, 0:2], 127.0 * C1,
                                                  None, ALU.mult), reads=(rtmp1,), writes=(rtmp1,))
            b.emit("v", lambda E: E.tensor_scalar(rtmp1.ap[:, 2:3], st8.ap[:, 8:9],
                                                  shalfB.ap[:, 0:1], 2.0 / 127.0, ALU.mult, ALU.mult),
                   reads=(st8, shalfB), writes=(rtmp1,))
            b.emit("v", lambda E: E.tensor_scalar(rtmp1.ap[:, 3:4], st8.ap[:, 9:10],
                                                  shalfB.ap[:, 0:1], 2.0 / 127.0, ALU.mult, ALU.mult),
                   reads=(st8, shalfB), writes=(rtmp1,))
            b.emit("v", lambda E: E.tensor_scalar(rtmp1.ap[:, 4:5], st8.ap[:, 8:9],
                                                  shalfB.ap[:, 1:2], 2.0 / 127.0, ALU.mult, ALU.mult),
                   reads=(st8, shalfB), writes=(rtmp1,))
            b.emit("v", lambda E: E.tensor_scalar(rtmp1.ap[:, 5:6], st8.ap[:, 9:10],
                                                  shalfB.ap[:, 1:2], 2.0 / 127.0, ALU.mult, ALU.mult),
                   reads=(st8, shalfB), writes=(rtmp1,))

        def q1rows():
            b.phase = "p4q1"
            for pp in range(4):
                b.emit("t", lambda E, pp=pp: E.transpose(
                    banks[0].ap[0:1, 128 * pp:128 * (pp + 1)], rtmp1.ap[:, pp:pp + 1],
                    C["identf"].ap), reads=(rtmp1, C["identf"]), writes=(banks[0],))
            b.emit("v", lambda E: E.tensor_copy(qrow.ap[0:1, 0:512], banks[0].ap[0:1, 0:512]),
                   reads=(banks[0],), writes=(qrow,))
            for pp in range(2):
                b.emit("t", lambda E, pp=pp: E.transpose(
                    banks[0].ap[0:1, 128 * pp:128 * (pp + 1)], rtmp1.ap[:, 4 + pp:5 + pp],
                    C["identf"].ap), reads=(rtmp1, C["identf"]), writes=(banks[0],))
            b.emit("v", lambda E: E.tensor_copy(qrow.ap[0:1, 512:768], banks[0].ap[0:1, 0:256]),
                   reads=(banks[0],), writes=(qrow,))
            b.emit("t", lambda E: E.matmul(banks[1].ap[:, 0:256], C["onesfr"].ap,
                                           qrow.ap[0:1, 0:256], start=True, stop=True),
                   reads=(C["onesfr"], qrow), writes=(banks[1],))
            b.emit("v", lambda E: E.tensor_copy(qss1.ap[:, 0:256], banks[1].ap[:, 0:256]),
                   reads=(banks[1],), writes=(qss1,))
            b.emit("t", lambda E: E.matmul(banks[1].ap[:, 0:512], C["onesfr"].ap,
                                           qrow.ap[0:1, 256:768], start=True, stop=True),
                   reads=(C["onesfr"], qrow), writes=(banks[1],))
            b.emit("v", lambda E, dqs=dqs: E.tensor_copy(dqs.ap[:, 0:512], banks[1].ap[:, 0:512]),
                   reads=(banks[1],), writes=(dqs,))

        def q1big(hf):
            b.phase = "p4q1"
            HC = NDC // 2
            sl = slice(hf * HC * T2, (hf + 1) * HC * T2)
            b.emit("v", lambda E, sl=sl: E.tensor_tensor(
                zt.ap[:, sl].rearrange("p (c q) -> p c q", c=HC),
                z1.ap[:, sl].rearrange("p (c q) -> p c q", c=HC),
                qss1.ap[:, 0:256].unsqueeze(1).broadcast_to([128, HC, 256]), ALU.mult),
                reads=(z1, qss1), writes=(zt,))
            b.emit("v", lambda E, sl=sl: E.tensor_scalar(zt.ap[:, sl], zt.ap[:, sl], C32, C32,
                                                         ALU.add, ALU.subtract),
                   reads=(zt,), writes=(zt,))
            b.emit("v", lambda E, sl=sl: E.tensor_scalar(z1.ap[:, sl], zt.ap[:, sl], C52, C52,
                                                         ALU.add, ALU.subtract),
                   reads=(zt,), writes=(z1,))
            b.emit("v", lambda E, sl=sl, xq1=xq1: E.scalar_tensor_tensor(
                xq1.ap[:, sl], zt.ap[:, sl], blend.ap[:, 1:2], z1.ap[:, sl],
                ALU.mult, ALU.add), reads=(zt, z1, blend), writes=(xq1,))

        for stt in range(2):
            fns.append(lambda stt=stt: ln2_stats(stt))
            for half in range(2):
                for grp in range(2):
                    fns.append(lambda stt=stt, half=half, grp=grp: ln2_tr(stt, half, grp))
        for c4 in range(4):
            fns.append(lambda c4=c4: fw1(c4))
        for lev in range(4):
            fns.append(lambda lev=lev: bf1(lev))
        fns.append(q1max)
        fns.append(q1scales)
        fns.append(q1rows)
        fns.append(lambda: q1big(0))
        fns.append(lambda: q1big(1))
        return fns

    def h2_emit(j, drain):
        p = j % 2
        xq1 = xq1_b[p]
        dqs = dqs_b[p]
        zqa = _View(xq1.ap.bitcast(F32)[:, 0:1024])
        zqb = _View(xq1.ap.bitcast(F32)[:, 1024:2048])
        b.phase = "p4gu"

        def tern_load(jj, hs):
            tg, tu = terng[hs % 2], ternu[hs % 2]
            r = (2 * hs) // (WGS // 128)
            col0 = (2 * hs) % (WGS // 128) * 128
            for m, dstT in ((0, tg), (1, tu)):
                tsrc = (tern_outg, tern_outu)[m]
                srcflat = tsrc.ap[r * GPART:(r + 1) * GPART]
                srcv = srcflat.rearrange("(c p o) -> c p o", c=NDC, p=128)[
                    :, :, col0:col0 + 256].transpose([1, 0, 2])
                b.dma(("s", "g")[m], dstT, dstT.ap.rearrange("p (c f) -> p c f", c=NDC),
                      tsrc, srcv)

        def stage_a(hs):
            par = hs % 2
            hhip, hlop = hhi[par], hlo[par]
            hbk = banks[4] if par == 0 else banks[7]
            for pi, srcT in enumerate((hhip, hlop)):
                b.emit("t", lambda E, srcT=srcT, st=(pi == 0), sp=(pi == 1), hbk=hbk:
                       E.matmul(hbk.ap[:, 0:512], C["h128"].ap, srcT.ap, start=st,
                                stop=sp), reads=(srcT, C["h128"]), writes=(hbk,),
                       inc=(pi == 1))
            b.emit("s", lambda E, hs=hs, hbk=hbk: E.activation(
                z2.ap[:, 512 * hs:512 * (hs + 1)], hbk.ap[:, 0:512], AF.Copy),
                reads=(hbk,), writes=(z2,))

        if j == 0:
            tern_load(j, 0)
            tern_load(j, 1)
        for hs in range(NIC // 2):
            tg, tu = terng[hs % 2], ternu[hs % 2]
            par = hs % 2
            gb, ub = (banks[2], banks[3]) if par == 0 else (banks[5], banks[6])
            hstp, u32p, hhip, hlop = hst[par], u32[par], hhi[par], hlo[par]
            for m, (wt, pb) in enumerate(((tg, gb), (tu, ub))):
                for dtl in range(2):
                    for cc in range(NDC):
                        b.emit("t", lambda E, wt=wt, pb=pb, cc=cc, dtl=dtl,
                               st=(cc == 0),
                               sp=(cc == NDC - 1): E.matmul(
                                   pb.ap[:, 256 * dtl:256 * (dtl + 1)],
                                   wt.ap[:, 256 * cc + 128 * dtl:256 * cc + 128 * (dtl + 1)],
                                   xq1.ap[:, T2 * cc:T2 * (cc + 1)], start=st, stop=sp),
                               reads=(wt, xq1), writes=(pb,),
                               inc=(dtl == 1 and cc == NDC - 1))
            if hs + 2 < NIC // 2:
                tern_load(j, hs + 2)
            if hs > 0:
                stage_a(hs - 1)
            dqv = dqs.ap[:, 0:256].unsqueeze(1).broadcast_to([128, 2, 256])
            dqv_u = dqs.ap[:, 256:512].unsqueeze(1).broadcast_to([128, 2, 256])
            b.emit("v", lambda E, gb=gb, hstp=hstp, dqv=dqv, dqs=dqs: E.tensor_tensor(
                hstp.ap.rearrange("p (c q) -> p c q", c=2),
                gb.ap[:, 0:512].rearrange("p (c q) -> p c q", c=2), dqv, ALU.mult),
                reads=(gb, dqs), writes=(hstp,))
            b.emit("s", lambda E, hstp=hstp: E.activation(hstp.ap, hstp.ap, AF.Silu),
                   reads=(hstp,), writes=(hstp,))
            b.emit("v", lambda E, ub=ub, u32p=u32p, dqv_u=dqv_u, dqs=dqs: E.tensor_tensor(
                u32p.ap.rearrange("p (c q) -> p c q", c=2),
                ub.ap[:, 0:512].rearrange("p (c q) -> p c q", c=2), dqv_u, ALU.mult),
                reads=(ub, dqs), writes=(u32p,))
            b.emit("v", lambda E, hstp=hstp, u32p=u32p: E.tensor_mul(
                hstp.ap, hstp.ap, u32p.ap), reads=(hstp, u32p), writes=(hstp,))
            b.emit("v", lambda E, hstp=hstp, hhip=hhip: E.tensor_copy(hhip.ap, hstp.ap),
                   reads=(hstp,), writes=(hhip,))
            b.emit("v", lambda E, hstp=hstp, hhip=hhip, hlop=hlop: E.tensor_sub(
                hlop.ap, hstp.ap, hhip.ap), reads=(hstp, hhip), writes=(hlop,))
            drain(1)
        stage_a(NIC // 2 - 1)
        if j + 1 < NT2:
            tern_load(j + 1, 0)
            tern_load(j + 1, 1)
        drain(-1)
        b.phase = "p4fw2"
        for lev in range(6):
            butterfly_level(z2, NIC, NIC // (2 ** (lev + 1)))
        b.phase = "p4q2"
        for seg in range(16):
            src = z2.ap[:, 1024 * seg:1024 * (seg + 1)]
            if seg == 0:
                b.emit("s", lambda E, src=src: E.activation(zqa.ap, src, AF.Abs),
                       reads=(z2,), writes=(xq1,))
            else:
                b.emit("s", lambda E, src=src: E.activation(zqb.ap, src, AF.Abs),
                       reads=(z2,), writes=(xq1,))
                b.emit("v", lambda E: E.tensor_tensor(zqa.ap, zqa.ap, zqb.ap, ALU.max),
                       reads=(xq1,), writes=(xq1,))
        b.emit("v", lambda E: E.tensor_tensor(zqa.ap[:, 0:512], zqa.ap[:, 0:512],
                                              zqa.ap[:, 512:1024], ALU.max),
               reads=(xq1,), writes=(xq1,))
        b.emit("v", lambda E: E.tensor_tensor(zqa.ap[:, 0:256], zqa.ap[:, 0:256],
                                              zqa.ap[:, 256:512], ALU.max),
               reads=(xq1,), writes=(xq1,))
        for pp in range(2):
            b.emit("t", lambda E, pp=pp: E.transpose(
                banks[2].ap[:, 128 * pp:128 * (pp + 1)], zqa.ap[:, 128 * pp:128 * (pp + 1)],
                C["identf"].ap), reads=(xq1, C["identf"]), writes=(banks[2],))
        b.emit("v", lambda E: E.tensor_copy(zqb.ap[:, 0:256], banks[2].ap[:, 0:256]),
               reads=(banks[2],), writes=(xq1,))
        b.emit("v", lambda E: E.reduce_max(st8b2.ap[:, 0:1], zqb.ap[:, 0:128], AX),
               reads=(xq1,), writes=(st8b2,))
        b.emit("v", lambda E: E.reduce_max(st8b2.ap[:, 1:2], zqb.ap[:, 128:256], AX),
               reads=(xq1,), writes=(st8b2,))
        b.emit("v", lambda E: E.tensor_scalar(st8b2.ap[:, 0:2], st8b2.ap[:, 0:2], C2, EPS_Q,
                                              ALU.mult, ALU.max), reads=(st8b2,), writes=(st8b2,))
        b.emit("v", lambda E: E.reciprocal(rtmpB.ap[:, 0:2], st8b2.ap[:, 0:2]),
               reads=(st8b2,), writes=(rtmpB,))
        b.emit("v", lambda E: E.tensor_scalar(rtmpB.ap[:, 0:2], rtmpB.ap[:, 0:2], 127.0 * C2,
                                              None, ALU.mult), reads=(rtmpB,), writes=(rtmpB,))
        b.emit("v", lambda E: E.tensor_scalar(st8b2.ap[:, 0:2], st8b2.ap[:, 0:2],
                                              shalfB.ap[:, 2:3], 2.0 / 127.0, ALU.mult, ALU.mult),
               reads=(st8b2, shalfB), writes=(st8b2,))
        for pp in range(2):
            b.emit("t", lambda E, pp=pp: E.transpose(
                banks[2].ap[0:1, 128 * pp:128 * (pp + 1)], rtmpB.ap[:, pp:pp + 1],
                C["identf"].ap), reads=(rtmpB, C["identf"]), writes=(banks[2],))
        b.emit("v", lambda E: E.tensor_copy(qrow.ap[0:1, 768:1024], banks[2].ap[0:1, 0:256]),
               reads=(banks[2],), writes=(qrow,))
        b.emit("t", lambda E: E.matmul(banks[3].ap[:, 0:256], C["onesfr"].ap,
                                       qrow.ap[0:1, 768:1024], start=True, stop=True),
               reads=(C["onesfr"], qrow), writes=(banks[3],))
        b.emit("v", lambda E: E.tensor_copy(qss2.ap[:, 0:256], banks[3].ap[:, 0:256]),
               reads=(banks[3],), writes=(qss2,))
        b.phase = "p4dn"

        def down_cc(cc):
            wb_ = wdb[cc % 2]
            rr = cc // (WDS // 128)
            blk = cc % (WDS // 128)
            srcw = tern_outd.ap[rr * DPART + blk * 128 * D:
                                rr * DPART + (blk + 1) * 128 * D]
            b.dma(("s", "g", "y")[cc % 3], wb_, wb_.ap, tern_outd,
                  srcw.rearrange("(p o) -> p o", p=128))
            kk = 0
            for th in range(2):
                for g in range(4):
                    pb = banks[th * 4 + g]
                    xap = z2bf[:, T2 * cc + 128 * th:T2 * cc + 128 * (th + 1)]
                    b.emit("t", lambda E, pb=pb, g=g, wb_=wb_, xap=xap,
                           st=(cc == 0), sp=(cc == NIC - 1): E.matmul(
                               pb.ap[:, 0:512], xap,
                               wb_.ap[:, 512 * g:512 * (g + 1)], start=st, stop=sp),
                           reads=(z2, wb_), writes=(pb,), inc=(kk == 7))
                    kk += 1

        for seg in range(16):
            sl = slice(1024 * seg, 1024 * (seg + 1))
            b.emit("v", lambda E, sl=sl: E.tensor_tensor(
                zqa.ap.rearrange("p (c q) -> p c q", c=4),
                z2.ap[:, sl].rearrange("p (c q) -> p c q", c=4),
                qss2.ap[:, 0:256].unsqueeze(1).broadcast_to([128, 4, 256]), ALU.mult),
                reads=(z2, qss2), writes=(xq1,))
            b.emit("v", lambda E: E.tensor_scalar(zqa.ap, zqa.ap, C32, C32,
                                                  ALU.add, ALU.subtract),
                   reads=(xq1,), writes=(xq1,))
            b.emit("v", lambda E: E.tensor_scalar(zqb.ap, zqa.ap, C52, C52,
                                                  ALU.add, ALU.subtract),
                   reads=(xq1,), writes=(xq1,))
            b.emit("v", lambda E, seg=seg: E.scalar_tensor_tensor(
                z2bf[:, 1024 * seg:1024 * (seg + 1)], zqa.ap, blend.ap[:, 1:2],
                zqb.ap, ALU.mult, ALU.add),
                reads=(xq1, blend), writes=(z2,))
            for cc in range(4 * seg, 4 * seg + 4):
                down_cc(cc)
        for th in range(2):
            for g in range(4):
                pb = banks[th * 4 + g]
                xt = x1t4[p][th]
                b.emit("v", lambda E, pb=pb, th=th, g=g, xt=xt: E.scalar_tensor_tensor(
                    xt.ap[:, 512 * g:512 * (g + 1)], pb.ap[:, 0:512],
                    st8b2.ap[:, th:th + 1], xt.ap[:, 512 * g:512 * (g + 1)],
                    ALU.mult, ALU.add), reads=(pb, st8b2, xt), writes=(xt,))
            ti = 2 * j + th
            b.dma("y", out_d, out_d.ap[128 * ti:128 * (ti + 1), :],
                  x1t4[p][th], x1t4[p][th].ap)

    pend = h1_stage_fns(0)
    while pend:
        pend.pop(0)()
    for j in range(NT2):
        pend = h1_stage_fns(j + 1) if j + 1 < NT2 else []

        def drain(k, pend=pend):
            if k < 0:
                while pend:
                    pend.pop(0)()
            else:
                for _ in range(k):
                    if pend:
                        pend.pop(0)()
        h2_emit(j, drain)


    tap_outs = {}
    if "kv_in" in taps:
        tap_outs["t_kv"] = dram("t_kv", [NC, KVP], BF16, "ExternalOutput")
        b.dma("y", tap_outs["t_kv"], tap_outs["t_kv"].ap[:, :], kv_in, kv_in.ap[:, :])
    if "x_tm" in taps:
        tap_outs["t_x"] = dram("t_x", [NXT, 128, D], F32, "ExternalOutput")
        b.dma("y", tap_outs["t_x"], tap_outs["t_x"].ap[:], x_tm_d, x_tm_d.ap[:])
    if "svals" in taps:
        tap_outs["t_sv"] = dram("t_sv", [1, 3], F32, "ExternalOutput")
        b.emit("y", lambda E: E.dma_start(out=tap_outs["t_sv"].ap, in_=svals.ap[0:1, 4:7]),
               reads=(svals,), writes=(tap_outs["t_sv"],), dma=True)
    if "q" in taps:
        tap_outs["t_q"] = dram("t_q", [NC, QOP], BF16, "ExternalOutput")
        b.dma("y", tap_outs["t_q"], tap_outs["t_q"].ap[:, :], q_aout, q_aout.ap[:, :])
    if "oa" in taps:
        tap_outs["t_oa"] = dram("t_oa", [2, NC, OHP], BF16, "ExternalOutput")
        b.dma("y", tap_outs["t_oa"], tap_outs["t_oa"].ap[0], o_aout0, o_aout0.ap[:, :])
        b.dma("y", tap_outs["t_oa"], tap_outs["t_oa"].ap[1], o_aout1, o_aout1.ap[:, :])
    if "x1" in taps:
        tap_outs["t_x1"] = dram("t_x1", [NXT, 128, D], F32, "ExternalOutput")
        b.dma("y", tap_outs["t_x1"], tap_outs["t_x1"].ap[:], x1_d, x1_d.ap[:])
    if "tern" in taps:
        tap_outs["t_tern"] = dram("t_tern", [TERN_STRIDE], FP8, "ExternalOutput")
        b.dma("y", tap_outs["t_tern"], tap_outs["t_tern"].ap[:], tern_in, tern_in.ap[:])

    for q in ("DY", "DS", "DG"):
        for i in range(NQSEM):
            b.wait("y", (f"{q}{i}", b.cnt[f"{q}{i}"]))
    for e in ("G", "V", "S", "T"):
        b.wait("y", (e, b.cnt[e]))

    b.finalize()
    return nc


def kernel(L0_attn_x, L0_attn_Wq, L0_attn_Wk, L0_attn_Wv, L0_attn_Wo,
           L0_attn_gq, L0_attn_gk, L0_ffn_w_gate, L0_ffn_w_up, L0_ffn_w_down,
           _taps=(), _return_raw=False):
    x = np.ascontiguousarray(np.asarray(L0_attn_x, dtype=np.float32).reshape(S, D))
    nc = build_program(taps=_taps)
    in_maps = []
    for c in range(NC):
        lo = c * CH
        hi = (15 - c) * CH
        x_own = np.ascontiguousarray(np.concatenate([x[lo:lo + CH], x[hi:hi + CH]], axis=0))
        in_maps.append({
            "x_own": x_own,
            "Wq": np.ascontiguousarray(L0_attn_Wq, dtype=np.float32),
            "Wk": np.ascontiguousarray(L0_attn_Wk, dtype=np.float32),
            "Wv": np.ascontiguousarray(L0_attn_Wv, dtype=np.float32),
            "Wo": np.ascontiguousarray(L0_attn_Wo, dtype=np.float32),
            "gq": np.ascontiguousarray(np.asarray(L0_attn_gq, dtype=np.float32).reshape(1, DH)),
            "gk": np.ascontiguousarray(np.asarray(L0_attn_gk, dtype=np.float32).reshape(1, DH)),
            "wg_sl": np.ascontiguousarray(np.asarray(L0_ffn_w_gate, dtype=np.float32)[c * WGS:(c + 1) * WGS, :]),
            "wu_sl": np.ascontiguousarray(np.asarray(L0_ffn_w_up, dtype=np.float32)[c * WGS:(c + 1) * WGS, :]),
            "wd_sl": np.ascontiguousarray(np.asarray(L0_ffn_w_down, dtype=np.float32)[:, c * WDS:(c + 1) * WDS]),
        })
    res = run_bass_kernel_spmd(nc, in_maps, core_ids=list(range(NC)))
    if _return_raw:
        return res
    full = np.zeros((S, D), dtype=np.float32)
    for c in range(NC):
        o = res.results[c]["out"]
        full[c * CH:(c + 1) * CH] = o[:CH]
        full[(15 - c) * CH:(16 - c) * CH] = o[CH:]
    return full.reshape(1, S, D)
